# version 11
# baseline (speedup 1.0000x reference)
import numpy as np
import concourse.bass as bass
import concourse.mybir as mybir
import concourse.bass_isa as bass_isa
from concourse.bass_utils import run_bass_kernel_spmd
from contextlib import ExitStack

F32 = mybir.dt.float32
BF16 = mybir.dt.bfloat16
I8 = mybir.dt.int8
I32 = mybir.dt.int32
U8 = mybir.dt.uint8
AL = mybir.AluOpType
AF = mybir.ActivationFunctionType
AX = mybir.AxisListType

SEQ = 2048
D = 1024
H = 16
NT = 16
NCH = 8
EPS_LN = 1e-5
EPS_Q = 1e-6


class SemBox:
    def __init__(self, nc, es, names):
        self.h = {n: es.enter_context(nc.semaphore(n)) for n in names}
        self.c = {n: 0 for n in names}
        self.waits = []

    def inc(self, inst, name, amt=1):
        inst.then_inc(self.h[name], amt)
        self.c[name] += amt
        return self.c[name]

    def wait(self, eng, name, val):
        if val <= 0:
            return
        self.waits.append((name, val))
        eng.wait_ge(self.h[name], val)

    def check(self):
        for name, val in self.waits:
            assert val <= self.c[name], f"wait {name}>={val} but max {self.c[name]}"


class DrainAll:

    def __init__(self, eng):
        self._eng = eng

    def __getattr__(self, n):
        f = getattr(self._eng, n)
        if not callable(f):
            return f

        def g(*a, **k):
            r = f(*a, **k)
            self._eng.drain()
            return r
        return g


def build_program(upto="p3", debug=False, debug_names=None):
    _ORDER = ["init", "p0", "p1q", "p1k", "p1v", "attn", "n", "p3"]
    _lim = _ORDER.index(upto)
    def phase_on(name):
        return _ORDER.index(name) <= _lim

    nc = bass.Bass("TRN2", target_bir_lowering=False)

    x_hbm = nc.dram_tensor("x", [SEQ, D], F32, kind="ExternalInput")
    wqt = nc.dram_tensor("wqt", [D, D], F32, kind="ExternalInput")
    wkt = nc.dram_tensor("wkt", [D, D], F32, kind="ExternalInput")
    wvt = nc.dram_tensor("wvt", [D, D], F32, kind="ExternalInput")
    wot = nc.dram_tensor("wot", [D, D], F32, kind="ExternalInput")
    wft = nc.dram_tensor("wft", [D, D], I8, kind="ExternalInput")
    bq_h = nc.dram_tensor("bq", [D], F32, kind="ExternalInput")
    bk_h = nc.dram_tensor("bk", [D], F32, kind="ExternalInput")
    bv_h = nc.dram_tensor("bv", [D], F32, kind="ExternalInput")
    bo_h = nc.dram_tensor("bo", [D], F32, kind="ExternalInput")
    ws_h = nc.dram_tensor("ws", [D], F32, kind="ExternalInput")
    fb_h = nc.dram_tensor("fb", [D], F32, kind="ExternalInput")
    cc_h = nc.dram_tensor("cc", [128, SEQ], F32, kind="ExternalInput")
    ss_h = nc.dram_tensor("ss", [128, SEQ], F32, kind="ExternalInput")
    swap_h = nc.dram_tensor("swapm", [128, 128], F32, kind="ExternalInput")
    esel_h = nc.dram_tensor("esel", [128, H], F32, kind="ExternalInput")
    isel_h = nc.dram_tensor("isel", [128, NCH * 128], F32, kind="ExternalInput")
    mq_h = nc.dram_tensor("maskq", [16, NCH], F32, kind="ExternalInput")
    mk_h = nc.dram_tensor("maskk", [16, H], F32, kind="ExternalInput")
    idm_h = nc.dram_tensor("idm", [128, 128], F32, kind="ExternalInput")
    out_hbm = nc.dram_tensor("out", [SEQ, D], F32, kind="ExternalOutput")
    vspill = nc.dram_tensor("vspill", [SEQ, D], BF16, kind="Internal")
    qspill = nc.dram_tensor("qspill", [NCH * 128, SEQ], BF16, kind="Internal")
    if debug:
        dbg = {nm: nc.dram_tensor("dbg_" + nm, shp, dt, kind="ExternalOutput")
               for nm, shp, dt in [
                   ("KP", [128, H * SEQ], F32), ("A", [128, NCH * SEQ], F32),
                   ("Qwin", [128, 2 * SEQ], F32), ("den", [128, SEQ], F32),
                   ("negM", [128, 1], F32), ("B_o", [128, D], F32),
                   ("B_ws", [128, D], F32), ("B_fb", [128, D], F32),
                   ("nrm", [16, 64], F32), ("perd", [128, 24], F32),
                   ("x1b", [128, 2 * D], F32), ("vwin", [128, 2 * NT * 66], F32),
                   ("WA", [128, NCH * D], F32), ("WB", [128, NCH * D], F32),
                   ("Qwork", [128, 2 * SEQ], F32),
                   ("xstg", [128, 2 * D], F32), ("hstg", [128, D], F32),
                   ("hstg2", [128, D], F32), ("smalls", [128, 64], F32),
               ]}

    es = ExitStack()
    with es:
        sbuf = lambda *a: es.enter_context(nc.sbuf_tensor(*a))

        A = sbuf("A", [128, NCH * SEQ], BF16)
        KP = sbuf("KP", [128, H * SEQ], BF16)
        WA = sbuf("WA", [128, NCH * D], BF16)
        WB = sbuf("WB", [128, NCH * D], BF16)
        Qwork = sbuf("Qwork", [128, 2 * SEQ], BF16)
        Qwin = sbuf("Qwin", [128, 2 * SEQ], BF16)
        xstg = sbuf("xstg", [128, 2 * D], F32)
        R1 = sbuf("R1", [128, 8192], U8)
        R2 = sbuf("R2", [128, 8192], U8)
        R3 = sbuf("R3", [128, 8192], U8)
        R4 = sbuf("R4", [128, 8192], U8)
        vwin = sbuf("vwin", [128, 2 * NT * 66], BF16)
        iselb = sbuf("iselb", [128, NCH * 128], BF16)
        swapb = sbuf("swapb", [128, 128], BF16)
        eselb = sbuf("eselb", [128, H], BF16)
        maskq = sbuf("maskq_s", [16, NCH], F32)
        maskk = sbuf("maskk_s", [16, H], F32)
        smalls = sbuf("smalls", [128, 64], F32)
        perdq = sbuf("perdq", [128, NCH], F32)
        perdk = sbuf("perdk", [128, H], F32)
        perdqb = sbuf("perdqb", [128, NCH], BF16)
        perdkb = sbuf("perdkb", [128, H], BF16)
        nrm = sbuf("nrm", [16, 64], F32)
        negM = sbuf("negM", [128, 1], F32)
        E0 = sbuf("E0", [128, 128], BF16)
        negMb = sbuf("negMb", [128, 1], BF16)
        bqt = sbuf("bqt", [128, NCH], F32)
        bkt = sbuf("bkt", [128, NCH], F32)
        bvb = sbuf("bvb", [128, NCH], BF16)
        st3 = sbuf("st3", [128, 256], F32)
        idbf = sbuf("idbf", [128, 128], BF16)
        Brow = sbuf("Brow", [128, D], BF16)
        x1r = sbuf("x1r", [128, 2 * D], F32)

        def at(base, off, name, shape, dtype):
            addr = nc.lookup_mloc(base).addr
            return nc.alloc_sbuf_tensor_at(name, shape, dtype, offset=addr + off)

        hstg = at(R1, 0, "hstg", [128, D], BF16)
        hstg2 = at(R1, 2048, "hstg2", [128, D], BF16)
        vstg = at(R1, 4096, "vstg", [128, 2 * D], BF16)
        h2b = at(R1, 0, "h2b", [128, 2 * D], F32)
        tmp1 = at(R2, 0, "tmp1", [128, SEQ], BF16)
        tmp2 = at(R2, 4096, "tmp2", [128, SEQ], BF16)
        denomC = at(R2, 0, "denomC", [128, SEQ], BF16)
        x1b = at(R2, 0, "x1b", [128, 2 * D], F32)
        ccb = at(R3, 0, "ccb", [128, SEQ], BF16)
        ssb = at(R3, 4096, "ssb", [128, SEQ], BF16)
        sqb = at(R3, 0, "sqb", [128, D], F32)
        iqb = at(R3, 4096, "iqb", [128, D], I32)
        Pt = at(R4, 0, "Pt", [128, 2 * 1024], BF16)
        stg = at(R4, 4096, "stg", [128, 2 * 1024], BF16)
        xqb = at(R4, 0, "xqb", [128, 2 * D], BF16)
        xqT = at(R4, 4096, "xqT", [128, 2 * D], BF16)
        B_o = at(Qwin, 0, "B_o", [128, D], F32)
        B_ws = at(Qwin, 4096, "B_ws", [128, D], F32)
        B_fb = at(Qwork, 0, "B_fb", [128, D], F32)
        rowbuf = at(Qwork, 4096, "rowbuf", [128, D], F32)
        rowbufb = at(R4, 4096, "rowbufb", [128, D], BF16)

        sems = SemBox(nc, es, [
            "sDX0", "sDX1", "sDI", "sDC", "sDWA", "sDWB",
            "sDT0", "sDT1", "sDVS0", "sDVS1", "sDVL0", "sDVL1",
            "sDQS0", "sDQS1", "sDQL0", "sDQL1", "sDSH0", "sDSH1",
            "sDRW", "sDO0", "sDO1", "sDOB0", "sDOB1", "sDXL0", "sDXL1",
            "sDG", "sTE", "sVE", "sAC", "sGP", "sPL"])
        cp = {}

        if phase_on('init'):
            with nc.Block() as blk:
                @blk.sync
                def _(sync):
                    sems.inc(sync.dma_start(bqt[:, :], bq_h.rearrange("(g p) -> p g", p=128), allow_slow_non_contiguous=True), "sDI", 16)
                    sems.inc(sync.dma_start(bkt[:, :], bk_h.rearrange("(g p) -> p g", p=128), allow_slow_non_contiguous=True), "sDI", 16)
                    sems.inc(sync.dma_start(maskq[:, :], mq_h[:, :]), "sDI", 16)
                    sems.inc(sync.dma_start(maskk[:, :], mk_h[:, :]), "sDI", 16)
                    cp["init_sync"] = sems.c["sDI"]

                @blk.gpsimd
                def _(g):
                    for dst, src in ((ccb, cc_h), (ssb, ss_h), (swapb, swap_h),
                                     (eselb, esel_h), (iselb, isel_h), (idbf, idm_h)):
                        sems.inc(g.dma_start(dst[:, :], src[:, :]), "sDC", 16)
                    sems.inc(g.dma_start(bvb[:, :], bv_h.rearrange("(g p) -> p g", p=128), allow_slow_non_contiguous=True), "sDC", 16)
                    cp["consts_loaded"] = sems.c["sDC"]
                    sems.inc(g.dma_start(WA[:, :], wqt.rearrange("(g p) e -> p g e", p=128)), "sDWA", 16)
                    cp["wq_loaded"] = sems.c["sDWA"]
                    sems.inc(g.dma_start(WB[:, :], wkt.rearrange("(g p) e -> p g e", p=128)), "sDWB", 16)
                    cp["wk_loaded"] = sems.c["sDWB"]

                @blk.vector
                def _(vraw):
                    v = vraw
                    v.memset(E0[:, :], 0)
                    v.drain()
                    v.memset(E0[0:1, :], 1.0)
                    v.memset(smalls[:, :], 0)
                    v.memset(nrm[:, :], 0)
                    v.memset(negM[:, :], 0)
                    v.memset(perdq[:, :], 0)
                    v.memset(perdk[:, :], 0)
                    v.memset(rowbufb[:, :], 0)
                    v.memset(Brow[:, :], 0)
                    v.memset(negMb[:, :], 0)
                    for hh in range(H):
                        v.memset(KP[:, hh * SEQ:(hh + 1) * SEQ], 0)
                    v.memset(vwin[:, :], 0)
                    v.drain()
                    inst = None
                    for j in range(2 * NT):
                        inst = v.memset(vwin[:, j * 66 + 64: j * 66 + 65], 1.0)
                    sems.inc(inst, "sVE")
                    cp["zeros_done"] = sems.c["sVE"]

        if phase_on('p0'):
            bVE0 = sems.c["sVE"]
            bAC0 = sems.c["sAC"]
            bTE0 = sems.c["sTE"]

            def veS(i):
                return bVE0 + 1 + i
            VPE0 = bVE0 + NT + 1
            COMB0 = bVE0 + NT + 2

            def veEv(i):
                return COMB0 + 1 + i

            def acH(i):
                return bAC0 + 2 + i

            def tr0(i):
                return bTE0 + 1 + i
            cp["p0_done"] = COMB0 + NT
            with ExitStack() as pes, nc.Block() as blk:
                psT0 = pes.enter_context(nc.psum_tensor("psT0", [128, D], BF16))
                psT1 = pes.enter_context(nc.psum_tensor("psT1", [128, D], BF16))

                @blk.sync
                def _(sync):
                    for i in range(NT):
                        if i >= 2:
                            sems.wait(sync, "sVE", veS(i - 2))
                        sems.inc(sync.dma_start(xstg[:, (i % 2) * D:(i % 2 + 1) * D],
                                                x_hbm[i * 128:(i + 1) * 128, :]),
                                 "sDX" + str(i % 2), 16)
                    for i in range(NT):
                        if i < 2:
                            sems.wait(sync, "sVE", veS(14 + i))
                        else:
                            sems.wait(sync, "sAC", acH(i - 2))
                        sems.inc(sync.dma_start(xstg[:, (i % 2) * D:(i % 2 + 1) * D],
                                                x_hbm[i * 128:(i + 1) * 128, :]),
                                 "sDX" + str(i % 2), 16)

                @blk.scalar
                def _(sc):
                    sems.wait(sc, "sVE", VPE0)
                    inst = sc.activation(st3[:, 208:224], st3[:, 208:224], AF.Sqrt)
                    assert sems.inc(inst, "sAC") == bAC0 + 1
                    for i in range(NT):
                        sems.wait(sc, "sVE", COMB0)
                        sems.wait(sc, "sDX" + str(i % 2), 16 * (8 + i // 2 + 1))
                        if i >= 2:
                            sems.wait(sc, "sTE", tr0(i - 2))
                        hh = hstg if i % 2 == 0 else hstg2
                        inst = sc.activation(hh[:, :], xstg[:, (i % 2) * D:(i % 2 + 1) * D],
                                             AF.Identity, bias=st3[:, 176 + i:177 + i],
                                             scale=st3[:, 160 + i:161 + i])
                        assert sems.inc(inst, "sAC") == acH(i)

                @blk.tensor
                def _(t):
                    sems.wait(t, "sDC", cp["consts_loaded"])
                    for i in range(NT):
                        sems.wait(t, "sAC", acH(i))
                        if i >= 2:
                            sems.wait(t, "sVE", veEv(i - 2))
                        hh = hstg if i % 2 == 0 else hstg2
                        pst = psT0 if i % 2 == 0 else psT1
                        for gch in range(NCH):
                            mm = t.transpose(pst[:, gch * 128:(gch + 1) * 128],
                                             hh[:, gch * 128:(gch + 1) * 128], idbf[:, :])
                        assert sems.inc(mm, "sTE") == tr0(i)

                @blk.vector
                def _(vraw):
                    st = st3
                    sA = smalls.rearrange("p (a b) -> p a b", b=2)
                    for i in range(NT):
                        sems.wait(vraw, "sDX" + str(i % 2), 16 * (i // 2 + 1))
                        xt = xstg[:, (i % 2) * D:(i % 2 + 1) * D]
                        vraw.bn_stats(st[:, 96:102], xt[:, 0:512])
                        vraw.bn_stats(st[:, 102:108], xt[:, 512:1024])
                        vraw.drain()
                        vraw.bn_aggr(smalls[:, 2 * i:2 * i + 2], st[:, 96:102])
                        inst = vraw.bn_aggr(smalls[:, 32 + 2 * i:34 + 2 * i], st[:, 102:108])
                        assert sems.inc(inst, "sVE") == veS(i)
                    vraw.drain()
                    v = DrainAll(vraw)
                    mA, vA = sA[:, 0:16, 0:1], sA[:, 0:16, 1:2]
                    mB, vB = sA[:, 16:32, 0:1], sA[:, 16:32, 1:2]
                    m_ = st[:, 112:128]
                    tA, tB = st[:, 128:144], st[:, 144:160]
                    is_, negmis = st[:, 160:176], st[:, 176:192]
                    vpe = st[:, 192:208]
                    v.tensor_tensor(m_, mA, mB, AL.add)
                    v.tensor_scalar(m_, m_, 0.5, None, AL.mult)
                    v.tensor_tensor(tA, mA, mB, AL.subtract)
                    v.tensor_tensor(tA, tA, tA, AL.mult)
                    v.tensor_tensor(vpe, vA, vB, AL.add)
                    v.tensor_scalar(vpe, vpe, 0.5, None, AL.mult)
                    v.tensor_scalar(tA, tA, 0.25, EPS_LN, AL.mult, AL.add)
                    v.tensor_tensor(vpe, vpe, tA, AL.add)
                    inst = v.tensor_copy(st[:, 208:224], vpe)
                    assert sems.inc(inst, "sVE") == VPE0
                    sems.wait(vraw, "sAC", bAC0 + 1)
                    v.reciprocal(is_, st[:, 208:224])
                    v.tensor_tensor(tB, is_, is_, AL.mult)
                    v.tensor_tensor(tB, tB, vpe, AL.mult)
                    v.tensor_scalar(tB, tB, 3.0, -0.5, AL.subtract, AL.mult)
                    v.tensor_tensor(is_, is_, tB, AL.mult)
                    v.tensor_tensor(negmis, m_, is_, AL.mult)
                    inst = v.tensor_scalar(negmis, negmis, -1.0, None, AL.mult)
                    assert sems.inc(inst, "sVE") == COMB0
                    for i in range(NT):
                        sems.wait(vraw, "sTE", tr0(i))
                        pst = psT0 if i % 2 == 0 else psT1
                        av = A.rearrange("p (g s) -> p g s", g=NCH)
                        inst = vraw.tensor_copy(av[:, :, i * 128:(i + 1) * 128],
                                                pst.rearrange("p (g c) -> p g c", g=NCH))
                        assert sems.inc(inst, "sVE") == veEv(i)

        if phase_on('p1q'):
            cnt_projq, cnt_swq = [], []
            bAC1 = sems.c["sAC"]
            bVE1 = sems.c["sVE"]
            with ExitStack() as pes, nc.Block() as blk:
                psP = pes.enter_context(nc.psum_tensor("psP", [128, SEQ], F32))
                psSW = pes.enter_context(nc.psum_tensor("psSW", [128, SEQ], F32))

                @blk.tensor
                def _(t):
                    sems.wait(t, "sVE", cp["p0_done"])
                    sems.wait(t, "sDWA", cp["wq_loaded"])
                    sems.wait(t, "sDC", cp["consts_loaded"])
                    for m in range(NCH + 1):
                        if m < NCH:
                            sems.wait(t, "sAC", bAC1 + m)
                            for nb in range(4):
                                for dd in range(NCH):
                                    mm = t.matmul(psP[:, nb * 512:(nb + 1) * 512],
                                                  WA[:, dd * D + m * 128: dd * D + (m + 1) * 128],
                                                  A[:, dd * SEQ + nb * 512: dd * SEQ + (nb + 1) * 512],
                                                  start=(dd == 0), stop=(dd == NCH - 1))
                            cnt_projq.append(sems.inc(mm, "sTE"))
                        if m >= 1:
                            j = m - 1
                            sems.wait(t, "sAC", bAC1 + j + 1)
                            if j >= 1:
                                sems.wait(t, "sVE", bVE1 + 4 * (j - 1) + 2)
                            src = Qwork[:, (j % 2) * SEQ:(j % 2 + 1) * SEQ]
                            for nb in range(4):
                                mm = t.matmul(psSW[:, nb * 512:(nb + 1) * 512], swapb[:, :],
                                              src[:, nb * 512:(nb + 1) * 512], start=True, stop=True)
                            cnt_swq.append(sems.inc(mm, "sTE"))

                @blk.scalar
                def _(sc):
                    sems.wait(sc, "sDI", 64)
                    for m in range(NCH):
                        sems.wait(sc, "sTE", cnt_projq[m])
                        if m >= 2:
                            sems.wait(sc, "sDQS" + str(m % 2), 16 * ((m - 2) // 2 + 1))
                            sems.wait(sc, "sVE", bVE1 + 4 * (m - 2) + 4)
                        inst = sc.activation(Qwork[:, (m % 2) * SEQ:(m % 2 + 1) * SEQ],
                                             psP[:, :], AF.Identity, bias=bqt[:, m:m + 1])
                        sems.inc(inst, "sAC")
                    cp["q_evac_done"] = sems.c["sAC"]

                @blk.vector
                def _(v):
                    for m in range(NCH):
                        src = Qwork[:, (m % 2) * SEQ:(m % 2 + 1) * SEQ]
                        sems.wait(v, "sTE", cnt_swq[m])
                        sems.inc(v.tensor_tensor(tmp1[:, :], src, ccb[:, :], AL.mult), "sVE")
                        sems.inc(v.tensor_tensor(src, psSW[:, :], ssb[:, :], AL.mult), "sVE")
                        v.drain()
                        sems.inc(v.tensor_tensor(src, src, tmp1[:, :], AL.add), "sVE")
                        v.drain()
                        inst = v.tensor_reduce(perdq[:, m:m + 1], src, AX.X, AL.max,
                                               apply_absolute_value=True)
                        sems.inc(inst, "sVE")
                        if m == NCH - 1:
                            v.drain()
                    cp["q_rope_done"] = sems.c["sVE"]

                @blk.gpsimd
                def _(g):
                    for m in range(NCH):
                        sems.wait(g, "sVE", bVE1 + 4 * m + 3)
                        inst = g.dma_start(qspill[m * 128:(m + 1) * 128, :],
                                           Qwork[:, (m % 2) * SEQ:(m % 2 + 1) * SEQ])
                        sems.inc(inst, "sDQS" + str(m % 2), 16)

        if phase_on('p1k'):
            cnt_projk, cnt_swk = [], []
            bACk = sems.c["sAC"]
            bVEk = sems.c["sVE"]
            with ExitStack() as pes, nc.Block() as blk:
                psP = pes.enter_context(nc.psum_tensor("psPk", [128, SEQ], F32))
                psSW = pes.enter_context(nc.psum_tensor("psSWk", [128, SEQ], F32))

                @blk.tensor
                def _(t):
                    sems.wait(t, "sDWB", cp["wk_loaded"])
                    sems.wait(t, "sVE", cp["zeros_done"])
                    for m in range(NCH + 1):
                        if m < NCH:
                            sems.wait(t, "sAC", bACk + m)
                            for nb in range(4):
                                for dd in range(NCH):
                                    mm = t.matmul(psP[:, nb * 512:(nb + 1) * 512],
                                                  WB[:, dd * D + m * 128: dd * D + (m + 1) * 128],
                                                  A[:, dd * SEQ + nb * 512: dd * SEQ + (nb + 1) * 512],
                                                  start=(dd == 0), stop=(dd == NCH - 1))
                            cnt_projk.append(sems.inc(mm, "sTE"))
                        if m >= 1:
                            for j in (2 * (m - 1), 2 * (m - 1) + 1):
                                sems.wait(t, "sAC", bACk + (m - 1) + 1)
                                if j >= 1:
                                    sems.wait(t, "sVE", bVEk + 4 * (j - 1) + 2)
                                src = KP[:, j * SEQ:(j + 1) * SEQ]
                                for nb in range(4):
                                    mm = t.matmul(psSW[:, nb * 512:(nb + 1) * 512], swapb[:, :],
                                                  src[:, nb * 512:(nb + 1) * 512],
                                                  start=True, stop=True)
                                cnt_swk.append(sems.inc(mm, "sTE"))

                @blk.scalar
                def _(sc):
                    sems.wait(sc, "sVE", cp["zeros_done"])
                    sems.wait(sc, "sDI", 64)
                    for m in range(NCH):
                        sems.wait(sc, "sTE", cnt_projk[m])
                        h0, h1 = 2 * m, 2 * m + 1
                        sc.activation(KP[0:64, h0 * SEQ:(h0 + 1) * SEQ], psP[0:64, :],
                                      AF.Identity, bias=bkt[0:64, m:m + 1])
                        inst = sc.activation(KP[64:128, h1 * SEQ:(h1 + 1) * SEQ],
                                             psP[64:128, :], AF.Identity,
                                             bias=bkt[64:128, m:m + 1])
                        sems.inc(inst, "sAC")
                    cp["k_evac_done"] = sems.c["sAC"]

                @blk.vector
                def _(v):
                    for j in range(H):
                        src = KP[:, j * SEQ:(j + 1) * SEQ]
                        sems.wait(v, "sTE", cnt_swk[j])
                        sems.inc(v.tensor_tensor(tmp1[:, :], src, ccb[:, :], AL.mult), "sVE")
                        sems.inc(v.tensor_tensor(src, psSW[:, :], ssb[:, :], AL.mult), "sVE")
                        v.drain()
                        sems.inc(v.tensor_tensor(src, src, tmp1[:, :], AL.add), "sVE")
                        v.drain()
                        inst = v.tensor_reduce(perdk[:, j:j + 1], src, AX.X, AL.max,
                                               apply_absolute_value=True)
                        sems.inc(inst, "sVE")
                        if j == H - 1:
                            v.drain()
                    cp["k_rope_done"] = sems.c["sVE"]

        if phase_on('p1v'):
            cnt_projv = []
            bACv = sems.c["sAC"]
            bVEv = sems.c["sVE"]
            cp["nb_sqrt_done"] = sems.c["sAC"] + NT + 1
            with ExitStack() as pes, nc.Block() as blk:
                psP = pes.enter_context(nc.psum_tensor("psPv", [128, SEQ], F32))
                psN = pes.enter_context(nc.psum_tensor("psN", [16, 64], F32))
                psNeg = pes.enter_context(nc.psum_tensor("psNeg", [128, 1], F32))

                @blk.gpsimd
                def _(g):
                    sems.wait(g, "sTE", cnt_projq[-1] if cnt_projq else 0)
                    sems.inc(g.dma_start(WA[:, :], wvt.rearrange("(g p) e -> p g e", p=128)), "sDWA", 16)
                    cp["wv_loaded"] = sems.c["sDWA"]

                @blk.tensor
                def _(t):
                    sems.wait(t, "sDWA", cp["wq_loaded"] + 16)
                    for i in range(NT):
                        half = (i % 2) * 1024
                        if i >= 2:
                            sems.wait(t, "sAC", bACv + (i - 2) + 1)
                        for dd in range(NCH):
                            for nb in range(2):
                                mm = t.matmul(psP[:, half + nb * 512: half + (nb + 1) * 512],
                                              A[:, dd * SEQ + i * 128: dd * SEQ + (i + 1) * 128],
                                              WA[:, dd * D + nb * 512: dd * D + (nb + 1) * 512],
                                              start=(dd == 0), stop=(dd == NCH - 1))
                        cnt_projv.append(sems.inc(mm, "sTE"))
                    sems.wait(t, "sVE", bVEv + 1)
                    t.matmul(psN[:, 0:NCH], eselb[:, :], perdqb[:, :], start=True, stop=True)
                    mm = t.matmul(psN[:, 32:32 + H], eselb[:, :], perdkb[:, :], start=True, stop=True)
                    cp["nb_mm_done"] = sems.inc(mm, "sTE")
                    sems.wait(t, "sAC", cp["nb_sqrt_done"])
                    mm = t.matmul(psNeg[:, :], E0[:, :], negMb[:, 0:1], start=True, stop=True)
                    cp["negmm_done"] = sems.inc(mm, "sTE")

                @blk.scalar
                def _(sc):
                    for i in range(NT):
                        half = (i % 2) * 1024
                        sems.wait(sc, "sTE", cnt_projv[i])
                        if i >= 2:
                            sems.wait(sc, "sDVS" + str(i % 2), 16 * ((i - 2) // 2 + 1))
                        inst = sc.activation(vstg[:, (i % 2) * D:(i % 2 + 1) * D],
                                             psP[:, half:half + D], AF.Copy)
                        sems.inc(inst, "sAC")
                    cp["v_evac_done"] = sems.c["sAC"]
                    sems.wait(sc, "sVE", bVEv + 3)
                    sc.activation(smalls[0:1, 33:34], smalls[0:1, 32:33], AF.Sqrt,
                                  scale=0.015625)
                    sc.drain()
                    sc.mul(smalls[0:1, 33:34], smalls[0:1, 33:34], -1.0)
                    sc.drain()
                    inst = sc.copy(negMb[0:1, 0:1], smalls[0:1, 33:34])
                    assert sems.c["sAC"] + 1 == cp["nb_sqrt_done"]
                    sems.inc(inst, "sAC")

                @blk.vector
                def _(vraw):
                    v = DrainAll(vraw)
                    sems.wait(vraw, "sDI", cp["init_sync"])
                    v.tensor_tensor(perdq[:, :], perdq[:, :], perdq[:, :], AL.mult)
                    v.tensor_tensor(perdk[:, :], perdk[:, :], perdk[:, :], AL.mult)
                    v.tensor_copy(perdqb[:, :], perdq[:, :])
                    inst = v.tensor_copy(perdkb[:, :], perdk[:, :])
                    cp["nb_sq_done"] = sems.inc(inst, "sVE")
                    sems.wait(vraw, "sTE", cp["nb_mm_done"])
                    v.tensor_tensor(nrm[:, 0:NCH], psN[:, 0:NCH], maskq[:, :], AL.mult)
                    v.tensor_tensor(nrm[:, 32:32 + H], psN[:, 32:32 + H], maskk[:, :], AL.mult)
                    v.tensor_reduce(nrm[:, 16:17], nrm[:, 0:NCH], AX.X, AL.add)
                    inst = v.tensor_reduce(nrm[:, 17:18], nrm[:, 32:32 + H], AX.X, AL.add)
                    cp["nb_red_done"] = sems.inc(inst, "sVE")
                    sems.wait(vraw, "sDG", 32)
                    v.tensor_reduce(smalls[0:1, 30:31], nrm[0:1, 20:36], AX.X, AL.max)
                    v.tensor_reduce(smalls[0:1, 31:32], nrm[0:1, 40:56], AX.X, AL.max)
                    inst = v.tensor_tensor(smalls[0:1, 32:33], smalls[0:1, 30:31],
                                           smalls[0:1, 31:32], AL.mult)
                    cp["nb_mul_done"] = sems.inc(inst, "sVE")
                    sems.wait(vraw, "sTE", cp["negmm_done"])
                    inst = v.tensor_copy(negM[:, :], psNeg[:, :])
                    cp["negM_ve"] = sems.inc(inst, "sVE")

                @blk.gpsimd
                def _(g):
                    basev = cp["v_evac_done"] - NT
                    for i in range(NT):
                        sems.wait(g, "sAC", basev + i + 1)
                        sems.inc(g.dma_start(vspill[i * 128:(i + 1) * 128, :],
                                             vstg[:, (i % 2) * D:(i % 2 + 1) * D]), "sDVS" + str(i % 2), 16)
                    sems.wait(g, "sVE", cp["nb_red_done"])
                    sems.inc(g.dma_start(nrm[0:1, 20:36], nrm[0:16, 16:17],
                                         allow_slow_non_contiguous=True), "sDG", 16)
                    sems.inc(g.dma_start(nrm[0:1, 40:56], nrm[0:16, 17:18],
                                         allow_slow_non_contiguous=True), "sDG", 16)
                    sems.wait(g, "sTE", cnt_projv[-1])
                    sems.inc(g.dma_start(WA[:, :], wot.rearrange("(g p) e -> p g e", p=128)), "sDWA", 16)
                    cp["wo_loaded"] = sems.c["sDWA"]
                    sems.inc(g.dma_start(WB[:, :], wft.rearrange("(g p) e -> p g e", p=128)), "sDWB", 16)
                    cp["wf_loaded"] = sems.c["sDWB"]

        if phase_on('attn'):
            bTEa = sems.c["sTE"]
            bACa = sems.c["sAC"]
            bVEa = sems.c["sVE"]

            NG = 2 * H * NT

            def s_cnt(g):
                return bTEa + (1 if g == 0 else 2 * g)

            def pv_cnt(g):
                return bTEa + (2 * g + 3 if g < NG - 1 else 2 * NG)

            def exp_cnt(g):
                return bACa + g + 1

            def drain_cnt(u):
                return bVEa + u + 1

            with ExitStack() as pes, nc.Block() as blk:
                Sa = pes.enter_context(nc.psum_tensor("Sa", [128, 1024], F32))
                Sb = pes.enter_context(nc.psum_tensor("Sb", [128, 1024], F32))
                PV0 = pes.enter_context(nc.psum_tensor("PV0", [65, 1024], F32))
                PV1 = pes.enter_context(nc.psum_tensor("PV1", [65, 1024], F32))

                @blk.tensor
                def _(t):
                    sems.wait(t, "sVE", cp["negM_ve"])

                    def emit_S(g):
                        u, tau = g // NT, g % NT
                        hh, sg = u // 2, u % 2
                        pp = hh // 2
                        if tau == 0 and sg == 0 and hh % 2 == 0:
                            sems.wait(t, "sDQL" + str(pp % 2), 16 * (pp // 2 + 1))
                        if tau == 0 and sg == 0:
                            sems.wait(t, "sDVL" + str(hh % 2), 16 * (hh // 2 + 1))
                        sps = Sa if g % 2 == 0 else Sb
                        if g >= 2:
                            sems.wait(t, "sAC", exp_cnt(g - 2))
                        qsl = Qwin[:, (pp % 2) * SEQ + sg * 1024:
                                   (pp % 2) * SEQ + (sg + 1) * 1024]
                        for nb in range(2):
                            mm = t.matmul(sps[:, nb * 512:(nb + 1) * 512],
                                          KP[:, hh * SEQ + tau * 128: hh * SEQ + (tau + 1) * 128],
                                          qsl[:, nb * 512:(nb + 1) * 512],
                                          start=True, stop=True)
                        sems.inc(mm, "sTE")

                    def emit_PV(g):
                        u, tau = g // NT, g % NT
                        hh = u // 2
                        pv = PV0 if u % 2 == 0 else PV1
                        if tau == 0 and u >= 2:
                            sems.wait(t, "sVE", drain_cnt(u - 2))
                        sems.wait(t, "sAC", exp_cnt(g))
                        pw = Pt[:, (g % 2) * 1024:(g % 2 + 1) * 1024]
                        vsl = vwin[:, (hh % 2) * NT * 66 + tau * 66:
                                   (hh % 2) * NT * 66 + tau * 66 + 65]
                        for nb in range(2):
                            mm = t.matmul(pv[:, nb * 512:(nb + 1) * 512], vsl,
                                          pw[:, nb * 512:(nb + 1) * 512],
                                          start=(tau == 0), stop=(tau == NT - 1))
                        sems.inc(mm, "sTE")

                    for g in range(NG):
                        emit_S(g)
                        if g >= 1:
                            emit_PV(g - 1)
                    emit_PV(NG - 1)
                    assert sems.c["sTE"] == bTEa + 2 * NG
                    cp["attn_te_done"] = sems.c["sTE"]

                @blk.scalar
                def _(sc):
                    for g in range(NG):
                        sps = Sa if g % 2 == 0 else Sb
                        sems.wait(sc, "sTE", s_cnt(g))
                        if g >= 2:
                            sems.wait(sc, "sTE", pv_cnt(g - 2))
                        inst = sc.activation(Pt[:, (g % 2) * 1024:(g % 2 + 1) * 1024],
                                             sps[:, :], AF.Exp, bias=negM[:, 0:1],
                                             scale=0.125)
                        sems.inc(inst, "sAC")
                    cp["attn_exp_done"] = sems.c["sAC"]

                @blk.vector
                def _(v):
                    v.memset(denomC[:, :], 1.0)
                    v.drain()
                    for u in range(2 * H):
                        pv = PV0 if u % 2 == 0 else PV1
                        sems.wait(v, "sTE", pv_cnt(u * NT + NT - 1))
                        if u >= 2:
                            sems.wait(v, "sDSH" + str(u % 2), 32 * ((u - 2) // 2 + 1))
                        inst = v.tensor_copy(stg[0:65, (u % 2) * 1024:(u % 2) * 1024 + 1024],
                                             pv[:, :])
                        sems.inc(inst, "sVE")
                    cp["attn_drain_done"] = sems.c["sVE"]

                @blk.gpsimd
                def _(g):
                    sems.wait(g, "sDQS0", 16 * 4)
                    sems.wait(g, "sDQS1", 16 * 4)
                    sems.wait(g, "sDVS0", 16 * 8)
                    sems.wait(g, "sDVS1", 16 * 8)
                    for pp in range(2):
                        sems.inc(g.dma_start(Qwin[:, (pp % 2) * SEQ:(pp % 2 + 1) * SEQ],
                                             qspill[pp * 128:(pp + 1) * 128, :]), "sDQL" + str(pp % 2), 16)
                    for hh in range(2):
                        dst = vwin.rearrange("p (w t c) -> p w t c", w=2, t=NT)
                        src = vspill.rearrange("(t p) e -> p t e", p=128)
                        sems.inc(g.dma_start(dst[:, hh % 2, :, 0:64],
                                             src[:, :, hh * 64:(hh + 1) * 64]), "sDVL" + str(hh % 2), 16)
                    for u in range(2 * H):
                        hh, sg = u // 2, u % 2
                        sems.wait(g, "sVE", drain_cnt(u))
                        ssl = stg[:, (u % 2) * 1024:(u % 2) * 1024 + 1024]
                        och, row0 = hh // 2, (hh % 2) * 64
                        sems.inc(g.dma_start(
                            A[row0:row0 + 64, och * SEQ + sg * 1024: och * SEQ + sg * 1024 + 1024],
                            ssl[0:64, :]), "sDSH" + str(u % 2), 16)
                        sems.inc(g.dma_start(
                            denomC[hh:hh + 1, sg * 1024:(sg + 1) * 1024],
                            ssl[64:65, :]), "sDSH" + str(u % 2), 16)
                        if sg == 1:
                            nh = hh + 2
                            if nh < H:
                                dst = vwin.rearrange("p (w t c) -> p w t c", w=2, t=NT)
                                src = vspill.rearrange("(t p) e -> p t e", p=128)
                                sems.inc(g.dma_start(dst[:, nh % 2, :, 0:64],
                                                     src[:, :, nh * 64:(nh + 1) * 64]),
                                         "sDVL" + str(nh % 2), 16)
                            if hh % 2 == 1:
                                np_ = hh // 2 + 2
                                if np_ < NCH:
                                    sems.inc(g.dma_start(
                                        Qwin[:, (np_ % 2) * SEQ:(np_ % 2 + 1) * SEQ],
                                        qspill[np_ * 128:(np_ + 1) * 128, :]), "sDQL" + str(np_ % 2), 16)

        if phase_on('n'):
            bTEn = sems.c["sTE"]
            bVEn = sems.c["sVE"]
            bACn = sems.c["sAC"]
            with ExitStack() as pes, nc.Block() as blk:
                psR = pes.enter_context(nc.psum_tensor("psR", [128, 1024], F32))
                psR2 = pes.enter_context(nc.psum_tensor("psR2", [128, 1024], F32))
                psBV = pes.enter_context(nc.psum_tensor("psBV", [1, D], F32))
                psBC = pes.enter_context(nc.psum_tensor("psBC", [128, D], F32))

                @blk.tensor
                def _(t):
                    sems.wait(t, "sVE", bVEn + 1)
                    for j in range(2 * NCH):
                        gch, sg = j // 2, j % 2
                        ps = psR if j % 2 == 0 else psR2
                        if j >= 2:
                            sems.wait(t, "sVE", bVEn + j)
                        for nb in range(2):
                            mm = t.matmul(ps[:, nb * 512:(nb + 1) * 512],
                                          iselb[:, gch * 128:(gch + 1) * 128],
                                          denomC[:, sg * 1024 + nb * 512: sg * 1024 + (nb + 1) * 512],
                                          start=True, stop=True)
                        sems.inc(mm, "sTE")
                    sems.wait(t, "sDWA", cp["wo_loaded"])
                    sems.wait(t, "sDC", cp["consts_loaded"])
                    for gch in range(NCH):
                        for nb in range(2):
                            mm = t.matmul(psBV[:, nb * 512:(nb + 1) * 512], bvb[:, gch:gch + 1],
                                          WA[:, gch * D + nb * 512: gch * D + (nb + 1) * 512],
                                          start=(gch == 0), stop=(gch == NCH - 1))
                    cp["bv_mm_done"] = sems.inc(mm, "sTE")
                    for k in range(3):
                        sems.wait(t, "sVE", bVEn + 18 + 2 * k)
                        for nb in range(2):
                            mm = t.matmul(psBC[:, nb * 512:(nb + 1) * 512], E0[:, :],
                                          rowbufb[:, nb * 512:(nb + 1) * 512],
                                          start=True, stop=True)
                        sems.inc(mm, "sTE")

                @blk.sync
                def _(sync):
                    sems.wait(sync, "sTE", cp["attn_te_done"])
                    sems.inc(sync.dma_start(rowbuf[0:1, :], ws_h[None, :]), "sDRW", 16)
                    sems.wait(sync, "sVE", bVEn + 20)
                    sems.inc(sync.dma_start(rowbuf[0:1, :], fb_h[None, :]), "sDRW", 16)
                    sems.wait(sync, "sVE", bVEn + 22)
                    sems.inc(sync.dma_start(rowbuf[0:1, :], bo_h[None, :]), "sDRW", 16)

                @blk.vector
                def _(vraw):
                    sems.wait(vraw, "sDSH0", 32 * 16)
                    sems.wait(vraw, "sDSH1", 32 * 16)
                    with nc.allow_low_precision(reason="bf16 softmax denom recip"):
                        inst = vraw.reciprocal(denomC[0:16, :], denomC[0:16, :])
                    sems.inc(inst, "sVE")
                    vraw.drain()
                    for j in range(2 * NCH):
                        gch, sg = j // 2, j % 2
                        ps = psR if j % 2 == 0 else psR2
                        sems.wait(vraw, "sTE", bTEn + j + 1)
                        osl = A[:, gch * SEQ + sg * 1024: gch * SEQ + sg * 1024 + 1024]
                        sems.inc(vraw.tensor_tensor(osl, osl, ps[:, :], AL.mult), "sVE")
                    cp["onorm_done"] = sems.c["sVE"]
                    v = DrainAll(vraw)
                    sems.wait(vraw, "sDRW", 16)
                    sems.inc(v.tensor_copy(rowbufb[0:1, :], rowbuf[0:1, :]), "sVE")
                    sems.wait(vraw, "sTE", bTEn + 18)
                    sems.inc(v.tensor_copy(B_ws[:, :], psBC[:, :]), "sVE")
                    v.tensor_copy(xstg[0:1, 0:D], rowbufb[0:1, :])
                    v.tensor_tensor(rowbuf[0:1, :], rowbuf[0:1, :], xstg[0:1, 0:D],
                                    AL.subtract)
                    sems.inc(v.tensor_copy(rowbufb[0:1, :], rowbuf[0:1, :]), "sVE")
                    sems.wait(vraw, "sTE", bTEn + 19)
                    sems.inc(v.tensor_tensor(B_ws[:, :], B_ws[:, :], psBC[:, :], AL.add), "sVE")
                    sems.wait(vraw, "sDRW", 32)
                    sems.inc(v.tensor_copy(rowbufb[0:1, :], rowbuf[0:1, :]), "sVE")
                    sems.wait(vraw, "sTE", bTEn + 20)
                    sems.inc(v.tensor_copy(B_fb[:, :], psBC[:, :]), "sVE")
                    sems.wait(vraw, "sDRW", 48)
                    sems.wait(vraw, "sTE", cp["bv_mm_done"])
                    v.tensor_tensor(rowbuf[0:1, :], rowbuf[0:1, :], psBV[:, :], AL.add)
                    inst = v.tensor_copy(Brow[0:1, :], rowbuf[0:1, :])
                    cp["brow_done"] = sems.inc(inst, "sVE")

        if phase_on('p3'):
            bVE3 = sems.c["sVE"]
            bAC3 = sems.c["sAC"]
            bTE3 = sems.c["sTE"]
            bX = (sems.c["sDX0"], sems.c["sDX1"])

            def veA(i):
                return bVE3 + i + 1
            VPE3 = bVE3 + NT + 1
            COMB3 = bVE3 + NT + 2
            veE = {}
            _c = COMB3
            for _s in range(NT + 2):
                if _s >= 2:
                    _c += 1; veE[(_s - 2, 3)] = _c
                if 1 <= _s <= NT:
                    _c += 1; veE[(_s - 1, 6)] = _c
                if _s < NT:
                    _c += 1; veE[(_s, 1)] = _c
                    _c += 1; veE[(_s, 2)] = _c
            cnt_wo = [bTE3 + 1 + i for i in range(NT)]
            cnt_tr, cnt_ffn = {}, {}
            _t = bTE3 + NT
            for _s in range(NT + 1):
                if _s < NT:
                    _t += 1; cnt_tr[_s] = _t
                if _s >= 1:
                    _t += 1; cnt_ffn[_s - 1] = _t

            acC = {}
            _a = bAC3 + 1
            for _s in range(NT + 1):
                if _s < NT:
                    _a += 1; acC[(_s, 'sq')] = _a
                if _s >= 1:
                    _a += 1; acC[(_s - 1, 'e2')] = _a

            def acSq(j):
                return acC[(j, 'sq')]

            def plOut(j):
                return j + 1

            with ExitStack() as pes, nc.Block() as blk:
                psW = pes.enter_context(nc.psum_tensor("psW", [128, D], F32))
                psF = pes.enter_context(nc.psum_tensor("psF", [128, D], F32))
                psF2 = pes.enter_context(nc.psum_tensor("psF2", [128, D], F32))
                psT = pes.enter_context(nc.psum_tensor("psT3", [128, D], F32))

                @blk.tensor
                def _(t):
                    sems.wait(t, "sVE", cp["brow_done"])
                    for i in range(NT):
                        if i >= 1:
                            sems.wait(t, "sVE", veA(i - 1))
                        for gch in range(NCH):
                            for nb in range(2):
                                t.matmul(psW[:, nb * 512:(nb + 1) * 512],
                                         A[:, gch * SEQ + i * 128: gch * SEQ + (i + 1) * 128],
                                         WA[:, gch * D + nb * 512: gch * D + (nb + 1) * 512],
                                         start=(gch == 0), stop=False)
                        for nb in range(2):
                            mm = t.matmul(psW[:, nb * 512:(nb + 1) * 512],
                                          E0[:, :], Brow[:, nb * 512:(nb + 1) * 512],
                                          start=False, stop=True)
                        assert sems.inc(mm, "sTE") == cnt_wo[i]
                    sems.wait(t, "sDWB", cp["wf_loaded"])
                    for ss in range(NT + 1):
                        if ss < NT:
                            j = ss
                            sems.wait(t, "sVE", veE[(j, 2)])
                            if j >= 1:
                                sems.wait(t, "sVE", veE[(j - 1, 6)])
                            xq = xqb[:, (j % 2) * D:(j % 2 + 1) * D]
                            for gch in range(NCH):
                                mm = t.matmul(psT[:, gch * 128:(gch + 1) * 128],
                                              xq[:, gch * 128:(gch + 1) * 128],
                                              idbf[:, :], start=True, stop=True)
                            assert sems.inc(mm, "sTE") == cnt_tr[j]
                        if ss >= 1:
                            j = ss - 1
                            psf = psF if j % 2 == 0 else psF2
                            sems.wait(t, "sVE", veE[(j, 6)])
                            if j >= 2:
                                sems.wait(t, "sVE", veE[(j - 2, 3)])
                            for gch in range(NCH):
                                for nb in range(2):
                                    mm = t.matmul(psf[:, nb * 512:(nb + 1) * 512],
                                                  xqT[:, (j % 2) * D + gch * 128:(j % 2) * D + (gch + 1) * 128],
                                                  WB[:, gch * D + nb * 512: gch * D + (nb + 1) * 512],
                                                  start=(gch == 0), stop=(gch == NCH - 1))
                            assert sems.inc(mm, "sTE") == cnt_ffn[j]

                @blk.sync
                def _(sync):
                    sems.wait(sync, "sVE", cp["brow_done"])
                    for i in range(NT):
                        if i >= 2:
                            sems.wait(sync, "sVE", veA(i - 2))
                        sems.inc(sync.dma_start(xstg[:, (i % 2) * D:(i % 2 + 1) * D],
                                                x_hbm[i * 128:(i + 1) * 128, :]),
                                 "sDX" + str(i % 2), 16)
                        sems.wait(sync, "sVE", veA(i))
                        sems.inc(sync.dma_start(out_hbm[i * 128:(i + 1) * 128, :],
                                                x1b[:, (i % 2) * D:(i % 2 + 1) * D]),
                                 "sDOB" + str(i % 2), 16)
                    for j in range(NT):
                        sems.wait(sync, "sPL", plOut(j))
                        sems.inc(sync.dma_start(out_hbm[j * 128:(j + 1) * 128, :],
                                                x1r[:, (j % 2) * D:(j % 2 + 1) * D]),
                                 "sDO" + str(j % 2), 16)
                    sems.wait(sync, "sDO0", 16 * 8)
                    sems.wait(sync, "sDO1", 16 * 8)

                @blk.gpsimd
                def _(g):
                    for j in range(2):
                        sems.wait(g, "sDOB" + str(j % 2), 16)
                        sems.inc(g.dma_start(x1r[:, (j % 2) * D:(j % 2 + 1) * D],
                                             out_hbm[j * 128:(j + 1) * 128, :]),
                                 "sDXL" + str(j % 2), 16)
                    for j in range(NT):
                        e2 = x1b[:, (j % 2) * D:(j % 2 + 1) * D]
                        x1sl = x1r[:, (j % 2) * D:(j % 2 + 1) * D]
                        sems.wait(g, "sAC", acC[(j, 'e2')])
                        g.tensor_tensor(e2, e2, B_fb[:, :], AL.add)
                        g.drain()
                        inst = g.tensor_tensor(x1sl, x1sl, e2, AL.add)
                        assert sems.inc(inst, "sPL") == plOut(j)
                        g.drain()

                @blk.scalar
                def _(sc):
                    sems.wait(sc, "sVE", VPE3)
                    inst = sc.activation(st3[:, 240:256], st3[:, 240:256], AF.Sqrt)
                    assert sems.inc(inst, "sAC") == bAC3 + 1
                    for ss in range(NT + 1):
                        if ss < NT:
                            j = ss
                            sems.wait(sc, "sDXL" + str(j % 2), 16 * (j // 2 + 1))
                            sems.wait(sc, "sVE", COMB3)
                            if j >= 1:
                                sems.wait(sc, "sVE", veE[(j - 1, 1)])
                            inst = sc.activation(sqb[:, :],
                                                 x1r[:, (j % 2) * D:(j % 2 + 1) * D],
                                                 AF.Identity, bias=st3[:, 208 + j:209 + j],
                                                 scale=st3[:, 192 + j:193 + j])
                            assert sems.inc(inst, "sAC") == acC[(j, 'sq')]
                        if ss >= 1:
                            j = ss - 1
                            sems.wait(sc, "sVE", veE[(j, 3)])
                            if j >= 2:
                                sems.wait(sc, "sPL", plOut(j - 2))
                            inst = sc.activation(x1b[:, (j % 2) * D:(j % 2 + 1) * D],
                                                 h2b[:, (j % 2) * D:(j % 2 + 1) * D],
                                                 AF.Identity,
                                                 scale=st3[:, 224 + j:225 + j])
                            assert sems.inc(inst, "sAC") == acC[(j, 'e2')]
                        if ss < NT and 2 <= ss + 1 < NT:
                            jj = ss + 1
                            sems.wait(sc, "sDOB" + str(jj % 2), 16 * (jj // 2 + 1))
                            sems.wait(sc, "sDO" + str(jj % 2), 16 * (jj // 2))
                            sems.inc(sc.dma_start(x1r[:, (jj % 2) * D:(jj % 2 + 1) * D],
                                                  out_hbm[jj * 128:(jj + 1) * 128, :]),
                                     "sDXL" + str(jj % 2), 16)

                @blk.vector
                def _(vraw):
                    st = st3
                    for i in range(NT):
                        psw = psW
                        xsl = xstg[:, (i % 2) * D:(i % 2 + 1) * D]
                        x1 = x1b[:, (i % 2) * D:(i % 2 + 1) * D]
                        sems.wait(vraw, "sTE", cnt_wo[i])
                        sems.wait(vraw, "sDX" + str(i % 2), bX[i % 2] + 16 * (i // 2 + 1))
                        if i >= 2:
                            sems.wait(vraw, "sDOB" + str(i % 2), 16 * (i // 2))
                        inst = vraw.tensor_tensor(x1, psw[:, :], xsl, AL.add)
                        assert sems.inc(inst, "sVE") == veA(i)
                        vraw.drain()
                        vraw.bn_stats(st[:, 96:102], x1[:, 0:512])
                        vraw.bn_stats(st[:, 102:108], x1[:, 512:1024])
                        vraw.tensor_reduce(st[:, 64 + i:65 + i], x1, AX.X, AL.max)
                        vraw.tensor_reduce(st[:, 80 + i:81 + i], x1, AX.X, AL.min)
                        vraw.drain()
                        vraw.bn_aggr(st[:, 2 * i:2 * i + 2], st[:, 96:102])
                        vraw.bn_aggr(st[:, 32 + 2 * i:34 + 2 * i], st[:, 102:108])
                    vraw.drain()
                    v = DrainAll(vraw)
                    stv = st3.rearrange("p (a b) -> p a b", b=2)
                    mA, vA = stv[:, 0:16, 0:1], stv[:, 0:16, 1:2]
                    mB, vB = stv[:, 16:32, 0:1], stv[:, 16:32, 1:2]
                    mx, mn = st[:, 64:80], st[:, 80:96]
                    m_, vpe = st[:, 112:128], st[:, 128:144]
                    tA, tB = st[:, 144:160], st[:, 160:176]
                    is_, isr = st[:, 176:192], st[:, 192:208]
                    nmisr, xsC = st[:, 208:224], st[:, 224:240]
                    v.tensor_tensor(m_, mA, mB, AL.add)
                    v.tensor_scalar(m_, m_, 0.5, None, AL.mult)
                    v.tensor_tensor(tA, mA, mB, AL.subtract)
                    v.tensor_tensor(tA, tA, tA, AL.mult)
                    v.tensor_tensor(vpe, vA, vB, AL.add)
                    v.tensor_scalar(vpe, vpe, 0.5, None, AL.mult)
                    v.tensor_scalar(tA, tA, 0.25, EPS_LN, AL.mult, AL.add)
                    v.tensor_tensor(vpe, vpe, tA, AL.add)
                    inst = v.tensor_copy(st[:, 240:256], vpe)
                    assert sems.inc(inst, "sVE") == VPE3
                    sems.wait(vraw, "sAC", bAC3 + 1)
                    v.reciprocal(is_, st[:, 240:256])
                    v.tensor_tensor(tB, is_, is_, AL.mult)
                    v.tensor_tensor(tB, tB, vpe, AL.mult)
                    v.tensor_scalar(tB, tB, 3.0, -0.5, AL.subtract, AL.mult)
                    v.tensor_tensor(is_, is_, tB, AL.mult)
                    v.tensor_tensor(tA, mx, m_, AL.subtract)
                    v.tensor_tensor(tB, m_, mn, AL.subtract)
                    v.tensor_tensor(tA, tA, tB, AL.max)
                    v.tensor_tensor(tA, tA, is_, AL.mult)
                    v.tensor_scalar(tA, tA, EPS_Q, None, AL.max)
                    v.tensor_scalar(xsC, tA, 1.0 / 127.0, None, AL.mult)
                    v.reciprocal(tB, tA)
                    v.tensor_scalar(tB, tB, 127.0, None, AL.mult)
                    v.tensor_tensor(isr, is_, tB, AL.mult)
                    v.tensor_tensor(nmisr, m_, isr, AL.mult)
                    inst = v.tensor_scalar(nmisr, nmisr, -1.0, None, AL.mult)
                    assert sems.inc(inst, "sVE") == COMB3
                    for ss in range(NT + 2):
                        if ss >= 2:
                            j = ss - 2
                            psf = psF if j % 2 == 0 else psF2
                            e1 = h2b[:, (j % 2) * D:(j % 2 + 1) * D]
                            sems.wait(vraw, "sTE", cnt_ffn[j])
                            if j >= 2:
                                sems.wait(vraw, "sAC", acC[(j - 2, 'e2')])
                            inst = vraw.tensor_tensor(e1, psf[:, :], B_ws[:, :], AL.mult)
                            assert sems.inc(inst, "sVE") == veE[(j, 3)]
                        if 1 <= ss <= NT:
                            j = ss - 1
                            sems.wait(vraw, "sTE", cnt_tr[j])
                            if j >= 2:
                                sems.wait(vraw, "sTE", cnt_ffn[j - 2])
                            inst = vraw.tensor_copy(
                                xqT[:, (j % 2) * D:(j % 2 + 1) * D], psT[:, :])
                            assert sems.inc(inst, "sVE") == veE[(j, 6)]
                        if ss < NT:
                            j = ss
                            sems.wait(vraw, "sAC", acSq(j))
                            if j >= 2:
                                sems.wait(vraw, "sTE", cnt_tr[j - 2])
                            inst = vraw.tensor_copy(iqb[:, :], sqb[:, :])
                            assert sems.inc(inst, "sVE") == veE[(j, 1)]
                            vraw.drain()
                            inst = vraw.tensor_copy(
                                xqb[:, (j % 2) * D:(j % 2 + 1) * D], iqb[:, :])
                            assert sems.inc(inst, "sVE") == veE[(j, 2)]
                    cp["p3_done"] = sems.c["sVE"]

        if _lim < len(_ORDER) - 1:
            with nc.Block() as blk:
                @blk.sync
                def _(sync):
                    for nm, v in list(sems.c.items()):
                        if v > 0:
                            sems.wait(sync, nm, v)
                    sems.inc(sync.dma_start(out_hbm[0:128, :], xstg[:, 0:D]), "sDO0", 16)
                    sems.wait(sync, "sDO0", sems.c["sDO0"])
        if debug:
            with nc.Block() as blk:
                @blk.vector
                def _(v):
                    for nm, val in list(sems.c.items()):
                        if val > 0 and nm not in ("sGP",):
                            sems.wait(v, nm, val)
                    v.engine_nop().then_inc(sems.h["sGP"], 1)
                    sems.c["sGP"] += 1

                @blk.sync
                def _(sync):
                    sems.wait(sync, "sGP", sems.c["sGP"])
                    ndma = 0
                    def dump(name, src, cast=True):
                        nonlocal ndma
                        pass
                @blk.gpsimd
                def _(g):
                    g.wait_ge(sems.h["sGP"], sems.c["sGP"])
                    nd = 0
                    _all = [("KP", KP[:, :]), ("A", A[:, :]),
                            ("Qwin", Qwin[:, :]), ("den", denomC[:, :]),
                            ("negM", negM[:, :]), ("B_o", B_o[:, :]),
                            ("B_ws", B_ws[:, :]), ("B_fb", B_fb[:, :]),
                            ("nrm", nrm[:, :]), ("perd", None),
                            ("x1b", x1b[:, :]), ("vwin", vwin[:, :]),
                            ("WA", WA[:, :]), ("WB", WB[:, :]),
                            ("Qwork", Qwork[:, :]),
                            ("xstg", xstg[:, :]), ("hstg", hstg[:, :]),
                            ("hstg2", hstg2[:, :]), ("smalls", smalls[:, :])]
                    if debug_names is not None:
                        _all = [p for p in _all if p[0] in debug_names]
                    for nm, src in _all:
                        if nm == "perd":
                            g.dma_start(dbg[nm][:, 0:NCH], perdq[:, :]).then_inc(sems.h["sDG"], 16)
                            g.dma_start(dbg[nm][:, 8:8 + H], perdk[:, :]).then_inc(sems.h["sDG"], 16)
                            sems.c["sDG"] += 32
                            nd += 2
                            continue
                        g.dma_start(dbg[nm][:, :], src).then_inc(sems.h["sDG"], 16)
                        sems.c["sDG"] += 16
                        nd += 1
                    g.wait_ge(sems.h["sDG"], sems.c["sDG"])
        sems.check()
    return nc


def _host_tables():
    idx = np.arange(32, dtype=np.float64)
    inv_freq = 10000.0 ** (-2.0 * idx / 64.0)
    t = np.arange(SEQ, dtype=np.float64)
    fr = np.outer(inv_freq, t)
    c32, s32 = np.cos(fr), np.sin(fr)
    cc = np.zeros((128, SEQ), np.float64)
    ss = np.zeros((128, SEQ), np.float64)
    for p in range(128):
        hd = p % 64
        cc[p] = c32[hd // 2]
        ss[p] = (-s32[hd // 2]) if hd % 2 == 0 else s32[hd // 2]
    swapm = np.zeros((128, 128), np.float32)
    for p in range(0, 128, 2):
        swapm[p + 1, p] = 1.0
        swapm[p, p + 1] = 1.0
    esel = np.zeros((128, H), np.float32)
    for hh in range(H):
        r0 = (hh % 2) * 64
        esel[r0:r0 + 64, hh] = 1.0
    isel = np.zeros((128, NCH, 128), np.float32)
    for gch in range(NCH):
        for p in range(128):
            isel[2 * gch + p // 64, gch, p] = 1.0
    maskq = np.zeros((16, NCH), np.float32)
    maskk = np.zeros((16, H), np.float32)
    for hh in range(H):
        maskq[hh, hh // 2] = 1.0
        maskk[hh, hh] = 1.0
    return (cc.astype(np.float32), ss.astype(np.float32), swapm, esel,
            isel.reshape(128, NCH * 128), maskq, maskk,
            np.eye(128, dtype=np.float32))


def kernel(L0_attn_x, L0_attn_Wq, L0_attn_bq, L0_attn_Wk, L0_attn_bk,
           L0_attn_Wv, L0_attn_bv, L0_attn_Wo, L0_attn_bo,
           L0_ffn_weight_q, L0_ffn_weight_scale, L0_ffn_bias):
    x = np.ascontiguousarray(np.asarray(L0_attn_x, dtype=np.float32))
    cc, ss, swapm, esel, isel, maskq, maskk, idm = _host_tables()
    shared = {
        "wqt": np.ascontiguousarray(np.asarray(L0_attn_Wq, np.float32).T),
        "wkt": np.ascontiguousarray(np.asarray(L0_attn_Wk, np.float32).T),
        "wvt": np.ascontiguousarray(np.asarray(L0_attn_Wv, np.float32).T),
        "wot": np.ascontiguousarray(np.asarray(L0_attn_Wo, np.float32).T),
        "wft": np.ascontiguousarray(np.asarray(L0_ffn_weight_q, np.int8).T),
        "bq": np.asarray(L0_attn_bq, np.float32),
        "bk": np.asarray(L0_attn_bk, np.float32),
        "bv": np.asarray(L0_attn_bv, np.float32),
        "bo": np.asarray(L0_attn_bo, np.float32),
        "ws": np.asarray(L0_ffn_weight_scale, np.float32),
        "fb": np.asarray(L0_ffn_bias, np.float32),
        "cc": cc, "ss": ss, "swapm": swapm, "esel": esel, "isel": isel,
        "maskq": maskq, "maskk": maskk, "idm": idm,
    }
    in_maps = [dict(shared, x=np.ascontiguousarray(x[c])) for c in range(8)]
    nc = build_program()
    res = run_bass_kernel_spmd(nc, in_maps, core_ids=list(range(8)))
    return np.stack([r["out"] for r in res.results], axis=0)


# version 12
# speedup vs baseline: 1.2520x; 1.2520x over previous
import numpy as np
import concourse.bass as bass
import concourse.mybir as mybir
import concourse.bass_isa as bass_isa
from concourse.bass_utils import run_bass_kernel_spmd
from contextlib import ExitStack

F32 = mybir.dt.float32
BF16 = mybir.dt.bfloat16
I8 = mybir.dt.int8
I32 = mybir.dt.int32
U8 = mybir.dt.uint8
AL = mybir.AluOpType
AF = mybir.ActivationFunctionType
AX = mybir.AxisListType

SEQ = 2048
D = 1024
H = 16
NT = 16
NCH = 8
EPS_LN = 1e-5
EPS_Q = 1e-6


class SemBox:
    def __init__(self, nc, es, names):
        self.h = {n: es.enter_context(nc.semaphore(n)) for n in names}
        self.c = {n: 0 for n in names}
        self.waits = []

    def inc(self, inst, name, amt=1):
        inst.then_inc(self.h[name], amt)
        self.c[name] += amt
        return self.c[name]

    def wait(self, eng, name, val):
        if val <= 0:
            return
        self.waits.append((name, val))
        eng.wait_ge(self.h[name], val)

    def check(self):
        for name, val in self.waits:
            assert val <= self.c[name], f"wait {name}>={val} but max {self.c[name]}"


class DrainAll:

    def __init__(self, eng):
        self._eng = eng

    def __getattr__(self, n):
        f = getattr(self._eng, n)
        if not callable(f):
            return f

        def g(*a, **k):
            r = f(*a, **k)
            self._eng.drain()
            return r
        return g


def build_program(upto="p3", debug=False, debug_names=None):
    _ORDER = ["init", "p0", "p1q", "p1k", "p1v", "attn", "n", "p3"]
    _lim = _ORDER.index(upto)
    def phase_on(name):
        return _ORDER.index(name) <= _lim

    nc = bass.Bass("TRN2", target_bir_lowering=False)

    x_hbm = nc.dram_tensor("x", [SEQ, D], F32, kind="ExternalInput")
    wqt = nc.dram_tensor("wqt", [D, D], F32, kind="ExternalInput")
    wkt = nc.dram_tensor("wkt", [D, D], F32, kind="ExternalInput")
    wvt = nc.dram_tensor("wvt", [D, D], F32, kind="ExternalInput")
    wot = nc.dram_tensor("wot", [D, D], F32, kind="ExternalInput")
    wft = nc.dram_tensor("wft", [D, D], I8, kind="ExternalInput")
    bq_h = nc.dram_tensor("bq", [D], F32, kind="ExternalInput")
    bk_h = nc.dram_tensor("bk", [D], F32, kind="ExternalInput")
    bv_h = nc.dram_tensor("bv", [D], F32, kind="ExternalInput")
    bo_h = nc.dram_tensor("bo", [D], F32, kind="ExternalInput")
    ws_h = nc.dram_tensor("ws", [D], F32, kind="ExternalInput")
    fb_h = nc.dram_tensor("fb", [D], F32, kind="ExternalInput")
    cc_h = nc.dram_tensor("cc", [128, SEQ], F32, kind="ExternalInput")
    ss_h = nc.dram_tensor("ss", [128, SEQ], F32, kind="ExternalInput")
    swap_h = nc.dram_tensor("swapm", [128, 128], F32, kind="ExternalInput")
    esel_h = nc.dram_tensor("esel", [128, H], F32, kind="ExternalInput")
    isel_h = nc.dram_tensor("isel", [128, NCH * 128], F32, kind="ExternalInput")
    mq_h = nc.dram_tensor("maskq", [16, NCH], F32, kind="ExternalInput")
    mk_h = nc.dram_tensor("maskk", [16, H], F32, kind="ExternalInput")
    idm_h = nc.dram_tensor("idm", [128, 128], F32, kind="ExternalInput")
    out_hbm = nc.dram_tensor("out", [SEQ, D], F32, kind="ExternalOutput")
    vspill = nc.dram_tensor("vspill", [SEQ, D], BF16, kind="Internal")
    qspill = nc.dram_tensor("qspill", [NCH * 128, SEQ], BF16, kind="Internal")
    if debug:
        dbg = {nm: nc.dram_tensor("dbg_" + nm, shp, dt, kind="ExternalOutput")
               for nm, shp, dt in [
                   ("KP", [128, H * SEQ], F32), ("A", [128, NCH * SEQ], F32),
                   ("Qwin", [128, 2 * SEQ], F32), ("den", [128, SEQ], F32),
                   ("negM", [128, 1], F32), ("B_o", [128, D], F32),
                   ("B_ws", [128, D], F32), ("B_fb", [128, D], F32),
                   ("nrm", [16, 64], F32), ("perd", [128, 24], F32),
                   ("x1b", [128, 2 * D], F32), ("vwin", [128, 2 * NT * 66], F32),
                   ("WA", [128, NCH * D], F32), ("WB", [128, NCH * D], F32),
                   ("Qwork", [128, 2 * SEQ], F32),
                   ("xstg", [128, 2 * D], F32), ("hstg", [128, D], F32),
                   ("hstg2", [128, D], F32), ("smalls", [128, 64], F32),
               ]}

    es = ExitStack()
    with es:
        sbuf = lambda *a: es.enter_context(nc.sbuf_tensor(*a))

        A = sbuf("A", [128, NCH * SEQ], BF16)
        KP = sbuf("KP", [128, H * SEQ], BF16)
        WA = sbuf("WA", [128, NCH * D], BF16)
        WB = sbuf("WB", [128, NCH * D], BF16)
        Qwork = sbuf("Qwork", [128, 2 * SEQ], BF16)
        Qwin = sbuf("Qwin", [128, 2 * SEQ], BF16)
        xstg = sbuf("xstg", [128, 2 * D], F32)
        R1 = sbuf("R1", [128, 8192], U8)
        R2 = sbuf("R2", [128, 8192], U8)
        R3 = sbuf("R3", [128, 8192], U8)
        R4 = sbuf("R4", [128, 8192], U8)
        vwin = sbuf("vwin", [128, 2 * NT * 66], BF16)
        iselb = sbuf("iselb", [128, NCH * 128], BF16)
        swapb = sbuf("swapb", [128, 128], BF16)
        eselb = sbuf("eselb", [128, H], BF16)
        maskq = sbuf("maskq_s", [16, NCH], F32)
        maskk = sbuf("maskk_s", [16, H], F32)
        smalls = sbuf("smalls", [128, 64], F32)
        perdq = sbuf("perdq", [128, NCH], F32)
        perdk = sbuf("perdk", [128, H], F32)
        perdqb = sbuf("perdqb", [128, NCH], BF16)
        perdkb = sbuf("perdkb", [128, H], BF16)
        nrm = sbuf("nrm", [16, 64], F32)
        negM = sbuf("negM", [128, 1], F32)
        E0 = sbuf("E0", [128, 128], BF16)
        negMb = sbuf("negMb", [128, 1], BF16)
        bqt = sbuf("bqt", [128, NCH], F32)
        bkt = sbuf("bkt", [128, NCH], F32)
        bvb = sbuf("bvb", [128, NCH], BF16)
        st3 = sbuf("st3", [128, 256], F32)
        idbf = sbuf("idbf", [128, 128], BF16)
        Brow = sbuf("Brow", [128, D], BF16)

        def at(base, off, name, shape, dtype):
            addr = nc.lookup_mloc(base).addr
            return nc.alloc_sbuf_tensor_at(name, shape, dtype, offset=addr + off)

        hstg = at(R1, 0, "hstg", [128, D], BF16)
        hstg2 = at(R1, 2048, "hstg2", [128, D], BF16)
        vstg = at(R1, 4096, "vstg", [128, 2 * D], BF16)
        h2b = at(R1, 0, "h2b", [128, 2 * D], F32)
        tmp1 = at(R2, 0, "tmp1", [128, SEQ], BF16)
        tmp2 = at(R2, 4096, "tmp2", [128, SEQ], BF16)
        denomC = at(R2, 0, "denomC", [128, SEQ], BF16)
        x1b = at(R2, 0, "x1b", [128, 2 * D], F32)
        ccb = at(R3, 0, "ccb", [128, SEQ], BF16)
        ssb = at(R3, 4096, "ssb", [128, SEQ], BF16)
        sqb = at(R3, 0, "sqb", [128, D], F32)
        iqb = at(R3, 4096, "iqb", [128, D], I32)
        Pt = at(R4, 0, "Pt", [128, 2 * 1024], BF16)
        stg = at(R4, 4096, "stg", [128, 2 * 1024], BF16)
        xqb = at(R4, 0, "xqb", [128, 2 * D], BF16)
        xqT = at(R4, 4096, "xqT", [128, 2 * D], BF16)
        B_o = at(Qwin, 0, "B_o", [128, D], F32)
        B_ws = at(Qwin, 4096, "B_ws", [128, D], F32)
        B_fb = at(Qwork, 0, "B_fb", [128, D], F32)
        rowbuf = at(Qwork, 4096, "rowbuf", [128, D], F32)
        rowbufb = at(R4, 4096, "rowbufb", [128, D], BF16)

        sems = SemBox(nc, es, [
            "sDX0", "sDX1", "sDI", "sDC", "sDWA", "sDWB",
            "sDT0", "sDT1", "sDVS0", "sDVS1", "sDVL0", "sDVL1",
            "sDQS0", "sDQS1", "sDQL0", "sDQL1", "sDSH0", "sDSH1",
            "sDRW", "sDO0", "sDO1", "sDOB0", "sDOB1", "sDXL0", "sDXL1",
            "sDG", "sTE", "sVE", "sAC", "sGP", "sPL"])
        cp = {}

        if phase_on('init'):
            with nc.Block() as blk:
                @blk.sync
                def _(sync):
                    sems.inc(sync.dma_start(bqt[:, :], bq_h.rearrange("(g p) -> p g", p=128), allow_slow_non_contiguous=True), "sDI", 16)
                    sems.inc(sync.dma_start(bkt[:, :], bk_h.rearrange("(g p) -> p g", p=128), allow_slow_non_contiguous=True), "sDI", 16)
                    sems.inc(sync.dma_start(maskq[:, :], mq_h[:, :]), "sDI", 16)
                    sems.inc(sync.dma_start(maskk[:, :], mk_h[:, :]), "sDI", 16)
                    cp["init_sync"] = sems.c["sDI"]

                @blk.gpsimd
                def _(g):
                    for dst, src in ((ccb, cc_h), (ssb, ss_h), (swapb, swap_h),
                                     (eselb, esel_h), (iselb, isel_h), (idbf, idm_h)):
                        sems.inc(g.dma_start(dst[:, :], src[:, :]), "sDC", 16)
                    sems.inc(g.dma_start(bvb[:, :], bv_h.rearrange("(g p) -> p g", p=128), allow_slow_non_contiguous=True), "sDC", 16)
                    cp["consts_loaded"] = sems.c["sDC"]
                    sems.inc(g.dma_start(WA[:, :], wqt.rearrange("(g p) e -> p g e", p=128)), "sDWA", 16)
                    cp["wq_loaded"] = sems.c["sDWA"]
                    sems.inc(g.dma_start(WB[:, :], wkt.rearrange("(g p) e -> p g e", p=128)), "sDWB", 16)
                    cp["wk_loaded"] = sems.c["sDWB"]

                @blk.vector
                def _(vraw):
                    v = vraw
                    v.memset(E0[:, :], 0)
                    v.drain()
                    v.memset(E0[0:1, :], 1.0)
                    v.memset(smalls[:, :], 0)
                    v.memset(nrm[:, :], 0)
                    v.memset(negM[:, :], 0)
                    v.memset(perdq[:, :], 0)
                    v.memset(perdk[:, :], 0)
                    v.memset(rowbufb[:, :], 0)
                    v.memset(Brow[:, :], 0)
                    v.memset(negMb[:, :], 0)
                    for hh in range(H):
                        v.memset(KP[:, hh * SEQ:(hh + 1) * SEQ], 0)
                    v.memset(vwin[:, :], 0)
                    v.drain()
                    inst = None
                    for j in range(2 * NT):
                        inst = v.memset(vwin[:, j * 66 + 64: j * 66 + 65], 1.0)
                    sems.inc(inst, "sVE")
                    cp["zeros_done"] = sems.c["sVE"]

        if phase_on('p0'):
            bVE0 = sems.c["sVE"]
            bAC0 = sems.c["sAC"]
            bTE0 = sems.c["sTE"]

            def veS(i):
                return bVE0 + 1 + i
            VPE0 = bVE0 + NT + 1
            COMB0 = bVE0 + NT + 2

            def veEv(i):
                return COMB0 + 1 + i

            def acH(i):
                return bAC0 + 2 + i

            def tr0(i):
                return bTE0 + 1 + i
            cp["p0_done"] = COMB0 + NT
            with ExitStack() as pes, nc.Block() as blk:
                psT0 = pes.enter_context(nc.psum_tensor("psT0", [128, D], BF16))
                psT1 = pes.enter_context(nc.psum_tensor("psT1", [128, D], BF16))

                @blk.sync
                def _(sync):
                    for i in range(NT):
                        if i >= 2:
                            sems.wait(sync, "sVE", veS(i - 2))
                        sems.inc(sync.dma_start(xstg[:, (i % 2) * D:(i % 2 + 1) * D],
                                                x_hbm[i * 128:(i + 1) * 128, :]),
                                 "sDX" + str(i % 2), 16)
                    for i in range(NT):
                        if i < 2:
                            sems.wait(sync, "sVE", veS(14 + i))
                        else:
                            sems.wait(sync, "sAC", acH(i - 2))
                        sems.inc(sync.dma_start(xstg[:, (i % 2) * D:(i % 2 + 1) * D],
                                                x_hbm[i * 128:(i + 1) * 128, :]),
                                 "sDX" + str(i % 2), 16)

                @blk.scalar
                def _(sc):
                    sems.wait(sc, "sVE", VPE0)
                    inst = sc.activation(st3[:, 208:224], st3[:, 208:224], AF.Sqrt)
                    assert sems.inc(inst, "sAC") == bAC0 + 1
                    for i in range(NT):
                        sems.wait(sc, "sVE", COMB0)
                        sems.wait(sc, "sDX" + str(i % 2), 16 * (8 + i // 2 + 1))
                        if i >= 2:
                            sems.wait(sc, "sTE", tr0(i - 2))
                        hh = hstg if i % 2 == 0 else hstg2
                        inst = sc.activation(hh[:, :], xstg[:, (i % 2) * D:(i % 2 + 1) * D],
                                             AF.Identity, bias=st3[:, 176 + i:177 + i],
                                             scale=st3[:, 160 + i:161 + i])
                        assert sems.inc(inst, "sAC") == acH(i)

                @blk.tensor
                def _(t):
                    sems.wait(t, "sDC", cp["consts_loaded"])
                    for i in range(NT):
                        sems.wait(t, "sAC", acH(i))
                        if i >= 2:
                            sems.wait(t, "sVE", veEv(i - 2))
                        hh = hstg if i % 2 == 0 else hstg2
                        pst = psT0 if i % 2 == 0 else psT1
                        for gch in range(NCH):
                            mm = t.transpose(pst[:, gch * 128:(gch + 1) * 128],
                                             hh[:, gch * 128:(gch + 1) * 128], idbf[:, :])
                        assert sems.inc(mm, "sTE") == tr0(i)

                @blk.vector
                def _(vraw):
                    st = st3
                    sA = smalls.rearrange("p (a b) -> p a b", b=2)
                    for i in range(NT):
                        sems.wait(vraw, "sDX" + str(i % 2), 16 * (i // 2 + 1))
                        xt = xstg[:, (i % 2) * D:(i % 2 + 1) * D]
                        vraw.bn_stats(st[:, 96:102], xt[:, 0:512])
                        vraw.bn_stats(st[:, 102:108], xt[:, 512:1024])
                        vraw.drain()
                        vraw.bn_aggr(smalls[:, 2 * i:2 * i + 2], st[:, 96:102])
                        inst = vraw.bn_aggr(smalls[:, 32 + 2 * i:34 + 2 * i], st[:, 102:108])
                        assert sems.inc(inst, "sVE") == veS(i)
                    vraw.drain()
                    v = DrainAll(vraw)
                    mA, vA = sA[:, 0:16, 0:1], sA[:, 0:16, 1:2]
                    mB, vB = sA[:, 16:32, 0:1], sA[:, 16:32, 1:2]
                    m_ = st[:, 112:128]
                    tA, tB = st[:, 128:144], st[:, 144:160]
                    is_, negmis = st[:, 160:176], st[:, 176:192]
                    vpe = st[:, 192:208]
                    v.tensor_tensor(m_, mA, mB, AL.add)
                    v.tensor_scalar(m_, m_, 0.5, None, AL.mult)
                    v.tensor_tensor(tA, mA, mB, AL.subtract)
                    v.tensor_tensor(tA, tA, tA, AL.mult)
                    v.tensor_tensor(vpe, vA, vB, AL.add)
                    v.tensor_scalar(vpe, vpe, 0.5, None, AL.mult)
                    v.tensor_scalar(tA, tA, 0.25, EPS_LN, AL.mult, AL.add)
                    v.tensor_tensor(vpe, vpe, tA, AL.add)
                    inst = v.tensor_copy(st[:, 208:224], vpe)
                    assert sems.inc(inst, "sVE") == VPE0
                    sems.wait(vraw, "sAC", bAC0 + 1)
                    v.reciprocal(is_, st[:, 208:224])
                    v.tensor_tensor(tB, is_, is_, AL.mult)
                    v.tensor_tensor(tB, tB, vpe, AL.mult)
                    v.tensor_scalar(tB, tB, 3.0, -0.5, AL.subtract, AL.mult)
                    v.tensor_tensor(is_, is_, tB, AL.mult)
                    v.tensor_tensor(negmis, m_, is_, AL.mult)
                    inst = v.tensor_scalar(negmis, negmis, -1.0, None, AL.mult)
                    assert sems.inc(inst, "sVE") == COMB0
                    for i in range(NT):
                        sems.wait(vraw, "sTE", tr0(i))
                        pst = psT0 if i % 2 == 0 else psT1
                        av = A.rearrange("p (g s) -> p g s", g=NCH)
                        inst = vraw.tensor_copy(av[:, :, i * 128:(i + 1) * 128],
                                                pst.rearrange("p (g c) -> p g c", g=NCH))
                        assert sems.inc(inst, "sVE") == veEv(i)

        if phase_on('p1q'):
            cnt_projq, cnt_swq = [], []
            bAC1 = sems.c["sAC"]
            bVE1 = sems.c["sVE"]
            with ExitStack() as pes, nc.Block() as blk:
                psP = pes.enter_context(nc.psum_tensor("psP", [128, SEQ], F32))
                psSW = pes.enter_context(nc.psum_tensor("psSW", [128, SEQ], F32))

                @blk.tensor
                def _(t):
                    sems.wait(t, "sVE", cp["p0_done"])
                    sems.wait(t, "sDWA", cp["wq_loaded"])
                    sems.wait(t, "sDC", cp["consts_loaded"])
                    for m in range(NCH + 1):
                        if m < NCH:
                            sems.wait(t, "sAC", bAC1 + m)
                            for nb in range(4):
                                for dd in range(NCH):
                                    mm = t.matmul(psP[:, nb * 512:(nb + 1) * 512],
                                                  WA[:, dd * D + m * 128: dd * D + (m + 1) * 128],
                                                  A[:, dd * SEQ + nb * 512: dd * SEQ + (nb + 1) * 512],
                                                  start=(dd == 0), stop=(dd == NCH - 1))
                            cnt_projq.append(sems.inc(mm, "sTE"))
                        if m >= 1:
                            j = m - 1
                            sems.wait(t, "sAC", bAC1 + j + 1)
                            if j >= 1:
                                sems.wait(t, "sVE", bVE1 + 4 * (j - 1) + 2)
                            src = Qwork[:, (j % 2) * SEQ:(j % 2 + 1) * SEQ]
                            for nb in range(4):
                                mm = t.matmul(psSW[:, nb * 512:(nb + 1) * 512], swapb[:, :],
                                              src[:, nb * 512:(nb + 1) * 512], start=True, stop=True)
                            cnt_swq.append(sems.inc(mm, "sTE"))

                @blk.scalar
                def _(sc):
                    sems.wait(sc, "sDI", 64)
                    for m in range(NCH):
                        sems.wait(sc, "sTE", cnt_projq[m])
                        if m >= 2:
                            sems.wait(sc, "sDQS" + str(m % 2), 16 * ((m - 2) // 2 + 1))
                            sems.wait(sc, "sVE", bVE1 + 4 * (m - 2) + 4)
                        inst = sc.activation(Qwork[:, (m % 2) * SEQ:(m % 2 + 1) * SEQ],
                                             psP[:, :], AF.Identity, bias=bqt[:, m:m + 1])
                        sems.inc(inst, "sAC")
                    cp["q_evac_done"] = sems.c["sAC"]

                @blk.vector
                def _(v):
                    for m in range(NCH):
                        src = Qwork[:, (m % 2) * SEQ:(m % 2 + 1) * SEQ]
                        sems.wait(v, "sTE", cnt_swq[m])
                        sems.inc(v.tensor_tensor(tmp1[:, :], src, ccb[:, :], AL.mult), "sVE")
                        sems.inc(v.tensor_tensor(src, psSW[:, :], ssb[:, :], AL.mult), "sVE")
                        v.drain()
                        sems.inc(v.tensor_tensor(src, src, tmp1[:, :], AL.add), "sVE")
                        v.drain()
                        inst = v.tensor_reduce(perdq[:, m:m + 1], src, AX.X, AL.max,
                                               apply_absolute_value=True)
                        sems.inc(inst, "sVE")
                        if m == NCH - 1:
                            v.drain()
                    cp["q_rope_done"] = sems.c["sVE"]

                @blk.gpsimd
                def _(g):
                    for m in range(NCH):
                        sems.wait(g, "sVE", bVE1 + 4 * m + 3)
                        inst = g.dma_start(qspill[m * 128:(m + 1) * 128, :],
                                           Qwork[:, (m % 2) * SEQ:(m % 2 + 1) * SEQ])
                        sems.inc(inst, "sDQS" + str(m % 2), 16)

        if phase_on('p1k'):
            cnt_projk, cnt_swk = [], []
            bACk = sems.c["sAC"]
            bVEk = sems.c["sVE"]
            with ExitStack() as pes, nc.Block() as blk:
                psP = pes.enter_context(nc.psum_tensor("psPk", [128, SEQ], F32))
                psSW = pes.enter_context(nc.psum_tensor("psSWk", [128, SEQ], F32))

                @blk.tensor
                def _(t):
                    sems.wait(t, "sDWB", cp["wk_loaded"])
                    sems.wait(t, "sVE", cp["zeros_done"])
                    for m in range(NCH + 1):
                        if m < NCH:
                            sems.wait(t, "sAC", bACk + m)
                            for nb in range(4):
                                for dd in range(NCH):
                                    mm = t.matmul(psP[:, nb * 512:(nb + 1) * 512],
                                                  WB[:, dd * D + m * 128: dd * D + (m + 1) * 128],
                                                  A[:, dd * SEQ + nb * 512: dd * SEQ + (nb + 1) * 512],
                                                  start=(dd == 0), stop=(dd == NCH - 1))
                            cnt_projk.append(sems.inc(mm, "sTE"))
                        if m >= 1:
                            for j in (2 * (m - 1), 2 * (m - 1) + 1):
                                sems.wait(t, "sAC", bACk + (m - 1) + 1)
                                if j >= 1:
                                    sems.wait(t, "sVE", bVEk + 4 * (j - 1) + 2)
                                src = KP[:, j * SEQ:(j + 1) * SEQ]
                                for nb in range(4):
                                    mm = t.matmul(psSW[:, nb * 512:(nb + 1) * 512], swapb[:, :],
                                                  src[:, nb * 512:(nb + 1) * 512],
                                                  start=True, stop=True)
                                cnt_swk.append(sems.inc(mm, "sTE"))

                @blk.scalar
                def _(sc):
                    sems.wait(sc, "sVE", cp["zeros_done"])
                    sems.wait(sc, "sDI", 64)
                    for m in range(NCH):
                        sems.wait(sc, "sTE", cnt_projk[m])
                        h0, h1 = 2 * m, 2 * m + 1
                        sc.activation(KP[0:64, h0 * SEQ:(h0 + 1) * SEQ], psP[0:64, :],
                                      AF.Identity, bias=bkt[0:64, m:m + 1])
                        inst = sc.activation(KP[64:128, h1 * SEQ:(h1 + 1) * SEQ],
                                             psP[64:128, :], AF.Identity,
                                             bias=bkt[64:128, m:m + 1])
                        sems.inc(inst, "sAC")
                    cp["k_evac_done"] = sems.c["sAC"]

                @blk.vector
                def _(v):
                    for j in range(H):
                        src = KP[:, j * SEQ:(j + 1) * SEQ]
                        sems.wait(v, "sTE", cnt_swk[j])
                        sems.inc(v.tensor_tensor(tmp1[:, :], src, ccb[:, :], AL.mult), "sVE")
                        sems.inc(v.tensor_tensor(src, psSW[:, :], ssb[:, :], AL.mult), "sVE")
                        v.drain()
                        sems.inc(v.tensor_tensor(src, src, tmp1[:, :], AL.add), "sVE")
                        v.drain()
                        inst = v.tensor_reduce(perdk[:, j:j + 1], src, AX.X, AL.max,
                                               apply_absolute_value=True)
                        sems.inc(inst, "sVE")
                        if j == H - 1:
                            v.drain()
                    cp["k_rope_done"] = sems.c["sVE"]

        if phase_on('p1v'):
            cnt_projv = []
            bACv = sems.c["sAC"]
            bVEv = sems.c["sVE"]
            cp["nb_sqrt_done"] = sems.c["sAC"] + NT + 1
            with ExitStack() as pes, nc.Block() as blk:
                psP = pes.enter_context(nc.psum_tensor("psPv", [128, SEQ], F32))
                psN = pes.enter_context(nc.psum_tensor("psN", [16, 64], F32))
                psNeg = pes.enter_context(nc.psum_tensor("psNeg", [128, 1], F32))

                @blk.gpsimd
                def _(g):
                    sems.wait(g, "sTE", cnt_projq[-1] if cnt_projq else 0)
                    sems.inc(g.dma_start(WA[:, :], wvt.rearrange("(g p) e -> p g e", p=128)), "sDWA", 16)
                    cp["wv_loaded"] = sems.c["sDWA"]

                @blk.tensor
                def _(t):
                    sems.wait(t, "sDWA", cp["wq_loaded"] + 16)
                    for i in range(NT):
                        half = (i % 2) * 1024
                        if i >= 2:
                            sems.wait(t, "sAC", bACv + (i - 2) + 1)
                        for dd in range(NCH):
                            for nb in range(2):
                                mm = t.matmul(psP[:, half + nb * 512: half + (nb + 1) * 512],
                                              A[:, dd * SEQ + i * 128: dd * SEQ + (i + 1) * 128],
                                              WA[:, dd * D + nb * 512: dd * D + (nb + 1) * 512],
                                              start=(dd == 0), stop=(dd == NCH - 1))
                        cnt_projv.append(sems.inc(mm, "sTE"))
                    sems.wait(t, "sVE", bVEv + 1)
                    t.matmul(psN[:, 0:NCH], eselb[:, :], perdqb[:, :], start=True, stop=True)
                    mm = t.matmul(psN[:, 32:32 + H], eselb[:, :], perdkb[:, :], start=True, stop=True)
                    cp["nb_mm_done"] = sems.inc(mm, "sTE")
                    sems.wait(t, "sAC", cp["nb_sqrt_done"])
                    mm = t.matmul(psNeg[:, :], E0[:, :], negMb[:, 0:1], start=True, stop=True)
                    cp["negmm_done"] = sems.inc(mm, "sTE")

                @blk.scalar
                def _(sc):
                    for i in range(NT):
                        half = (i % 2) * 1024
                        sems.wait(sc, "sTE", cnt_projv[i])
                        if i >= 2:
                            sems.wait(sc, "sDVS" + str(i % 2), 16 * ((i - 2) // 2 + 1))
                        inst = sc.activation(vstg[:, (i % 2) * D:(i % 2 + 1) * D],
                                             psP[:, half:half + D], AF.Copy)
                        sems.inc(inst, "sAC")
                    cp["v_evac_done"] = sems.c["sAC"]
                    sems.wait(sc, "sVE", bVEv + 3)
                    sc.activation(smalls[0:1, 33:34], smalls[0:1, 32:33], AF.Sqrt,
                                  scale=0.015625)
                    sc.drain()
                    sc.mul(smalls[0:1, 33:34], smalls[0:1, 33:34], -1.0)
                    sc.drain()
                    inst = sc.copy(negMb[0:1, 0:1], smalls[0:1, 33:34])
                    assert sems.c["sAC"] + 1 == cp["nb_sqrt_done"]
                    sems.inc(inst, "sAC")

                @blk.vector
                def _(vraw):
                    v = DrainAll(vraw)
                    sems.wait(vraw, "sDI", cp["init_sync"])
                    v.tensor_tensor(perdq[:, :], perdq[:, :], perdq[:, :], AL.mult)
                    v.tensor_tensor(perdk[:, :], perdk[:, :], perdk[:, :], AL.mult)
                    v.tensor_copy(perdqb[:, :], perdq[:, :])
                    inst = v.tensor_copy(perdkb[:, :], perdk[:, :])
                    cp["nb_sq_done"] = sems.inc(inst, "sVE")
                    sems.wait(vraw, "sTE", cp["nb_mm_done"])
                    v.tensor_tensor(nrm[:, 0:NCH], psN[:, 0:NCH], maskq[:, :], AL.mult)
                    v.tensor_tensor(nrm[:, 32:32 + H], psN[:, 32:32 + H], maskk[:, :], AL.mult)
                    v.tensor_reduce(nrm[:, 16:17], nrm[:, 0:NCH], AX.X, AL.add)
                    inst = v.tensor_reduce(nrm[:, 17:18], nrm[:, 32:32 + H], AX.X, AL.add)
                    cp["nb_red_done"] = sems.inc(inst, "sVE")
                    sems.wait(vraw, "sDG", 32)
                    v.tensor_reduce(smalls[0:1, 30:31], nrm[0:1, 20:36], AX.X, AL.max)
                    v.tensor_reduce(smalls[0:1, 31:32], nrm[0:1, 40:56], AX.X, AL.max)
                    inst = v.tensor_tensor(smalls[0:1, 32:33], smalls[0:1, 30:31],
                                           smalls[0:1, 31:32], AL.mult)
                    cp["nb_mul_done"] = sems.inc(inst, "sVE")
                    sems.wait(vraw, "sTE", cp["negmm_done"])
                    inst = v.tensor_copy(negM[:, :], psNeg[:, :])
                    cp["negM_ve"] = sems.inc(inst, "sVE")

                @blk.gpsimd
                def _(g):
                    basev = cp["v_evac_done"] - NT
                    for i in range(NT):
                        sems.wait(g, "sAC", basev + i + 1)
                        sems.inc(g.dma_start(vspill[i * 128:(i + 1) * 128, :],
                                             vstg[:, (i % 2) * D:(i % 2 + 1) * D]), "sDVS" + str(i % 2), 16)
                    sems.wait(g, "sVE", cp["nb_red_done"])
                    sems.inc(g.dma_start(nrm[0:1, 20:36], nrm[0:16, 16:17],
                                         allow_slow_non_contiguous=True), "sDG", 16)
                    sems.inc(g.dma_start(nrm[0:1, 40:56], nrm[0:16, 17:18],
                                         allow_slow_non_contiguous=True), "sDG", 16)
                    sems.wait(g, "sTE", cnt_projv[-1])
                    sems.inc(g.dma_start(WA[:, :], wot.rearrange("(g p) e -> p g e", p=128)), "sDWA", 16)
                    cp["wo_loaded"] = sems.c["sDWA"]
                    sems.inc(g.dma_start(WB[:, :], wft.rearrange("(g p) e -> p g e", p=128)), "sDWB", 16)
                    cp["wf_loaded"] = sems.c["sDWB"]

        if phase_on('attn'):
            bTEa = sems.c["sTE"]
            bACa = sems.c["sAC"]
            bVEa = sems.c["sVE"]

            NG = 2 * H * NT

            def s_cnt(g):
                return bTEa + (1 if g == 0 else 2 * g)

            def pv_cnt(g):
                return bTEa + (2 * g + 3 if g < NG - 1 else 2 * NG)

            def exp_cnt(g):
                return bACa + g + 1

            def drain_cnt(u):
                return bVEa + u + 1

            with ExitStack() as pes, nc.Block() as blk:
                Sa = pes.enter_context(nc.psum_tensor("Sa", [128, 1024], F32))
                Sb = pes.enter_context(nc.psum_tensor("Sb", [128, 1024], F32))
                PV0 = pes.enter_context(nc.psum_tensor("PV0", [65, 1024], F32))
                PV1 = pes.enter_context(nc.psum_tensor("PV1", [65, 1024], F32))

                @blk.tensor
                def _(t):
                    sems.wait(t, "sVE", cp["negM_ve"])

                    def emit_S(g):
                        u, tau = g // NT, g % NT
                        hh, sg = u // 2, u % 2
                        pp = hh // 2
                        if tau == 0 and sg == 0 and hh % 2 == 0:
                            sems.wait(t, "sDQL" + str(pp % 2), 16 * (pp // 2 + 1))
                        if tau == 0 and sg == 0:
                            sems.wait(t, "sDVL" + str(hh % 2), 16 * (hh // 2 + 1))
                        sps = Sa if g % 2 == 0 else Sb
                        if g >= 2:
                            sems.wait(t, "sAC", exp_cnt(g - 2))
                        qsl = Qwin[:, (pp % 2) * SEQ + sg * 1024:
                                   (pp % 2) * SEQ + (sg + 1) * 1024]
                        for nb in range(2):
                            mm = t.matmul(sps[:, nb * 512:(nb + 1) * 512],
                                          KP[:, hh * SEQ + tau * 128: hh * SEQ + (tau + 1) * 128],
                                          qsl[:, nb * 512:(nb + 1) * 512],
                                          start=True, stop=True)
                        sems.inc(mm, "sTE")

                    def emit_PV(g):
                        u, tau = g // NT, g % NT
                        hh = u // 2
                        pv = PV0 if u % 2 == 0 else PV1
                        if tau == 0 and u >= 2:
                            sems.wait(t, "sVE", drain_cnt(u - 2))
                        sems.wait(t, "sAC", exp_cnt(g))
                        pw = Pt[:, (g % 2) * 1024:(g % 2 + 1) * 1024]
                        vsl = vwin[:, (hh % 2) * NT * 66 + tau * 66:
                                   (hh % 2) * NT * 66 + tau * 66 + 65]
                        for nb in range(2):
                            mm = t.matmul(pv[:, nb * 512:(nb + 1) * 512], vsl,
                                          pw[:, nb * 512:(nb + 1) * 512],
                                          start=(tau == 0), stop=(tau == NT - 1))
                        sems.inc(mm, "sTE")

                    for g in range(NG):
                        emit_S(g)
                        if g >= 1:
                            emit_PV(g - 1)
                    emit_PV(NG - 1)
                    assert sems.c["sTE"] == bTEa + 2 * NG
                    cp["attn_te_done"] = sems.c["sTE"]

                @blk.scalar
                def _(sc):
                    for g in range(NG):
                        sps = Sa if g % 2 == 0 else Sb
                        sems.wait(sc, "sTE", s_cnt(g))
                        if g >= 2:
                            sems.wait(sc, "sTE", pv_cnt(g - 2))
                        inst = sc.activation(Pt[:, (g % 2) * 1024:(g % 2 + 1) * 1024],
                                             sps[:, :], AF.Exp, bias=negM[:, 0:1],
                                             scale=0.125)
                        sems.inc(inst, "sAC")
                    cp["attn_exp_done"] = sems.c["sAC"]

                @blk.vector
                def _(v):
                    v.memset(denomC[:, :], 1.0)
                    v.drain()
                    for u in range(2 * H):
                        pv = PV0 if u % 2 == 0 else PV1
                        sems.wait(v, "sTE", pv_cnt(u * NT + NT - 1))
                        if u >= 2:
                            sems.wait(v, "sDSH" + str(u % 2), 32 * ((u - 2) // 2 + 1))
                        inst = v.tensor_copy(stg[0:65, (u % 2) * 1024:(u % 2) * 1024 + 1024],
                                             pv[:, :])
                        sems.inc(inst, "sVE")
                    cp["attn_drain_done"] = sems.c["sVE"]

                @blk.gpsimd
                def _(g):
                    sems.wait(g, "sDQS0", 16 * 4)
                    sems.wait(g, "sDQS1", 16 * 4)
                    sems.wait(g, "sDVS0", 16 * 8)
                    sems.wait(g, "sDVS1", 16 * 8)
                    for pp in range(2):
                        sems.inc(g.dma_start(Qwin[:, (pp % 2) * SEQ:(pp % 2 + 1) * SEQ],
                                             qspill[pp * 128:(pp + 1) * 128, :]), "sDQL" + str(pp % 2), 16)
                    for hh in range(2):
                        dst = vwin.rearrange("p (w t c) -> p w t c", w=2, t=NT)
                        src = vspill.rearrange("(t p) e -> p t e", p=128)
                        sems.inc(g.dma_start(dst[:, hh % 2, :, 0:64],
                                             src[:, :, hh * 64:(hh + 1) * 64]), "sDVL" + str(hh % 2), 16)
                    for u in range(2 * H):
                        hh, sg = u // 2, u % 2
                        sems.wait(g, "sVE", drain_cnt(u))
                        ssl = stg[:, (u % 2) * 1024:(u % 2) * 1024 + 1024]
                        och, row0 = hh // 2, (hh % 2) * 64
                        sems.inc(g.dma_start(
                            A[row0:row0 + 64, och * SEQ + sg * 1024: och * SEQ + sg * 1024 + 1024],
                            ssl[0:64, :]), "sDSH" + str(u % 2), 16)
                        sems.inc(g.dma_start(
                            denomC[hh:hh + 1, sg * 1024:(sg + 1) * 1024],
                            ssl[64:65, :]), "sDSH" + str(u % 2), 16)
                        if sg == 1:
                            nh = hh + 2
                            if nh < H:
                                dst = vwin.rearrange("p (w t c) -> p w t c", w=2, t=NT)
                                src = vspill.rearrange("(t p) e -> p t e", p=128)
                                sems.inc(g.dma_start(dst[:, nh % 2, :, 0:64],
                                                     src[:, :, nh * 64:(nh + 1) * 64]),
                                         "sDVL" + str(nh % 2), 16)
                            if hh % 2 == 1:
                                np_ = hh // 2 + 2
                                if np_ < NCH:
                                    sems.inc(g.dma_start(
                                        Qwin[:, (np_ % 2) * SEQ:(np_ % 2 + 1) * SEQ],
                                        qspill[np_ * 128:(np_ + 1) * 128, :]), "sDQL" + str(np_ % 2), 16)

        if phase_on('n'):
            bTEn = sems.c["sTE"]
            bVEn = sems.c["sVE"]
            bACn = sems.c["sAC"]
            with ExitStack() as pes, nc.Block() as blk:
                psR = pes.enter_context(nc.psum_tensor("psR", [128, 1024], F32))
                psR2 = pes.enter_context(nc.psum_tensor("psR2", [128, 1024], F32))
                psBV = pes.enter_context(nc.psum_tensor("psBV", [1, D], F32))
                psBC = pes.enter_context(nc.psum_tensor("psBC", [128, D], F32))

                @blk.tensor
                def _(t):
                    sems.wait(t, "sVE", bVEn + 1)
                    for j in range(2 * NCH):
                        gch, sg = j // 2, j % 2
                        ps = psR if j % 2 == 0 else psR2
                        if j >= 2:
                            sems.wait(t, "sVE", bVEn + j)
                        for nb in range(2):
                            mm = t.matmul(ps[:, nb * 512:(nb + 1) * 512],
                                          iselb[:, gch * 128:(gch + 1) * 128],
                                          denomC[:, sg * 1024 + nb * 512: sg * 1024 + (nb + 1) * 512],
                                          start=True, stop=True)
                        sems.inc(mm, "sTE")
                    sems.wait(t, "sDWA", cp["wo_loaded"])
                    sems.wait(t, "sDC", cp["consts_loaded"])
                    for gch in range(NCH):
                        for nb in range(2):
                            mm = t.matmul(psBV[:, nb * 512:(nb + 1) * 512], bvb[:, gch:gch + 1],
                                          WA[:, gch * D + nb * 512: gch * D + (nb + 1) * 512],
                                          start=(gch == 0), stop=(gch == NCH - 1))
                    cp["bv_mm_done"] = sems.inc(mm, "sTE")
                    for k in range(3):
                        sems.wait(t, "sVE", bVEn + 18 + 2 * k)
                        for nb in range(2):
                            mm = t.matmul(psBC[:, nb * 512:(nb + 1) * 512], E0[:, :],
                                          rowbufb[:, nb * 512:(nb + 1) * 512],
                                          start=True, stop=True)
                        sems.inc(mm, "sTE")

                @blk.sync
                def _(sync):
                    sems.wait(sync, "sTE", cp["attn_te_done"])
                    sems.inc(sync.dma_start(rowbuf[0:1, :], ws_h[None, :]), "sDRW", 16)
                    sems.wait(sync, "sVE", bVEn + 20)
                    sems.inc(sync.dma_start(rowbuf[0:1, :], fb_h[None, :]), "sDRW", 16)
                    sems.wait(sync, "sVE", bVEn + 22)
                    sems.inc(sync.dma_start(rowbuf[0:1, :], bo_h[None, :]), "sDRW", 16)

                @blk.vector
                def _(vraw):
                    sems.wait(vraw, "sDSH0", 32 * 16)
                    sems.wait(vraw, "sDSH1", 32 * 16)
                    with nc.allow_low_precision(reason="bf16 softmax denom recip"):
                        inst = vraw.reciprocal(denomC[0:16, :], denomC[0:16, :])
                    sems.inc(inst, "sVE")
                    vraw.drain()
                    for j in range(2 * NCH):
                        gch, sg = j // 2, j % 2
                        ps = psR if j % 2 == 0 else psR2
                        sems.wait(vraw, "sTE", bTEn + j + 1)
                        osl = A[:, gch * SEQ + sg * 1024: gch * SEQ + sg * 1024 + 1024]
                        sems.inc(vraw.tensor_tensor(osl, osl, ps[:, :], AL.mult), "sVE")
                    cp["onorm_done"] = sems.c["sVE"]
                    v = DrainAll(vraw)
                    sems.wait(vraw, "sDRW", 16)
                    sems.inc(v.tensor_copy(rowbufb[0:1, :], rowbuf[0:1, :]), "sVE")
                    sems.wait(vraw, "sTE", bTEn + 18)
                    sems.inc(v.tensor_copy(B_ws[:, :], psBC[:, :]), "sVE")
                    v.tensor_copy(xstg[0:1, 0:D], rowbufb[0:1, :])
                    v.tensor_tensor(rowbuf[0:1, :], rowbuf[0:1, :], xstg[0:1, 0:D],
                                    AL.subtract)
                    sems.inc(v.tensor_copy(rowbufb[0:1, :], rowbuf[0:1, :]), "sVE")
                    sems.wait(vraw, "sTE", bTEn + 19)
                    sems.inc(v.tensor_tensor(B_ws[:, :], B_ws[:, :], psBC[:, :], AL.add), "sVE")
                    sems.wait(vraw, "sDRW", 32)
                    sems.inc(v.tensor_copy(rowbufb[0:1, :], rowbuf[0:1, :]), "sVE")
                    sems.wait(vraw, "sTE", bTEn + 20)
                    sems.inc(v.tensor_copy(B_fb[:, :], psBC[:, :]), "sVE")
                    sems.wait(vraw, "sDRW", 48)
                    sems.wait(vraw, "sTE", cp["bv_mm_done"])
                    v.tensor_tensor(rowbuf[0:1, :], rowbuf[0:1, :], psBV[:, :], AL.add)
                    inst = v.tensor_copy(Brow[0:1, :], rowbuf[0:1, :])
                    cp["brow_done"] = sems.inc(inst, "sVE")

        if phase_on('p3'):
            bVE3 = sems.c["sVE"]
            bAC3 = sems.c["sAC"]
            bTE3 = sems.c["sTE"]
            bX = (sems.c["sDX0"], sems.c["sDX1"])

            def veA(i):
                return bVE3 + i + 1
            VPE3 = bVE3 + NT + 1
            COMB3 = bVE3 + NT + 2
            veE = {}
            _c = COMB3
            for _s in range(NT + 2):
                if _s >= 2:
                    _c += 1; veE[(_s - 2, 3)] = _c
                if 1 <= _s <= NT:
                    _c += 1; veE[(_s - 1, 6)] = _c
                if _s < NT:
                    _c += 1; veE[(_s, 1)] = _c
                    _c += 1; veE[(_s, 2)] = _c
            acC = {}
            _a = bAC3 + 1
            for _s in range(NT + 1):
                if _s < NT:
                    _a += 1; acC[(_s, 'sq')] = _a
                if _s >= 1:
                    _a += 1; acC[(_s - 1, 'e2')] = _a
            cnt_wo = [bTE3 + 1 + i for i in range(NT)]
            cnt_tr, cnt_ffn = {}, {}
            _t = bTE3 + NT
            for _s in range(NT + 1):
                if _s < NT:
                    _t += 1; cnt_tr[_s] = _t
                if _s >= 1:
                    _t += 1; cnt_ffn[_s - 1] = _t

            def plOut(j):
                return j + 1

            x1r8 = at(KP, 0, "x1r8", [128, 8 * D], F32)
            e2b4 = at(KP, 32768, "e2b4", [128, 4 * D], F32)
            sqb1 = at(KP, 49152, "sqb1", [128, D], F32)
            iqb1 = at(KP, 53248, "iqb1", [128, D], I32)
            sqbs, iqbs = [sqb, sqb1], [iqb, iqb1]

            with ExitStack() as pes, nc.Block() as blk:
                psW = pes.enter_context(nc.psum_tensor("psW", [128, D], F32))
                psW2 = pes.enter_context(nc.psum_tensor("psW2", [128, D], F32))

                @blk.tensor
                def _(t):
                    sems.wait(t, "sVE", cp["brow_done"])
                    for i in range(NT):
                        psw = psW if i % 2 == 0 else psW2
                        if i >= 2:
                            sems.wait(t, "sVE", veA(i - 2))
                        for gch in range(NCH):
                            for nb in range(2):
                                t.matmul(psw[:, nb * 512:(nb + 1) * 512],
                                         A[:, gch * SEQ + i * 128: gch * SEQ + (i + 1) * 128],
                                         WA[:, gch * D + nb * 512: gch * D + (nb + 1) * 512],
                                         start=(gch == 0), stop=False)
                        for nb in range(2):
                            mm = t.matmul(psw[:, nb * 512:(nb + 1) * 512],
                                          E0[:, :], Brow[:, nb * 512:(nb + 1) * 512],
                                          start=False, stop=True)
                        assert sems.inc(mm, "sTE") == cnt_wo[i]

                @blk.sync
                def _(sync):
                    sems.wait(sync, "sVE", cp["brow_done"])
                    for i in range(NT):
                        if i >= 2:
                            sems.wait(sync, "sVE", veA(i - 2))
                        sems.inc(sync.dma_start(xstg[:, (i % 2) * D:(i % 2 + 1) * D],
                                                x_hbm[i * 128:(i + 1) * 128, :]),
                                 "sDX" + str(i % 2), 16)
                        sems.wait(sync, "sVE", veA(i))
                        sems.inc(sync.dma_start(out_hbm[i * 128:(i + 1) * 128, :],
                                                x1b[:, (i % 2) * D:(i % 2 + 1) * D]),
                                 "sDOB" + str(i % 2), 16)

                @blk.scalar
                def _(sc):
                    sems.wait(sc, "sVE", VPE3)
                    inst = sc.activation(st3[:, 240:256], st3[:, 240:256], AF.Sqrt)
                    assert sems.inc(inst, "sAC") == bAC3 + 1

                @blk.vector
                def _(vraw):
                    st = st3
                    for i in range(NT):
                        psw = psW if i % 2 == 0 else psW2
                        x1 = x1b[:, (i % 2) * D:(i % 2 + 1) * D]
                        sems.wait(vraw, "sTE", cnt_wo[i])
                        sems.wait(vraw, "sDX" + str(i % 2), bX[i % 2] + 16 * (i // 2 + 1))
                        if i >= 2:
                            sems.wait(vraw, "sDOB" + str(i % 2), 16 * (i // 2))
                        inst = vraw.tensor_tensor(x1, psw[:, :],
                                                  xstg[:, (i % 2) * D:(i % 2 + 1) * D], AL.add)
                        assert sems.inc(inst, "sVE") == veA(i)
                        vraw.drain()
                        vraw.bn_stats(st[:, 96:102], x1[:, 0:512])
                        vraw.bn_stats(st[:, 102:108], x1[:, 512:1024])
                        vraw.tensor_reduce(st[:, 64 + i:65 + i], x1, AX.X, AL.max)
                        vraw.tensor_reduce(st[:, 80 + i:81 + i], x1, AX.X, AL.min)
                        vraw.drain()
                        vraw.bn_aggr(st[:, 2 * i:2 * i + 2], st[:, 96:102])
                        vraw.bn_aggr(st[:, 32 + 2 * i:34 + 2 * i], st[:, 102:108])
                    vraw.drain()
                    v = DrainAll(vraw)
                    stv = st3.rearrange("p (a b) -> p a b", b=2)
                    mA, vA = stv[:, 0:16, 0:1], stv[:, 0:16, 1:2]
                    mB, vB = stv[:, 16:32, 0:1], stv[:, 16:32, 1:2]
                    mx, mn = st[:, 64:80], st[:, 80:96]
                    m_, vpe = st[:, 112:128], st[:, 128:144]
                    tA, tB = st[:, 144:160], st[:, 160:176]
                    is_, isr = st[:, 176:192], st[:, 192:208]
                    nmisr, xsC = st[:, 208:224], st[:, 224:240]
                    v.tensor_tensor(m_, mA, mB, AL.add)
                    v.tensor_scalar(m_, m_, 0.5, None, AL.mult)
                    v.tensor_tensor(tA, mA, mB, AL.subtract)
                    v.tensor_tensor(tA, tA, tA, AL.mult)
                    v.tensor_tensor(vpe, vA, vB, AL.add)
                    v.tensor_scalar(vpe, vpe, 0.5, None, AL.mult)
                    v.tensor_scalar(tA, tA, 0.25, EPS_LN, AL.mult, AL.add)
                    v.tensor_tensor(vpe, vpe, tA, AL.add)
                    inst = v.tensor_copy(st[:, 240:256], vpe)
                    assert sems.inc(inst, "sVE") == VPE3
                    sems.wait(vraw, "sAC", bAC3 + 1)
                    v.reciprocal(is_, st[:, 240:256])
                    v.tensor_tensor(tB, is_, is_, AL.mult)
                    v.tensor_tensor(tB, tB, vpe, AL.mult)
                    v.tensor_scalar(tB, tB, 3.0, -0.5, AL.subtract, AL.mult)
                    v.tensor_tensor(is_, is_, tB, AL.mult)
                    v.tensor_tensor(tA, mx, m_, AL.subtract)
                    v.tensor_tensor(tB, m_, mn, AL.subtract)
                    v.tensor_tensor(tA, tA, tB, AL.max)
                    v.tensor_tensor(tA, tA, is_, AL.mult)
                    v.tensor_scalar(tA, tA, EPS_Q, None, AL.max)
                    v.tensor_scalar(xsC, tA, 1.0 / 127.0, None, AL.mult)
                    v.reciprocal(tB, tA)
                    v.tensor_scalar(tB, tB, 127.0, None, AL.mult)
                    v.tensor_tensor(isr, is_, tB, AL.mult)
                    v.tensor_tensor(nmisr, m_, isr, AL.mult)
                    inst = v.tensor_scalar(nmisr, nmisr, -1.0, None, AL.mult)
                    assert sems.inc(inst, "sVE") == COMB3

            with ExitStack() as pes, nc.Block() as blk:
                psF = pes.enter_context(nc.psum_tensor("psF", [128, D], F32))
                psF2 = pes.enter_context(nc.psum_tensor("psF2", [128, D], F32))
                psT = pes.enter_context(nc.psum_tensor("psT3", [128, D], F32))
                psT2 = pes.enter_context(nc.psum_tensor("psT4", [128, D], F32))

                @blk.tensor
                def _(t):
                    sems.wait(t, "sDWB", cp["wf_loaded"])
                    for ss in range(NT + 1):
                        if ss < NT:
                            j = ss
                            sems.wait(t, "sVE", veE[(j, 2)])
                            if j >= 2:
                                sems.wait(t, "sVE", veE[(j - 2, 6)])
                            xq = xqb[:, (j % 2) * D:(j % 2 + 1) * D]
                            pst = psT if j % 2 == 0 else psT2
                            for gch in range(NCH):
                                mm = t.matmul(pst[:, gch * 128:(gch + 1) * 128],
                                              xq[:, gch * 128:(gch + 1) * 128],
                                              idbf[:, :], start=True, stop=True)
                            assert sems.inc(mm, "sTE") == cnt_tr[j]
                        if ss >= 1:
                            j = ss - 1
                            psf = psF if j % 2 == 0 else psF2
                            sems.wait(t, "sVE", veE[(j, 6)])
                            if j >= 2:
                                sems.wait(t, "sVE", veE[(j - 2, 3)])
                            for gch in range(NCH):
                                for nb in range(2):
                                    mm = t.matmul(psf[:, nb * 512:(nb + 1) * 512],
                                                  xqT[:, (j % 2) * D + gch * 128:(j % 2) * D + (gch + 1) * 128],
                                                  WB[:, gch * D + nb * 512: gch * D + (nb + 1) * 512],
                                                  start=(gch == 0), stop=(gch == NCH - 1))
                            assert sems.inc(mm, "sTE") == cnt_ffn[j]

                @blk.scalar
                def _(sc):
                    for ss in range(NT + 1):
                        if ss < NT:
                            j = ss
                            sems.wait(sc, "sDXL" + str(j % 2), 16 * (j // 2 + 1))
                            sems.wait(sc, "sVE", COMB3)
                            if j >= 2:
                                sems.wait(sc, "sVE", veE[(j - 2, 1)])
                            inst = sc.activation(sqbs[j % 2][:, :],
                                                 x1r8[:, (j % 8) * D:(j % 8 + 1) * D],
                                                 AF.Identity, bias=st3[:, 208 + j:209 + j],
                                                 scale=st3[:, 192 + j:193 + j])
                            assert sems.inc(inst, "sAC") == acC[(j, 'sq')]
                        if ss >= 1:
                            j = ss - 1
                            sems.wait(sc, "sVE", veE[(j, 3)])
                            if j >= 4:
                                sems.wait(sc, "sPL", plOut(j - 4))
                            inst = sc.activation(e2b4[:, (j % 4) * D:(j % 4 + 1) * D],
                                                 h2b[:, (j % 2) * D:(j % 2 + 1) * D],
                                                 AF.Identity,
                                                 scale=st3[:, 224 + j:225 + j])
                            assert sems.inc(inst, "sAC") == acC[(j, 'e2')]
                        if ss < NT and 2 <= ss + 1 < NT:
                            jj = ss + 1
                            sems.wait(sc, "sDOB" + str(jj % 2), 16 * (jj // 2 + 1))
                            if jj >= 8:
                                sems.wait(sc, "sDO" + str(jj % 2), 16 * ((jj - 8) // 2 + 1))
                            sems.inc(sc.dma_start(x1r8[:, (jj % 8) * D:(jj % 8 + 1) * D],
                                                  out_hbm[jj * 128:(jj + 1) * 128, :]),
                                     "sDXL" + str(jj % 2), 16)

                @blk.gpsimd
                def _(g):
                    for j in range(2):
                        sems.wait(g, "sDOB" + str(j % 2), 16)
                        sems.inc(g.dma_start(x1r8[:, (j % 8) * D:(j % 8 + 1) * D],
                                             out_hbm[j * 128:(j + 1) * 128, :]),
                                 "sDXL" + str(j % 2), 16)
                    for j in range(NT):
                        e2 = e2b4[:, (j % 4) * D:(j % 4 + 1) * D]
                        x1sl = x1r8[:, (j % 8) * D:(j % 8 + 1) * D]
                        sems.wait(g, "sAC", acC[(j, 'e2')])
                        g.tensor_tensor(e2, e2, B_fb[:, :], AL.add)
                        g.drain()
                        inst = g.tensor_tensor(x1sl, x1sl, e2, AL.add)
                        assert sems.inc(inst, "sPL") == plOut(j)
                        g.drain()

                @blk.vector
                def _(vraw):
                    for ss in range(NT + 2):
                        if ss >= 2:
                            j = ss - 2
                            psf = psF if j % 2 == 0 else psF2
                            e1 = h2b[:, (j % 2) * D:(j % 2 + 1) * D]
                            sems.wait(vraw, "sTE", cnt_ffn[j])
                            if j >= 2:
                                sems.wait(vraw, "sAC", acC[(j - 2, 'e2')])
                            inst = vraw.tensor_tensor(e1, psf[:, :], B_ws[:, :], AL.mult)
                            assert sems.inc(inst, "sVE") == veE[(j, 3)]
                        if 1 <= ss <= NT:
                            j = ss - 1
                            pst = psT if j % 2 == 0 else psT2
                            sems.wait(vraw, "sTE", cnt_tr[j])
                            if j >= 2:
                                sems.wait(vraw, "sTE", cnt_ffn[j - 2])
                            inst = vraw.tensor_copy(
                                xqT[:, (j % 2) * D:(j % 2 + 1) * D], pst[:, :])
                            assert sems.inc(inst, "sVE") == veE[(j, 6)]
                        if ss < NT:
                            j = ss
                            sems.wait(vraw, "sAC", acC[(j, 'sq')])
                            if j >= 2:
                                sems.wait(vraw, "sTE", cnt_tr[j - 2])
                            inst = vraw.tensor_copy(iqbs[j % 2][:, :], sqbs[j % 2][:, :])
                            assert sems.inc(inst, "sVE") == veE[(j, 1)]
                            vraw.drain()
                            inst = vraw.tensor_copy(
                                xqb[:, (j % 2) * D:(j % 2 + 1) * D], iqbs[j % 2][:, :])
                            assert sems.inc(inst, "sVE") == veE[(j, 2)]
                    cp["p3_done"] = sems.c["sVE"]

                @blk.sync
                def _(sync):
                    for j in range(NT):
                        sems.wait(sync, "sPL", plOut(j))
                        sems.inc(sync.dma_start(out_hbm[j * 128:(j + 1) * 128, :],
                                                x1r8[:, (j % 8) * D:(j % 8 + 1) * D]),
                                 "sDO" + str(j % 2), 16)
                    sems.wait(sync, "sDO0", 16 * 8)
                    sems.wait(sync, "sDO1", 16 * 8)

        if _lim < len(_ORDER) - 1:
            with nc.Block() as blk:
                @blk.sync
                def _(sync):
                    for nm, v in list(sems.c.items()):
                        if v > 0:
                            sems.wait(sync, nm, v)
                    sems.inc(sync.dma_start(out_hbm[0:128, :], xstg[:, 0:D]), "sDO0", 16)
                    sems.wait(sync, "sDO0", sems.c["sDO0"])
        if debug:
            with nc.Block() as blk:
                @blk.vector
                def _(v):
                    for nm, val in list(sems.c.items()):
                        if val > 0 and nm not in ("sGP",):
                            sems.wait(v, nm, val)
                    v.engine_nop().then_inc(sems.h["sGP"], 1)
                    sems.c["sGP"] += 1

                @blk.sync
                def _(sync):
                    sems.wait(sync, "sGP", sems.c["sGP"])
                    ndma = 0
                    def dump(name, src, cast=True):
                        nonlocal ndma
                        pass
                @blk.gpsimd
                def _(g):
                    g.wait_ge(sems.h["sGP"], sems.c["sGP"])
                    nd = 0
                    _all = [("KP", KP[:, :]), ("A", A[:, :]),
                            ("Qwin", Qwin[:, :]), ("den", denomC[:, :]),
                            ("negM", negM[:, :]), ("B_o", B_o[:, :]),
                            ("B_ws", B_ws[:, :]), ("B_fb", B_fb[:, :]),
                            ("nrm", nrm[:, :]), ("perd", None),
                            ("x1b", x1b[:, :]), ("vwin", vwin[:, :]),
                            ("WA", WA[:, :]), ("WB", WB[:, :]),
                            ("Qwork", Qwork[:, :]),
                            ("xstg", xstg[:, :]), ("hstg", hstg[:, :]),
                            ("hstg2", hstg2[:, :]), ("smalls", smalls[:, :])]
                    if debug_names is not None:
                        _all = [p for p in _all if p[0] in debug_names]
                    for nm, src in _all:
                        if nm == "perd":
                            g.dma_start(dbg[nm][:, 0:NCH], perdq[:, :]).then_inc(sems.h["sDG"], 16)
                            g.dma_start(dbg[nm][:, 8:8 + H], perdk[:, :]).then_inc(sems.h["sDG"], 16)
                            sems.c["sDG"] += 32
                            nd += 2
                            continue
                        g.dma_start(dbg[nm][:, :], src).then_inc(sems.h["sDG"], 16)
                        sems.c["sDG"] += 16
                        nd += 1
                    g.wait_ge(sems.h["sDG"], sems.c["sDG"])
        sems.check()
    return nc


def _host_tables():
    idx = np.arange(32, dtype=np.float64)
    inv_freq = 10000.0 ** (-2.0 * idx / 64.0)
    t = np.arange(SEQ, dtype=np.float64)
    fr = np.outer(inv_freq, t)
    c32, s32 = np.cos(fr), np.sin(fr)
    cc = np.zeros((128, SEQ), np.float64)
    ss = np.zeros((128, SEQ), np.float64)
    for p in range(128):
        hd = p % 64
        cc[p] = c32[hd // 2]
        ss[p] = (-s32[hd // 2]) if hd % 2 == 0 else s32[hd // 2]
    swapm = np.zeros((128, 128), np.float32)
    for p in range(0, 128, 2):
        swapm[p + 1, p] = 1.0
        swapm[p, p + 1] = 1.0
    esel = np.zeros((128, H), np.float32)
    for hh in range(H):
        r0 = (hh % 2) * 64
        esel[r0:r0 + 64, hh] = 1.0
    isel = np.zeros((128, NCH, 128), np.float32)
    for gch in range(NCH):
        for p in range(128):
            isel[2 * gch + p // 64, gch, p] = 1.0
    maskq = np.zeros((16, NCH), np.float32)
    maskk = np.zeros((16, H), np.float32)
    for hh in range(H):
        maskq[hh, hh // 2] = 1.0
        maskk[hh, hh] = 1.0
    return (cc.astype(np.float32), ss.astype(np.float32), swapm, esel,
            isel.reshape(128, NCH * 128), maskq, maskk,
            np.eye(128, dtype=np.float32))


def kernel(L0_attn_x, L0_attn_Wq, L0_attn_bq, L0_attn_Wk, L0_attn_bk,
           L0_attn_Wv, L0_attn_bv, L0_attn_Wo, L0_attn_bo,
           L0_ffn_weight_q, L0_ffn_weight_scale, L0_ffn_bias):
    x = np.ascontiguousarray(np.asarray(L0_attn_x, dtype=np.float32))
    cc, ss, swapm, esel, isel, maskq, maskk, idm = _host_tables()
    shared = {
        "wqt": np.ascontiguousarray(np.asarray(L0_attn_Wq, np.float32).T),
        "wkt": np.ascontiguousarray(np.asarray(L0_attn_Wk, np.float32).T),
        "wvt": np.ascontiguousarray(np.asarray(L0_attn_Wv, np.float32).T),
        "wot": np.ascontiguousarray(np.asarray(L0_attn_Wo, np.float32).T),
        "wft": np.ascontiguousarray(np.asarray(L0_ffn_weight_q, np.int8).T),
        "bq": np.asarray(L0_attn_bq, np.float32),
        "bk": np.asarray(L0_attn_bk, np.float32),
        "bv": np.asarray(L0_attn_bv, np.float32),
        "bo": np.asarray(L0_attn_bo, np.float32),
        "ws": np.asarray(L0_ffn_weight_scale, np.float32),
        "fb": np.asarray(L0_ffn_bias, np.float32),
        "cc": cc, "ss": ss, "swapm": swapm, "esel": esel, "isel": isel,
        "maskq": maskq, "maskk": maskk, "idm": idm,
    }
    in_maps = [dict(shared, x=np.ascontiguousarray(x[c])) for c in range(8)]
    nc = build_program()
    res = run_bass_kernel_spmd(nc, in_maps, core_ids=list(range(8)))
    return np.stack([r["out"] for r in res.results], axis=0)


# version 13
# speedup vs baseline: 1.2900x; 1.0304x over previous
import numpy as np
import concourse.bass as bass
import concourse.mybir as mybir
import concourse.bass_isa as bass_isa
from concourse.bass_utils import run_bass_kernel_spmd
from contextlib import ExitStack

F32 = mybir.dt.float32
BF16 = mybir.dt.bfloat16
I8 = mybir.dt.int8
I32 = mybir.dt.int32
U8 = mybir.dt.uint8
AL = mybir.AluOpType
AF = mybir.ActivationFunctionType
AX = mybir.AxisListType

SEQ = 2048
D = 1024
H = 16
NT = 16
NCH = 8
EPS_LN = 1e-5
EPS_Q = 1e-6


class SemBox:
    def __init__(self, nc, es, names):
        self.h = {n: es.enter_context(nc.semaphore(n)) for n in names}
        self.c = {n: 0 for n in names}
        self.waits = []

    def inc(self, inst, name, amt=1):
        inst.then_inc(self.h[name], amt)
        self.c[name] += amt
        return self.c[name]

    def wait(self, eng, name, val):
        if val <= 0:
            return
        self.waits.append((name, val))
        eng.wait_ge(self.h[name], val)

    def check(self):
        for name, val in self.waits:
            assert val <= self.c[name], f"wait {name}>={val} but max {self.c[name]}"


class DrainAll:

    def __init__(self, eng):
        self._eng = eng

    def __getattr__(self, n):
        f = getattr(self._eng, n)
        if not callable(f):
            return f

        def g(*a, **k):
            r = f(*a, **k)
            self._eng.drain()
            return r
        return g


def build_program(upto="p3", debug=False, debug_names=None):
    _ORDER = ["init", "p0", "p1q", "p1k", "p1v", "attn", "n", "p3"]
    _lim = _ORDER.index(upto)
    def phase_on(name):
        return _ORDER.index(name) <= _lim

    nc = bass.Bass("TRN2", target_bir_lowering=False)

    x_hbm = nc.dram_tensor("x", [SEQ, D], F32, kind="ExternalInput")
    wqt = nc.dram_tensor("wqt", [D, D], F32, kind="ExternalInput")
    wkt = nc.dram_tensor("wkt", [D, D], F32, kind="ExternalInput")
    wvt = nc.dram_tensor("wvt", [D, D], F32, kind="ExternalInput")
    wot = nc.dram_tensor("wot", [D, D], F32, kind="ExternalInput")
    wft = nc.dram_tensor("wft", [D, D], I8, kind="ExternalInput")
    bq_h = nc.dram_tensor("bq", [D], F32, kind="ExternalInput")
    bk_h = nc.dram_tensor("bk", [D], F32, kind="ExternalInput")
    bv_h = nc.dram_tensor("bv", [D], F32, kind="ExternalInput")
    bo_h = nc.dram_tensor("bo", [D], F32, kind="ExternalInput")
    ws_h = nc.dram_tensor("ws", [D], F32, kind="ExternalInput")
    fb_h = nc.dram_tensor("fb", [D], F32, kind="ExternalInput")
    cc_h = nc.dram_tensor("cc", [128, SEQ], F32, kind="ExternalInput")
    ss_h = nc.dram_tensor("ss", [128, SEQ], F32, kind="ExternalInput")
    swap_h = nc.dram_tensor("swapm", [128, 128], F32, kind="ExternalInput")
    esel_h = nc.dram_tensor("esel", [128, H], F32, kind="ExternalInput")
    isel_h = nc.dram_tensor("isel", [128, NCH * 128], F32, kind="ExternalInput")
    mq_h = nc.dram_tensor("maskq", [16, NCH], F32, kind="ExternalInput")
    mk_h = nc.dram_tensor("maskk", [16, H], F32, kind="ExternalInput")
    idm_h = nc.dram_tensor("idm", [128, 128], F32, kind="ExternalInput")
    out_hbm = nc.dram_tensor("out", [SEQ, D], F32, kind="ExternalOutput")
    vspill = nc.dram_tensor("vspill", [SEQ, D], BF16, kind="Internal")
    qspill = nc.dram_tensor("qspill", [NCH * 128, SEQ], BF16, kind="Internal")
    if debug:
        dbg = {nm: nc.dram_tensor("dbg_" + nm, shp, dt, kind="ExternalOutput")
               for nm, shp, dt in [
                   ("KP", [128, H * SEQ], F32), ("A", [128, NCH * SEQ], F32),
                   ("Qwin", [128, 2 * SEQ], F32), ("den", [128, SEQ], F32),
                   ("negM", [128, 1], F32), ("B_o", [128, D], F32),
                   ("B_ws", [128, D], F32), ("B_fb", [128, D], F32),
                   ("nrm", [16, 64], F32), ("perd", [128, 24], F32),
                   ("x1b", [128, 2 * D], F32), ("vwin", [128, 2 * NT * 66], F32),
                   ("WA", [128, NCH * D], F32), ("WB", [128, NCH * D], F32),
                   ("Qwork", [128, 2 * SEQ], F32),
                   ("xstg", [128, 2 * D], F32), ("hstg", [128, D], F32),
                   ("hstg2", [128, D], F32), ("smalls", [128, 64], F32),
               ]}

    es = ExitStack()
    with es:
        sbuf = lambda *a: es.enter_context(nc.sbuf_tensor(*a))

        A = sbuf("A", [128, NCH * SEQ], BF16)
        KP = sbuf("KP", [128, H * SEQ], BF16)
        WA = sbuf("WA", [128, NCH * D], BF16)
        WB = sbuf("WB", [128, NCH * D], BF16)
        Qwork = sbuf("Qwork", [128, 2 * SEQ], BF16)
        Qwin = sbuf("Qwin", [128, 2 * SEQ], BF16)
        xstg = sbuf("xstg", [128, 2 * D], F32)
        R1 = sbuf("R1", [128, 8192], U8)
        R2 = sbuf("R2", [128, 8192], U8)
        R3 = sbuf("R3", [128, 8192], U8)
        R4 = sbuf("R4", [128, 8192], U8)
        vwin = sbuf("vwin", [128, 2 * NT * 66], BF16)
        iselb = sbuf("iselb", [128, NCH * 128], BF16)
        swapb = sbuf("swapb", [128, 128], BF16)
        eselb = sbuf("eselb", [128, H], BF16)
        maskq = sbuf("maskq_s", [16, NCH], F32)
        maskk = sbuf("maskk_s", [16, H], F32)
        smalls = sbuf("smalls", [128, 64], F32)
        perdq = sbuf("perdq", [128, NCH], F32)
        perdk = sbuf("perdk", [128, H], F32)
        perdqb = sbuf("perdqb", [128, NCH], BF16)
        perdkb = sbuf("perdkb", [128, H], BF16)
        nrm = sbuf("nrm", [16, 64], F32)
        negM = sbuf("negM", [128, 1], F32)
        E0 = sbuf("E0", [128, 128], BF16)
        negMb = sbuf("negMb", [128, 1], BF16)
        bqt = sbuf("bqt", [128, NCH], F32)
        bkt = sbuf("bkt", [128, NCH], F32)
        bvb = sbuf("bvb", [128, NCH], BF16)
        st3 = sbuf("st3", [128, 256], F32)
        idbf = sbuf("idbf", [128, 128], BF16)
        Brow = sbuf("Brow", [128, D], BF16)

        def at(base, off, name, shape, dtype):
            addr = nc.lookup_mloc(base).addr
            return nc.alloc_sbuf_tensor_at(name, shape, dtype, offset=addr + off)

        hstg = at(R1, 0, "hstg", [128, D], BF16)
        hstg2 = at(R1, 2048, "hstg2", [128, D], BF16)
        vstg = at(R1, 4096, "vstg", [128, 2 * D], BF16)
        h2b = at(R1, 0, "h2b", [128, 2 * D], F32)
        tmp1 = at(R2, 0, "tmp1", [128, SEQ], BF16)
        tmp2 = at(R2, 4096, "tmp2", [128, SEQ], BF16)
        denomC = at(R2, 0, "denomC", [128, SEQ], BF16)
        x1b = at(R2, 0, "x1b", [128, 2 * D], F32)
        ccb = at(R3, 0, "ccb", [128, SEQ], BF16)
        ssb = at(R3, 4096, "ssb", [128, SEQ], BF16)
        sqb = at(R3, 0, "sqb", [128, D], F32)
        iqb = at(R3, 4096, "iqb", [128, D], I32)
        Pt = at(R4, 0, "Pt", [128, 2 * 1024], BF16)
        stg = at(R4, 4096, "stg", [128, 2 * 1024], BF16)
        xqb = at(R4, 0, "xqb", [128, 2 * D], BF16)
        xqT = at(R4, 4096, "xqT", [128, 2 * D], BF16)
        B_o = at(Qwin, 0, "B_o", [128, D], F32)
        B_ws = at(Qwin, 4096, "B_ws", [128, D], F32)
        B_fb = at(Qwork, 0, "B_fb", [128, D], F32)
        rowbuf = at(Qwork, 4096, "rowbuf", [128, D], F32)
        rowbufb = at(R4, 4096, "rowbufb", [128, D], BF16)

        sems = SemBox(nc, es, [
            "sDX0", "sDX1", "sDI", "sDC", "sDWA", "sDWB",
            "sDT0", "sDT1", "sDVS0", "sDVS1", "sDVL0", "sDVL1",
            "sDQS0", "sDQS1", "sDQL0", "sDQL1", "sDSH0", "sDSH1",
            "sDRW", "sDO0", "sDO1", "sDOB0", "sDOB1", "sDXL0", "sDXL1",
            "sDG", "sTE", "sVE", "sAC", "sGP", "sPL"])
        cp = {}

        if phase_on('init'):
            with nc.Block() as blk:
                @blk.sync
                def _(sync):
                    sems.inc(sync.dma_start(bqt[:, :], bq_h.rearrange("(g p) -> p g", p=128), allow_slow_non_contiguous=True), "sDI", 16)
                    sems.inc(sync.dma_start(bkt[:, :], bk_h.rearrange("(g p) -> p g", p=128), allow_slow_non_contiguous=True), "sDI", 16)
                    sems.inc(sync.dma_start(maskq[:, :], mq_h[:, :]), "sDI", 16)
                    sems.inc(sync.dma_start(maskk[:, :], mk_h[:, :]), "sDI", 16)
                    cp["init_sync"] = sems.c["sDI"]

                @blk.gpsimd
                def _(g):
                    for dst, src in ((ccb, cc_h), (ssb, ss_h), (swapb, swap_h),
                                     (eselb, esel_h), (iselb, isel_h), (idbf, idm_h)):
                        sems.inc(g.dma_start(dst[:, :], src[:, :]), "sDC", 16)
                    sems.inc(g.dma_start(bvb[:, :], bv_h.rearrange("(g p) -> p g", p=128), allow_slow_non_contiguous=True), "sDC", 16)
                    cp["consts_loaded"] = sems.c["sDC"]
                    sems.inc(g.dma_start(WA[:, :], wqt.rearrange("(g p) e -> p g e", p=128)), "sDWA", 16)
                    cp["wq_loaded"] = sems.c["sDWA"]
                    sems.inc(g.dma_start(WB[:, :], wkt.rearrange("(g p) e -> p g e", p=128)), "sDWB", 16)
                    cp["wk_loaded"] = sems.c["sDWB"]

                @blk.vector
                def _(vraw):
                    v = vraw
                    v.memset(E0[:, :], 0)
                    v.drain()
                    v.memset(E0[0:1, :], 1.0)
                    v.memset(smalls[:, :], 0)
                    v.memset(nrm[:, :], 0)
                    v.memset(negM[:, :], 0)
                    v.memset(perdq[:, :], 0)
                    v.memset(perdk[:, :], 0)
                    v.memset(rowbufb[:, :], 0)
                    v.memset(Brow[:, :], 0)
                    v.memset(negMb[:, :], 0)
                    for hh in range(H):
                        v.memset(KP[:, hh * SEQ:(hh + 1) * SEQ], 0)
                    v.memset(vwin[:, :], 0)
                    v.drain()
                    inst = None
                    for j in range(2 * NT):
                        inst = v.memset(vwin[:, j * 66 + 64: j * 66 + 65], 1.0)
                    sems.inc(inst, "sVE")
                    cp["zeros_done"] = sems.c["sVE"]

        if phase_on('p0'):
            bVE0 = sems.c["sVE"]
            bAC0 = sems.c["sAC"]
            bTE0 = sems.c["sTE"]

            def veS(i):
                return bVE0 + 1 + i
            VPE0 = bVE0 + NT + 1
            COMB0 = bVE0 + NT + 2

            def veEv(i):
                return COMB0 + 1 + i

            def acH(i):
                return bAC0 + 2 + i

            def tr0(i):
                return bTE0 + 1 + i
            cp["p0_done"] = COMB0 + NT
            with ExitStack() as pes, nc.Block() as blk:
                psT0 = pes.enter_context(nc.psum_tensor("psT0", [128, D], BF16))
                psT1 = pes.enter_context(nc.psum_tensor("psT1", [128, D], BF16))

                @blk.sync
                def _(sync):
                    for i in range(NT):
                        if i >= 2:
                            sems.wait(sync, "sVE", veS(i - 2))
                        sems.inc(sync.dma_start(xstg[:, (i % 2) * D:(i % 2 + 1) * D],
                                                x_hbm[i * 128:(i + 1) * 128, :]),
                                 "sDX" + str(i % 2), 16)
                    for i in range(NT):
                        if i < 2:
                            sems.wait(sync, "sVE", veS(14 + i))
                        else:
                            sems.wait(sync, "sAC", acH(i - 2))
                        sems.inc(sync.dma_start(xstg[:, (i % 2) * D:(i % 2 + 1) * D],
                                                x_hbm[i * 128:(i + 1) * 128, :]),
                                 "sDX" + str(i % 2), 16)

                @blk.scalar
                def _(sc):
                    sems.wait(sc, "sVE", VPE0)
                    inst = sc.activation(st3[:, 208:224], st3[:, 208:224], AF.Sqrt)
                    assert sems.inc(inst, "sAC") == bAC0 + 1
                    for i in range(NT):
                        sems.wait(sc, "sVE", COMB0)
                        sems.wait(sc, "sDX" + str(i % 2), 16 * (8 + i // 2 + 1))
                        if i >= 2:
                            sems.wait(sc, "sTE", tr0(i - 2))
                        hh = hstg if i % 2 == 0 else hstg2
                        inst = sc.activation(hh[:, :], xstg[:, (i % 2) * D:(i % 2 + 1) * D],
                                             AF.Identity, bias=st3[:, 176 + i:177 + i],
                                             scale=st3[:, 160 + i:161 + i])
                        assert sems.inc(inst, "sAC") == acH(i)

                @blk.tensor
                def _(t):
                    sems.wait(t, "sDC", cp["consts_loaded"])
                    for i in range(NT):
                        sems.wait(t, "sAC", acH(i))
                        if i >= 2:
                            sems.wait(t, "sVE", veEv(i - 2))
                        hh = hstg if i % 2 == 0 else hstg2
                        pst = psT0 if i % 2 == 0 else psT1
                        for gch in range(NCH):
                            mm = t.transpose(pst[:, gch * 128:(gch + 1) * 128],
                                             hh[:, gch * 128:(gch + 1) * 128], idbf[:, :])
                        assert sems.inc(mm, "sTE") == tr0(i)

                @blk.vector
                def _(vraw):
                    st = st3
                    sA = smalls.rearrange("p (a b) -> p a b", b=2)
                    for i in range(NT):
                        sems.wait(vraw, "sDX" + str(i % 2), 16 * (i // 2 + 1))
                        xt = xstg[:, (i % 2) * D:(i % 2 + 1) * D]
                        vraw.bn_stats(st[:, 96:102], xt[:, 0:512])
                        vraw.bn_stats(st[:, 102:108], xt[:, 512:1024])
                        vraw.drain()
                        vraw.bn_aggr(smalls[:, 2 * i:2 * i + 2], st[:, 96:102])
                        inst = vraw.bn_aggr(smalls[:, 32 + 2 * i:34 + 2 * i], st[:, 102:108])
                        assert sems.inc(inst, "sVE") == veS(i)
                    vraw.drain()
                    v = DrainAll(vraw)
                    mA, vA = sA[:, 0:16, 0:1], sA[:, 0:16, 1:2]
                    mB, vB = sA[:, 16:32, 0:1], sA[:, 16:32, 1:2]
                    m_ = st[:, 112:128]
                    tA, tB = st[:, 128:144], st[:, 144:160]
                    is_, negmis = st[:, 160:176], st[:, 176:192]
                    vpe = st[:, 192:208]
                    v.tensor_tensor(m_, mA, mB, AL.add)
                    v.tensor_scalar(m_, m_, 0.5, None, AL.mult)
                    v.tensor_tensor(tA, mA, mB, AL.subtract)
                    v.tensor_tensor(tA, tA, tA, AL.mult)
                    v.tensor_tensor(vpe, vA, vB, AL.add)
                    v.tensor_scalar(vpe, vpe, 0.5, None, AL.mult)
                    v.tensor_scalar(tA, tA, 0.25, EPS_LN, AL.mult, AL.add)
                    v.tensor_tensor(vpe, vpe, tA, AL.add)
                    inst = v.tensor_copy(st[:, 208:224], vpe)
                    assert sems.inc(inst, "sVE") == VPE0
                    sems.wait(vraw, "sAC", bAC0 + 1)
                    v.reciprocal(is_, st[:, 208:224])
                    v.tensor_tensor(tB, is_, is_, AL.mult)
                    v.tensor_tensor(tB, tB, vpe, AL.mult)
                    v.tensor_scalar(tB, tB, 3.0, -0.5, AL.subtract, AL.mult)
                    v.tensor_tensor(is_, is_, tB, AL.mult)
                    v.tensor_tensor(negmis, m_, is_, AL.mult)
                    inst = v.tensor_scalar(negmis, negmis, -1.0, None, AL.mult)
                    assert sems.inc(inst, "sVE") == COMB0
                    for i in range(NT):
                        sems.wait(vraw, "sTE", tr0(i))
                        pst = psT0 if i % 2 == 0 else psT1
                        av = A.rearrange("p (g s) -> p g s", g=NCH)
                        inst = vraw.tensor_copy(av[:, :, i * 128:(i + 1) * 128],
                                                pst.rearrange("p (g c) -> p g c", g=NCH))
                        assert sems.inc(inst, "sVE") == veEv(i)

        if phase_on('p1q'):
            cnt_projq, cnt_swq = [], []
            bAC1 = sems.c["sAC"]
            bVE1 = sems.c["sVE"]
            with ExitStack() as pes, nc.Block() as blk:
                psP = pes.enter_context(nc.psum_tensor("psP", [128, SEQ], F32))
                psSW = pes.enter_context(nc.psum_tensor("psSW", [128, SEQ], F32))

                @blk.tensor
                def _(t):
                    sems.wait(t, "sVE", cp["p0_done"])
                    sems.wait(t, "sDWA", cp["wq_loaded"])
                    sems.wait(t, "sDC", cp["consts_loaded"])
                    for m in range(NCH + 1):
                        if m < NCH:
                            sems.wait(t, "sAC", bAC1 + m)
                            for nb in range(4):
                                for dd in range(NCH):
                                    mm = t.matmul(psP[:, nb * 512:(nb + 1) * 512],
                                                  WA[:, dd * D + m * 128: dd * D + (m + 1) * 128],
                                                  A[:, dd * SEQ + nb * 512: dd * SEQ + (nb + 1) * 512],
                                                  start=(dd == 0), stop=(dd == NCH - 1))
                            cnt_projq.append(sems.inc(mm, "sTE"))
                        if m >= 1:
                            j = m - 1
                            sems.wait(t, "sAC", bAC1 + j + 1)
                            if j >= 1:
                                sems.wait(t, "sVE", bVE1 + 4 * (j - 1) + 2)
                            src = Qwork[:, (j % 2) * SEQ:(j % 2 + 1) * SEQ]
                            for nb in range(4):
                                mm = t.matmul(psSW[:, nb * 512:(nb + 1) * 512], swapb[:, :],
                                              src[:, nb * 512:(nb + 1) * 512], start=True, stop=True)
                            cnt_swq.append(sems.inc(mm, "sTE"))

                @blk.scalar
                def _(sc):
                    sems.wait(sc, "sDI", 64)
                    for m in range(NCH):
                        sems.wait(sc, "sTE", cnt_projq[m])
                        if m >= 2:
                            sems.wait(sc, "sDQS" + str(m % 2), 16 * ((m - 2) // 2 + 1))
                            sems.wait(sc, "sVE", bVE1 + 4 * (m - 2) + 4)
                        inst = sc.activation(Qwork[:, (m % 2) * SEQ:(m % 2 + 1) * SEQ],
                                             psP[:, :], AF.Identity, bias=bqt[:, m:m + 1])
                        sems.inc(inst, "sAC")
                    cp["q_evac_done"] = sems.c["sAC"]

                @blk.vector
                def _(v):
                    for m in range(NCH):
                        src = Qwork[:, (m % 2) * SEQ:(m % 2 + 1) * SEQ]
                        sems.wait(v, "sTE", cnt_swq[m])
                        sems.inc(v.tensor_tensor(tmp1[:, :], src, ccb[:, :], AL.mult), "sVE")
                        sems.inc(v.tensor_tensor(src, psSW[:, :], ssb[:, :], AL.mult), "sVE")
                        v.drain()
                        sems.inc(v.tensor_tensor(src, src, tmp1[:, :], AL.add), "sVE")
                        v.drain()
                        inst = v.tensor_reduce(perdq[:, m:m + 1], src, AX.X, AL.max,
                                               apply_absolute_value=True)
                        sems.inc(inst, "sVE")
                        if m == NCH - 1:
                            v.drain()
                    cp["q_rope_done"] = sems.c["sVE"]

                @blk.gpsimd
                def _(g):
                    for m in range(NCH):
                        sems.wait(g, "sVE", bVE1 + 4 * m + 3)
                        inst = g.dma_start(qspill[m * 128:(m + 1) * 128, :],
                                           Qwork[:, (m % 2) * SEQ:(m % 2 + 1) * SEQ])
                        sems.inc(inst, "sDQS" + str(m % 2), 16)

        if phase_on('p1k'):
            cnt_projk, cnt_swk = [], []
            bACk = sems.c["sAC"]
            bVEk = sems.c["sVE"]
            with ExitStack() as pes, nc.Block() as blk:
                psP = pes.enter_context(nc.psum_tensor("psPk", [128, SEQ], F32))
                psSW = pes.enter_context(nc.psum_tensor("psSWk", [128, SEQ], F32))

                @blk.tensor
                def _(t):
                    sems.wait(t, "sDWB", cp["wk_loaded"])
                    sems.wait(t, "sVE", cp["zeros_done"])
                    for m in range(NCH + 1):
                        if m < NCH:
                            sems.wait(t, "sAC", bACk + m)
                            for nb in range(4):
                                for dd in range(NCH):
                                    mm = t.matmul(psP[:, nb * 512:(nb + 1) * 512],
                                                  WB[:, dd * D + m * 128: dd * D + (m + 1) * 128],
                                                  A[:, dd * SEQ + nb * 512: dd * SEQ + (nb + 1) * 512],
                                                  start=(dd == 0), stop=(dd == NCH - 1))
                            cnt_projk.append(sems.inc(mm, "sTE"))
                        if m >= 1:
                            for j in (2 * (m - 1), 2 * (m - 1) + 1):
                                sems.wait(t, "sAC", bACk + (m - 1) + 1)
                                if j >= 1:
                                    sems.wait(t, "sVE", bVEk + 4 * (j - 1) + 2)
                                src = KP[:, j * SEQ:(j + 1) * SEQ]
                                for nb in range(4):
                                    mm = t.matmul(psSW[:, nb * 512:(nb + 1) * 512], swapb[:, :],
                                                  src[:, nb * 512:(nb + 1) * 512],
                                                  start=True, stop=True)
                                cnt_swk.append(sems.inc(mm, "sTE"))

                @blk.scalar
                def _(sc):
                    sems.wait(sc, "sVE", cp["zeros_done"])
                    sems.wait(sc, "sDI", 64)
                    for m in range(NCH):
                        sems.wait(sc, "sTE", cnt_projk[m])
                        h0, h1 = 2 * m, 2 * m + 1
                        sc.activation(KP[0:64, h0 * SEQ:(h0 + 1) * SEQ], psP[0:64, :],
                                      AF.Identity, bias=bkt[0:64, m:m + 1])
                        inst = sc.activation(KP[64:128, h1 * SEQ:(h1 + 1) * SEQ],
                                             psP[64:128, :], AF.Identity,
                                             bias=bkt[64:128, m:m + 1])
                        sems.inc(inst, "sAC")
                    cp["k_evac_done"] = sems.c["sAC"]

                @blk.vector
                def _(v):
                    for j in range(H):
                        src = KP[:, j * SEQ:(j + 1) * SEQ]
                        sems.wait(v, "sTE", cnt_swk[j])
                        sems.inc(v.tensor_tensor(tmp1[:, :], src, ccb[:, :], AL.mult), "sVE")
                        sems.inc(v.tensor_tensor(src, psSW[:, :], ssb[:, :], AL.mult), "sVE")
                        v.drain()
                        sems.inc(v.tensor_tensor(src, src, tmp1[:, :], AL.add), "sVE")
                        v.drain()
                        inst = v.tensor_reduce(perdk[:, j:j + 1], src, AX.X, AL.max,
                                               apply_absolute_value=True)
                        sems.inc(inst, "sVE")
                        if j == H - 1:
                            v.drain()
                    cp["k_rope_done"] = sems.c["sVE"]

        if phase_on('p1v'):
            cnt_projv = []
            bACv = sems.c["sAC"]
            bVEv = sems.c["sVE"]
            cp["nb_sqrt_done"] = sems.c["sAC"] + NT + 1
            with ExitStack() as pes, nc.Block() as blk:
                psP = pes.enter_context(nc.psum_tensor("psPv", [128, SEQ], F32))
                psN = pes.enter_context(nc.psum_tensor("psN", [16, 64], F32))
                psNeg = pes.enter_context(nc.psum_tensor("psNeg", [128, 1], F32))

                @blk.gpsimd
                def _(g):
                    sems.wait(g, "sTE", cnt_projq[-1] if cnt_projq else 0)
                    sems.inc(g.dma_start(WA[:, :], wvt.rearrange("(g p) e -> p g e", p=128)), "sDWA", 16)
                    cp["wv_loaded"] = sems.c["sDWA"]

                @blk.tensor
                def _(t):
                    sems.wait(t, "sDWA", cp["wq_loaded"] + 16)
                    for i in range(NT):
                        half = (i % 2) * 1024
                        if i >= 2:
                            sems.wait(t, "sAC", bACv + (i - 2) + 1)
                        for dd in range(NCH):
                            for nb in range(2):
                                mm = t.matmul(psP[:, half + nb * 512: half + (nb + 1) * 512],
                                              A[:, dd * SEQ + i * 128: dd * SEQ + (i + 1) * 128],
                                              WA[:, dd * D + nb * 512: dd * D + (nb + 1) * 512],
                                              start=(dd == 0), stop=(dd == NCH - 1))
                        cnt_projv.append(sems.inc(mm, "sTE"))
                    sems.wait(t, "sVE", bVEv + 1)
                    t.matmul(psN[:, 0:NCH], eselb[:, :], perdqb[:, :], start=True, stop=True)
                    mm = t.matmul(psN[:, 32:32 + H], eselb[:, :], perdkb[:, :], start=True, stop=True)
                    cp["nb_mm_done"] = sems.inc(mm, "sTE")
                    sems.wait(t, "sAC", cp["nb_sqrt_done"])
                    mm = t.matmul(psNeg[:, :], E0[:, :], negMb[:, 0:1], start=True, stop=True)
                    cp["negmm_done"] = sems.inc(mm, "sTE")

                @blk.scalar
                def _(sc):
                    for i in range(NT):
                        half = (i % 2) * 1024
                        sems.wait(sc, "sTE", cnt_projv[i])
                        if i >= 2:
                            sems.wait(sc, "sDVS" + str(i % 2), 16 * ((i - 2) // 2 + 1))
                        inst = sc.activation(vstg[:, (i % 2) * D:(i % 2 + 1) * D],
                                             psP[:, half:half + D], AF.Copy)
                        sems.inc(inst, "sAC")
                    cp["v_evac_done"] = sems.c["sAC"]
                    sems.wait(sc, "sVE", bVEv + 3)
                    sc.activation(smalls[0:1, 33:34], smalls[0:1, 32:33], AF.Sqrt,
                                  scale=0.015625)
                    sc.drain()
                    sc.mul(smalls[0:1, 33:34], smalls[0:1, 33:34], -1.0)
                    sc.drain()
                    inst = sc.copy(negMb[0:1, 0:1], smalls[0:1, 33:34])
                    assert sems.c["sAC"] + 1 == cp["nb_sqrt_done"]
                    sems.inc(inst, "sAC")

                @blk.vector
                def _(vraw):
                    v = DrainAll(vraw)
                    sems.wait(vraw, "sDI", cp["init_sync"])
                    v.tensor_tensor(perdq[:, :], perdq[:, :], perdq[:, :], AL.mult)
                    v.tensor_tensor(perdk[:, :], perdk[:, :], perdk[:, :], AL.mult)
                    v.tensor_copy(perdqb[:, :], perdq[:, :])
                    inst = v.tensor_copy(perdkb[:, :], perdk[:, :])
                    cp["nb_sq_done"] = sems.inc(inst, "sVE")
                    sems.wait(vraw, "sTE", cp["nb_mm_done"])
                    v.tensor_tensor(nrm[:, 0:NCH], psN[:, 0:NCH], maskq[:, :], AL.mult)
                    v.tensor_tensor(nrm[:, 32:32 + H], psN[:, 32:32 + H], maskk[:, :], AL.mult)
                    v.tensor_reduce(nrm[:, 16:17], nrm[:, 0:NCH], AX.X, AL.add)
                    inst = v.tensor_reduce(nrm[:, 17:18], nrm[:, 32:32 + H], AX.X, AL.add)
                    cp["nb_red_done"] = sems.inc(inst, "sVE")
                    sems.wait(vraw, "sDG", 32)
                    v.tensor_reduce(smalls[0:1, 30:31], nrm[0:1, 20:36], AX.X, AL.max)
                    v.tensor_reduce(smalls[0:1, 31:32], nrm[0:1, 40:56], AX.X, AL.max)
                    inst = v.tensor_tensor(smalls[0:1, 32:33], smalls[0:1, 30:31],
                                           smalls[0:1, 31:32], AL.mult)
                    cp["nb_mul_done"] = sems.inc(inst, "sVE")
                    sems.wait(vraw, "sTE", cp["negmm_done"])
                    inst = v.tensor_copy(negM[:, :], psNeg[:, :])
                    cp["negM_ve"] = sems.inc(inst, "sVE")

                @blk.gpsimd
                def _(g):
                    basev = cp["v_evac_done"] - NT
                    for i in range(NT):
                        sems.wait(g, "sAC", basev + i + 1)
                        sems.inc(g.dma_start(vspill[i * 128:(i + 1) * 128, :],
                                             vstg[:, (i % 2) * D:(i % 2 + 1) * D]), "sDVS" + str(i % 2), 16)
                    sems.wait(g, "sVE", cp["nb_red_done"])
                    sems.inc(g.dma_start(nrm[0:1, 20:36], nrm[0:16, 16:17],
                                         allow_slow_non_contiguous=True), "sDG", 16)
                    sems.inc(g.dma_start(nrm[0:1, 40:56], nrm[0:16, 17:18],
                                         allow_slow_non_contiguous=True), "sDG", 16)
                    sems.wait(g, "sTE", cnt_projv[-1])
                    sems.inc(g.dma_start(WA[:, :], wot.rearrange("(g p) e -> p g e", p=128)), "sDWA", 16)
                    cp["wo_loaded"] = sems.c["sDWA"]
                    sems.inc(g.dma_start(WB[:, :], wft.rearrange("(g p) e -> p g e", p=128)), "sDWB", 16)
                    cp["wf_loaded"] = sems.c["sDWB"]

        if phase_on('attn'):
            bTEa = sems.c["sTE"]
            bACa = sems.c["sAC"]
            bVEa = sems.c["sVE"]

            NG = 2 * H * NT

            def s_cnt(g):
                return bTEa + (1 if g == 0 else 2 * g)

            def pv_cnt(g):
                return bTEa + (2 * g + 3 if g < NG - 1 else 2 * NG)

            def exp_cnt(g):
                return bACa + g + 1

            def drain_cnt(u):
                return bVEa + u + 1

            with ExitStack() as pes, nc.Block() as blk:
                Sa = pes.enter_context(nc.psum_tensor("Sa", [128, 1024], F32))
                Sb = pes.enter_context(nc.psum_tensor("Sb", [128, 1024], F32))
                PV0 = pes.enter_context(nc.psum_tensor("PV0", [65, 1024], F32))
                PV1 = pes.enter_context(nc.psum_tensor("PV1", [65, 1024], F32))

                @blk.tensor
                def _(t):
                    sems.wait(t, "sVE", cp["negM_ve"])

                    def emit_S(g):
                        u, tau = g // NT, g % NT
                        hh, sg = u // 2, u % 2
                        pp = hh // 2
                        if tau == 0 and sg == 0 and hh % 2 == 0:
                            sems.wait(t, "sDQL" + str(pp % 2), 16 * (pp // 2 + 1))
                        if tau == 0 and sg == 0:
                            sems.wait(t, "sDVL" + str(hh % 2), 16 * (hh // 2 + 1))
                        sps = Sa if g % 2 == 0 else Sb
                        if g >= 2:
                            sems.wait(t, "sAC", exp_cnt(g - 2))
                        qsl = Qwin[:, (pp % 2) * SEQ + sg * 1024:
                                   (pp % 2) * SEQ + (sg + 1) * 1024]
                        for nb in range(2):
                            mm = t.matmul(sps[:, nb * 512:(nb + 1) * 512],
                                          KP[:, hh * SEQ + tau * 128: hh * SEQ + (tau + 1) * 128],
                                          qsl[:, nb * 512:(nb + 1) * 512],
                                          start=True, stop=True)
                        sems.inc(mm, "sTE")

                    def emit_PV(g):
                        u, tau = g // NT, g % NT
                        hh = u // 2
                        pv = PV0 if u % 2 == 0 else PV1
                        if tau == 0 and u >= 2:
                            sems.wait(t, "sVE", drain_cnt(u - 2))
                        sems.wait(t, "sAC", exp_cnt(g))
                        pw = Pt[:, (g % 2) * 1024:(g % 2 + 1) * 1024]
                        vsl = vwin[:, (hh % 2) * NT * 66 + tau * 66:
                                   (hh % 2) * NT * 66 + tau * 66 + 65]
                        for nb in range(2):
                            mm = t.matmul(pv[:, nb * 512:(nb + 1) * 512], vsl,
                                          pw[:, nb * 512:(nb + 1) * 512],
                                          start=(tau == 0), stop=(tau == NT - 1))
                        sems.inc(mm, "sTE")

                    for g in range(NG):
                        emit_S(g)
                        if g >= 1:
                            emit_PV(g - 1)
                    emit_PV(NG - 1)
                    assert sems.c["sTE"] == bTEa + 2 * NG
                    cp["attn_te_done"] = sems.c["sTE"]

                @blk.scalar
                def _(sc):
                    for g in range(NG):
                        sps = Sa if g % 2 == 0 else Sb
                        sems.wait(sc, "sTE", s_cnt(g))
                        if g >= 2:
                            sems.wait(sc, "sTE", pv_cnt(g - 2))
                        inst = sc.activation(Pt[:, (g % 2) * 1024:(g % 2 + 1) * 1024],
                                             sps[:, :], AF.Exp, bias=negM[:, 0:1],
                                             scale=0.125)
                        sems.inc(inst, "sAC")
                    cp["attn_exp_done"] = sems.c["sAC"]

                @blk.vector
                def _(v):
                    v.memset(denomC[:, :], 1.0)
                    v.drain()
                    for u in range(2 * H):
                        pv = PV0 if u % 2 == 0 else PV1
                        sems.wait(v, "sTE", pv_cnt(u * NT + NT - 1))
                        if u >= 2:
                            sems.wait(v, "sDSH" + str(u % 2), 32 * ((u - 2) // 2 + 1))
                        inst = v.tensor_copy(stg[0:65, (u % 2) * 1024:(u % 2) * 1024 + 1024],
                                             pv[:, :])
                        sems.inc(inst, "sVE")
                    cp["attn_drain_done"] = sems.c["sVE"]

                @blk.gpsimd
                def _(g):
                    sems.wait(g, "sDQS0", 16 * 4)
                    sems.wait(g, "sDQS1", 16 * 4)
                    sems.wait(g, "sDVS0", 16 * 8)
                    sems.wait(g, "sDVS1", 16 * 8)
                    for pp in range(2):
                        sems.inc(g.dma_start(Qwin[:, (pp % 2) * SEQ:(pp % 2 + 1) * SEQ],
                                             qspill[pp * 128:(pp + 1) * 128, :]), "sDQL" + str(pp % 2), 16)
                    for hh in range(2):
                        dst = vwin.rearrange("p (w t c) -> p w t c", w=2, t=NT)
                        src = vspill.rearrange("(t p) e -> p t e", p=128)
                        sems.inc(g.dma_start(dst[:, hh % 2, :, 0:64],
                                             src[:, :, hh * 64:(hh + 1) * 64]), "sDVL" + str(hh % 2), 16)
                    for u in range(2 * H):
                        hh, sg = u // 2, u % 2
                        sems.wait(g, "sVE", drain_cnt(u))
                        ssl = stg[:, (u % 2) * 1024:(u % 2) * 1024 + 1024]
                        och, row0 = hh // 2, (hh % 2) * 64
                        sems.inc(g.dma_start(
                            A[row0:row0 + 64, och * SEQ + sg * 1024: och * SEQ + sg * 1024 + 1024],
                            ssl[0:64, :]), "sDSH" + str(u % 2), 16)
                        sems.inc(g.dma_start(
                            denomC[hh:hh + 1, sg * 1024:(sg + 1) * 1024],
                            ssl[64:65, :]), "sDSH" + str(u % 2), 16)
                        if sg == 1:
                            nh = hh + 2
                            if nh < H:
                                dst = vwin.rearrange("p (w t c) -> p w t c", w=2, t=NT)
                                src = vspill.rearrange("(t p) e -> p t e", p=128)
                                sems.inc(g.dma_start(dst[:, nh % 2, :, 0:64],
                                                     src[:, :, nh * 64:(nh + 1) * 64]),
                                         "sDVL" + str(nh % 2), 16)
                            if hh % 2 == 1:
                                np_ = hh // 2 + 2
                                if np_ < NCH:
                                    sems.inc(g.dma_start(
                                        Qwin[:, (np_ % 2) * SEQ:(np_ % 2 + 1) * SEQ],
                                        qspill[np_ * 128:(np_ + 1) * 128, :]), "sDQL" + str(np_ % 2), 16)

        if phase_on('n'):
            bTEn = sems.c["sTE"]
            bVEn = sems.c["sVE"]
            bACn = sems.c["sAC"]
            with ExitStack() as pes, nc.Block() as blk:
                psR = pes.enter_context(nc.psum_tensor("psR", [128, 1024], F32))
                psR2 = pes.enter_context(nc.psum_tensor("psR2", [128, 1024], F32))
                psBV = pes.enter_context(nc.psum_tensor("psBV", [1, D], F32))
                psBC = pes.enter_context(nc.psum_tensor("psBC", [128, D], F32))

                @blk.tensor
                def _(t):
                    sems.wait(t, "sVE", bVEn + 1)
                    for j in range(2 * NCH):
                        gch, sg = j // 2, j % 2
                        ps = psR if j % 2 == 0 else psR2
                        if j >= 2:
                            sems.wait(t, "sVE", bVEn + j)
                        for nb in range(2):
                            mm = t.matmul(ps[:, nb * 512:(nb + 1) * 512],
                                          iselb[:, gch * 128:(gch + 1) * 128],
                                          denomC[:, sg * 1024 + nb * 512: sg * 1024 + (nb + 1) * 512],
                                          start=True, stop=True)
                        sems.inc(mm, "sTE")
                    sems.wait(t, "sDWA", cp["wo_loaded"])
                    sems.wait(t, "sDC", cp["consts_loaded"])
                    for gch in range(NCH):
                        for nb in range(2):
                            mm = t.matmul(psBV[:, nb * 512:(nb + 1) * 512], bvb[:, gch:gch + 1],
                                          WA[:, gch * D + nb * 512: gch * D + (nb + 1) * 512],
                                          start=(gch == 0), stop=(gch == NCH - 1))
                    cp["bv_mm_done"] = sems.inc(mm, "sTE")
                    for k in range(3):
                        sems.wait(t, "sVE", bVEn + 18 + 2 * k)
                        for nb in range(2):
                            mm = t.matmul(psBC[:, nb * 512:(nb + 1) * 512], E0[:, :],
                                          rowbufb[:, nb * 512:(nb + 1) * 512],
                                          start=True, stop=True)
                        sems.inc(mm, "sTE")

                @blk.sync
                def _(sync):
                    sems.wait(sync, "sTE", cp["attn_te_done"])
                    sems.inc(sync.dma_start(rowbuf[0:1, :], ws_h[None, :]), "sDRW", 16)
                    sems.wait(sync, "sVE", bVEn + 20)
                    sems.inc(sync.dma_start(rowbuf[0:1, :], fb_h[None, :]), "sDRW", 16)
                    sems.wait(sync, "sVE", bVEn + 22)
                    sems.inc(sync.dma_start(rowbuf[0:1, :], bo_h[None, :]), "sDRW", 16)

                @blk.vector
                def _(vraw):
                    sems.wait(vraw, "sDSH0", 32 * 16)
                    sems.wait(vraw, "sDSH1", 32 * 16)
                    with nc.allow_low_precision(reason="bf16 softmax denom recip"):
                        inst = vraw.reciprocal(denomC[0:16, :], denomC[0:16, :])
                    sems.inc(inst, "sVE")
                    vraw.drain()
                    for j in range(2 * NCH):
                        gch, sg = j // 2, j % 2
                        ps = psR if j % 2 == 0 else psR2
                        sems.wait(vraw, "sTE", bTEn + j + 1)
                        osl = A[:, gch * SEQ + sg * 1024: gch * SEQ + sg * 1024 + 1024]
                        sems.inc(vraw.tensor_tensor(osl, osl, ps[:, :], AL.mult), "sVE")
                    cp["onorm_done"] = sems.c["sVE"]
                    v = DrainAll(vraw)
                    sems.wait(vraw, "sDRW", 16)
                    sems.inc(v.tensor_copy(rowbufb[0:1, :], rowbuf[0:1, :]), "sVE")
                    sems.wait(vraw, "sTE", bTEn + 18)
                    sems.inc(v.tensor_copy(B_ws[:, :], psBC[:, :]), "sVE")
                    v.tensor_copy(xstg[0:1, 0:D], rowbufb[0:1, :])
                    v.tensor_tensor(rowbuf[0:1, :], rowbuf[0:1, :], xstg[0:1, 0:D],
                                    AL.subtract)
                    sems.inc(v.tensor_copy(rowbufb[0:1, :], rowbuf[0:1, :]), "sVE")
                    sems.wait(vraw, "sTE", bTEn + 19)
                    sems.inc(v.tensor_tensor(B_ws[:, :], B_ws[:, :], psBC[:, :], AL.add), "sVE")
                    sems.wait(vraw, "sDRW", 32)
                    sems.inc(v.tensor_copy(rowbufb[0:1, :], rowbuf[0:1, :]), "sVE")
                    sems.wait(vraw, "sTE", bTEn + 20)
                    sems.inc(v.tensor_copy(B_fb[:, :], psBC[:, :]), "sVE")
                    sems.wait(vraw, "sDRW", 48)
                    sems.wait(vraw, "sTE", cp["bv_mm_done"])
                    v.tensor_tensor(rowbuf[0:1, :], rowbuf[0:1, :], psBV[:, :], AL.add)
                    inst = v.tensor_copy(Brow[0:1, :], rowbuf[0:1, :])
                    cp["brow_done"] = sems.inc(inst, "sVE")

        if phase_on('p3'):
            bVE3 = sems.c["sVE"]
            bAC3 = sems.c["sAC"]
            bTE3 = sems.c["sTE"]
            bX = (sems.c["sDX0"], sems.c["sDX1"])

            def veA(i):
                return bVE3 + i + 1
            VPE3 = bVE3 + NT + 1
            COMB3 = bVE3 + NT + 2
            veE = {}
            _c = COMB3
            for _s in range(NT + 2):
                if _s >= 2:
                    _c += 1; veE[(_s - 2, 3)] = _c
                    _c += 1; veE[(_s - 2, 4)] = _c
                    _c += 1; veE[(_s - 2, 5)] = _c
                if 1 <= _s <= NT:
                    _c += 1; veE[(_s - 1, 6)] = _c
                if _s < NT:
                    _c += 1; veE[(_s, 1)] = _c
                    _c += 1; veE[(_s, 2)] = _c
            cnt_wo = [bTE3 + 1 + i for i in range(NT)]
            cnt_tr, cnt_ffn = {}, {}
            _t = bTE3 + NT
            for _s in range(NT + 1):
                if _s < NT:
                    _t += 1; cnt_tr[_s] = _t
                if _s >= 1:
                    _t += 1; cnt_ffn[_s - 1] = _t

            def acSq(j):
                return bAC3 + 2 + j

            x1r8 = at(KP, 0, "x1r8", [128, 8 * D], F32)
            sqb1 = at(KP, 49152, "sqb1", [128, D], F32)
            iqb1 = at(KP, 53248, "iqb1", [128, D], I32)
            sqbs, iqbs = [sqb, sqb1], [iqb, iqb1]

            with ExitStack() as pes, nc.Block() as blk:
                psW = pes.enter_context(nc.psum_tensor("psW", [128, D], F32))
                psF = pes.enter_context(nc.psum_tensor("psF", [128, D], F32))
                psF2 = pes.enter_context(nc.psum_tensor("psF2", [128, D], F32))
                psT = pes.enter_context(nc.psum_tensor("psT3", [128, D], F32))

                @blk.tensor
                def _(t):
                    sems.wait(t, "sVE", cp["brow_done"])
                    for i in range(NT):
                        if i >= 1:
                            sems.wait(t, "sVE", veA(i - 1))
                        for gch in range(NCH):
                            for nb in range(2):
                                t.matmul(psW[:, nb * 512:(nb + 1) * 512],
                                         A[:, gch * SEQ + i * 128: gch * SEQ + (i + 1) * 128],
                                         WA[:, gch * D + nb * 512: gch * D + (nb + 1) * 512],
                                         start=(gch == 0), stop=False)
                        for nb in range(2):
                            mm = t.matmul(psW[:, nb * 512:(nb + 1) * 512],
                                          E0[:, :], Brow[:, nb * 512:(nb + 1) * 512],
                                          start=False, stop=True)
                        assert sems.inc(mm, "sTE") == cnt_wo[i]
                    sems.wait(t, "sDWB", cp["wf_loaded"])
                    for ss in range(NT + 1):
                        if ss < NT:
                            j = ss
                            sems.wait(t, "sVE", veE[(j, 2)])
                            if j >= 1:
                                sems.wait(t, "sVE", veE[(j - 1, 6)])
                            xq = xqb[:, (j % 2) * D:(j % 2 + 1) * D]
                            for gch in range(NCH):
                                mm = t.matmul(psT[:, gch * 128:(gch + 1) * 128],
                                              xq[:, gch * 128:(gch + 1) * 128],
                                              idbf[:, :], start=True, stop=True)
                            assert sems.inc(mm, "sTE") == cnt_tr[j]
                        if ss >= 1:
                            j = ss - 1
                            psf = psF if j % 2 == 0 else psF2
                            sems.wait(t, "sVE", veE[(j, 6)])
                            if j >= 2:
                                sems.wait(t, "sVE", veE[(j - 2, 3)])
                            for gch in range(NCH):
                                for nb in range(2):
                                    mm = t.matmul(psf[:, nb * 512:(nb + 1) * 512],
                                                  xqT[:, (j % 2) * D + gch * 128:(j % 2) * D + (gch + 1) * 128],
                                                  WB[:, gch * D + nb * 512: gch * D + (nb + 1) * 512],
                                                  start=(gch == 0), stop=(gch == NCH - 1))
                            assert sems.inc(mm, "sTE") == cnt_ffn[j]

                @blk.sync
                def _(sync):
                    sems.wait(sync, "sVE", cp["brow_done"])
                    for i in range(NT):
                        if i >= 2:
                            sems.wait(sync, "sVE", veA(i - 2))
                        sems.inc(sync.dma_start(xstg[:, (i % 2) * D:(i % 2 + 1) * D],
                                                x_hbm[i * 128:(i + 1) * 128, :]),
                                 "sDX" + str(i % 2), 16)
                        sems.wait(sync, "sVE", veA(i))
                        sems.inc(sync.dma_start(out_hbm[i * 128:(i + 1) * 128, :],
                                                x1b[:, (i % 2) * D:(i % 2 + 1) * D]),
                                 "sDOB" + str(i % 2), 16)
                    for j in range(NT):
                        sems.wait(sync, "sVE", veE[(j, 5)])
                        sems.inc(sync.dma_start(out_hbm[j * 128:(j + 1) * 128, :],
                                                x1r8[:, (j % 8) * D:(j % 8 + 1) * D]),
                                 "sDO" + str(j % 2), 16)
                    sems.wait(sync, "sDO0", 16 * 8)
                    sems.wait(sync, "sDO1", 16 * 8)

                @blk.gpsimd
                def _(g):
                    for j in range(2):
                        sems.wait(g, "sDOB" + str(j % 2), 16)
                        sems.inc(g.dma_start(x1r8[:, (j % 8) * D:(j % 8 + 1) * D],
                                             out_hbm[j * 128:(j + 1) * 128, :]),
                                 "sDXL" + str(j % 2), 16)

                @blk.scalar
                def _(sc):
                    sems.wait(sc, "sVE", VPE3)
                    inst = sc.activation(st3[:, 240:256], st3[:, 240:256], AF.Sqrt)
                    assert sems.inc(inst, "sAC") == bAC3 + 1
                    for j in range(NT):
                        sems.wait(sc, "sDXL" + str(j % 2), 16 * (j // 2 + 1))
                        sems.wait(sc, "sVE", COMB3)
                        if j >= 2:
                            sems.wait(sc, "sVE", veE[(j - 2, 1)])
                        inst = sc.activation(sqbs[j % 2][:, :],
                                             x1r8[:, (j % 8) * D:(j % 8 + 1) * D],
                                             AF.Identity, bias=st3[:, 208 + j:209 + j],
                                             scale=st3[:, 192 + j:193 + j])
                        assert sems.inc(inst, "sAC") == acSq(j)
                        if 2 <= j + 1 < NT:
                            jj = j + 1
                            sems.wait(sc, "sDOB" + str(jj % 2), 16 * (jj // 2 + 1))
                            if jj >= 8:
                                sems.wait(sc, "sDO" + str(jj % 2), 16 * ((jj - 8) // 2 + 1))
                            sems.inc(sc.dma_start(x1r8[:, (jj % 8) * D:(jj % 8 + 1) * D],
                                                  out_hbm[jj * 128:(jj + 1) * 128, :]),
                                     "sDXL" + str(jj % 2), 16)

                @blk.vector
                def _(vraw):
                    st = st3
                    for i in range(NT):
                        psw = psW
                        xsl = xstg[:, (i % 2) * D:(i % 2 + 1) * D]
                        x1 = x1b[:, (i % 2) * D:(i % 2 + 1) * D]
                        sems.wait(vraw, "sTE", cnt_wo[i])
                        sems.wait(vraw, "sDX" + str(i % 2), bX[i % 2] + 16 * (i // 2 + 1))
                        if i >= 2:
                            sems.wait(vraw, "sDOB" + str(i % 2), 16 * (i // 2))
                        inst = vraw.tensor_tensor(x1, psw[:, :], xsl, AL.add)
                        assert sems.inc(inst, "sVE") == veA(i)
                        vraw.drain()
                        vraw.bn_stats(st[:, 96:102], x1[:, 0:512])
                        vraw.bn_stats(st[:, 102:108], x1[:, 512:1024])
                        vraw.tensor_reduce(st[:, 64 + i:65 + i], x1, AX.X, AL.max)
                        vraw.tensor_reduce(st[:, 80 + i:81 + i], x1, AX.X, AL.min)
                        vraw.drain()
                        vraw.bn_aggr(st[:, 2 * i:2 * i + 2], st[:, 96:102])
                        vraw.bn_aggr(st[:, 32 + 2 * i:34 + 2 * i], st[:, 102:108])
                    vraw.drain()
                    v = DrainAll(vraw)
                    stv = st3.rearrange("p (a b) -> p a b", b=2)
                    mA, vA = stv[:, 0:16, 0:1], stv[:, 0:16, 1:2]
                    mB, vB = stv[:, 16:32, 0:1], stv[:, 16:32, 1:2]
                    mx, mn = st[:, 64:80], st[:, 80:96]
                    m_, vpe = st[:, 112:128], st[:, 128:144]
                    tA, tB = st[:, 144:160], st[:, 160:176]
                    is_, isr = st[:, 176:192], st[:, 192:208]
                    nmisr, xsC = st[:, 208:224], st[:, 224:240]
                    v.tensor_tensor(m_, mA, mB, AL.add)
                    v.tensor_scalar(m_, m_, 0.5, None, AL.mult)
                    v.tensor_tensor(tA, mA, mB, AL.subtract)
                    v.tensor_tensor(tA, tA, tA, AL.mult)
                    v.tensor_tensor(vpe, vA, vB, AL.add)
                    v.tensor_scalar(vpe, vpe, 0.5, None, AL.mult)
                    v.tensor_scalar(tA, tA, 0.25, EPS_LN, AL.mult, AL.add)
                    v.tensor_tensor(vpe, vpe, tA, AL.add)
                    inst = v.tensor_copy(st[:, 240:256], vpe)
                    assert sems.inc(inst, "sVE") == VPE3
                    sems.wait(vraw, "sAC", bAC3 + 1)
                    v.reciprocal(is_, st[:, 240:256])
                    v.tensor_tensor(tB, is_, is_, AL.mult)
                    v.tensor_tensor(tB, tB, vpe, AL.mult)
                    v.tensor_scalar(tB, tB, 3.0, -0.5, AL.subtract, AL.mult)
                    v.tensor_tensor(is_, is_, tB, AL.mult)
                    v.tensor_tensor(tA, mx, m_, AL.subtract)
                    v.tensor_tensor(tB, m_, mn, AL.subtract)
                    v.tensor_tensor(tA, tA, tB, AL.max)
                    v.tensor_tensor(tA, tA, is_, AL.mult)
                    v.tensor_scalar(tA, tA, EPS_Q, None, AL.max)
                    v.tensor_scalar(xsC, tA, 1.0 / 127.0, None, AL.mult)
                    v.reciprocal(tB, tA)
                    v.tensor_scalar(tB, tB, 127.0, None, AL.mult)
                    v.tensor_tensor(isr, is_, tB, AL.mult)
                    v.tensor_tensor(nmisr, m_, isr, AL.mult)
                    inst = v.tensor_scalar(nmisr, nmisr, -1.0, None, AL.mult)
                    assert sems.inc(inst, "sVE") == COMB3
                    for ss in range(NT + 2):
                        if ss >= 2:
                            j = ss - 2
                            psf = psF if j % 2 == 0 else psF2
                            e1 = h2b[:, (j % 2) * D:(j % 2 + 1) * D]
                            sems.wait(vraw, "sTE", cnt_ffn[j])
                            inst = vraw.tensor_tensor(e1, psf[:, :], B_ws[:, :], AL.mult)
                            assert sems.inc(inst, "sVE") == veE[(j, 3)]
                            vraw.drain()
                            inst = vraw.scalar_tensor_tensor(
                                e1, e1, st[:, 224 + j:225 + j], B_fb[:, :],
                                AL.mult, AL.add)
                            assert sems.inc(inst, "sVE") == veE[(j, 4)]
                            vraw.drain()
                            x1sl = x1r8[:, (j % 8) * D:(j % 8 + 1) * D]
                            inst = vraw.tensor_tensor(x1sl, x1sl, e1, AL.add)
                            assert sems.inc(inst, "sVE") == veE[(j, 5)]
                        if 1 <= ss <= NT:
                            j = ss - 1
                            sems.wait(vraw, "sTE", cnt_tr[j])
                            if j >= 2:
                                sems.wait(vraw, "sTE", cnt_ffn[j - 2])
                            inst = vraw.tensor_copy(
                                xqT[:, (j % 2) * D:(j % 2 + 1) * D], psT[:, :])
                            assert sems.inc(inst, "sVE") == veE[(j, 6)]
                        if ss < NT:
                            j = ss
                            sems.wait(vraw, "sAC", acSq(j))
                            if j >= 2:
                                sems.wait(vraw, "sTE", cnt_tr[j - 2])
                            inst = vraw.tensor_copy(iqbs[j % 2][:, :], sqbs[j % 2][:, :])
                            assert sems.inc(inst, "sVE") == veE[(j, 1)]
                            vraw.drain()
                            inst = vraw.tensor_copy(
                                xqb[:, (j % 2) * D:(j % 2 + 1) * D], iqbs[j % 2][:, :])
                            assert sems.inc(inst, "sVE") == veE[(j, 2)]
                    cp["p3_done"] = sems.c["sVE"]

        if _lim < len(_ORDER) - 1:
            with nc.Block() as blk:
                @blk.sync
                def _(sync):
                    for nm, v in list(sems.c.items()):
                        if v > 0:
                            sems.wait(sync, nm, v)
                    sems.inc(sync.dma_start(out_hbm[0:128, :], xstg[:, 0:D]), "sDO0", 16)
                    sems.wait(sync, "sDO0", sems.c["sDO0"])
        if debug:
            with nc.Block() as blk:
                @blk.vector
                def _(v):
                    for nm, val in list(sems.c.items()):
                        if val > 0 and nm not in ("sGP",):
                            sems.wait(v, nm, val)
                    v.engine_nop().then_inc(sems.h["sGP"], 1)
                    sems.c["sGP"] += 1

                @blk.sync
                def _(sync):
                    sems.wait(sync, "sGP", sems.c["sGP"])
                    ndma = 0
                    def dump(name, src, cast=True):
                        nonlocal ndma
                        pass
                @blk.gpsimd
                def _(g):
                    g.wait_ge(sems.h["sGP"], sems.c["sGP"])
                    nd = 0
                    _all = [("KP", KP[:, :]), ("A", A[:, :]),
                            ("Qwin", Qwin[:, :]), ("den", denomC[:, :]),
                            ("negM", negM[:, :]), ("B_o", B_o[:, :]),
                            ("B_ws", B_ws[:, :]), ("B_fb", B_fb[:, :]),
                            ("nrm", nrm[:, :]), ("perd", None),
                            ("x1b", x1b[:, :]), ("vwin", vwin[:, :]),
                            ("WA", WA[:, :]), ("WB", WB[:, :]),
                            ("Qwork", Qwork[:, :]),
                            ("xstg", xstg[:, :]), ("hstg", hstg[:, :]),
                            ("hstg2", hstg2[:, :]), ("smalls", smalls[:, :])]
                    if debug_names is not None:
                        _all = [p for p in _all if p[0] in debug_names]
                    for nm, src in _all:
                        if nm == "perd":
                            g.dma_start(dbg[nm][:, 0:NCH], perdq[:, :]).then_inc(sems.h["sDG"], 16)
                            g.dma_start(dbg[nm][:, 8:8 + H], perdk[:, :]).then_inc(sems.h["sDG"], 16)
                            sems.c["sDG"] += 32
                            nd += 2
                            continue
                        g.dma_start(dbg[nm][:, :], src).then_inc(sems.h["sDG"], 16)
                        sems.c["sDG"] += 16
                        nd += 1
                    g.wait_ge(sems.h["sDG"], sems.c["sDG"])
        sems.check()
    return nc


def _host_tables():
    idx = np.arange(32, dtype=np.float64)
    inv_freq = 10000.0 ** (-2.0 * idx / 64.0)
    t = np.arange(SEQ, dtype=np.float64)
    fr = np.outer(inv_freq, t)
    c32, s32 = np.cos(fr), np.sin(fr)
    cc = np.zeros((128, SEQ), np.float64)
    ss = np.zeros((128, SEQ), np.float64)
    for p in range(128):
        hd = p % 64
        cc[p] = c32[hd // 2]
        ss[p] = (-s32[hd // 2]) if hd % 2 == 0 else s32[hd // 2]
    swapm = np.zeros((128, 128), np.float32)
    for p in range(0, 128, 2):
        swapm[p + 1, p] = 1.0
        swapm[p, p + 1] = 1.0
    esel = np.zeros((128, H), np.float32)
    for hh in range(H):
        r0 = (hh % 2) * 64
        esel[r0:r0 + 64, hh] = 1.0
    isel = np.zeros((128, NCH, 128), np.float32)
    for gch in range(NCH):
        for p in range(128):
            isel[2 * gch + p // 64, gch, p] = 1.0
    maskq = np.zeros((16, NCH), np.float32)
    maskk = np.zeros((16, H), np.float32)
    for hh in range(H):
        maskq[hh, hh // 2] = 1.0
        maskk[hh, hh] = 1.0
    return (cc.astype(np.float32), ss.astype(np.float32), swapm, esel,
            isel.reshape(128, NCH * 128), maskq, maskk,
            np.eye(128, dtype=np.float32))


def kernel(L0_attn_x, L0_attn_Wq, L0_attn_bq, L0_attn_Wk, L0_attn_bk,
           L0_attn_Wv, L0_attn_bv, L0_attn_Wo, L0_attn_bo,
           L0_ffn_weight_q, L0_ffn_weight_scale, L0_ffn_bias):
    x = np.ascontiguousarray(np.asarray(L0_attn_x, dtype=np.float32))
    cc, ss, swapm, esel, isel, maskq, maskk, idm = _host_tables()
    shared = {
        "wqt": np.ascontiguousarray(np.asarray(L0_attn_Wq, np.float32).T),
        "wkt": np.ascontiguousarray(np.asarray(L0_attn_Wk, np.float32).T),
        "wvt": np.ascontiguousarray(np.asarray(L0_attn_Wv, np.float32).T),
        "wot": np.ascontiguousarray(np.asarray(L0_attn_Wo, np.float32).T),
        "wft": np.ascontiguousarray(np.asarray(L0_ffn_weight_q, np.int8).T),
        "bq": np.asarray(L0_attn_bq, np.float32),
        "bk": np.asarray(L0_attn_bk, np.float32),
        "bv": np.asarray(L0_attn_bv, np.float32),
        "bo": np.asarray(L0_attn_bo, np.float32),
        "ws": np.asarray(L0_ffn_weight_scale, np.float32),
        "fb": np.asarray(L0_ffn_bias, np.float32),
        "cc": cc, "ss": ss, "swapm": swapm, "esel": esel, "isel": isel,
        "maskq": maskq, "maskk": maskk, "idm": idm,
    }
    in_maps = [dict(shared, x=np.ascontiguousarray(x[c])) for c in range(8)]
    nc = build_program()
    res = run_bass_kernel_spmd(nc, in_maps, core_ids=list(range(8)))
    return np.stack([r["out"] for r in res.results], axis=0)


# version 14 (final)
# speedup vs baseline: 1.2973x; 1.0057x over previous
import numpy as np
import concourse.bass as bass
import concourse.mybir as mybir
import concourse.bass_isa as bass_isa
from concourse.bass_utils import run_bass_kernel_spmd
from contextlib import ExitStack

F32 = mybir.dt.float32
BF16 = mybir.dt.bfloat16
I8 = mybir.dt.int8
I32 = mybir.dt.int32
U8 = mybir.dt.uint8
AL = mybir.AluOpType
AF = mybir.ActivationFunctionType
AX = mybir.AxisListType

SEQ = 2048
D = 1024
H = 16
NT = 16
NCH = 8
EPS_LN = 1e-5
EPS_Q = 1e-6


class SemBox:
    def __init__(self, nc, es, names):
        self.h = {n: es.enter_context(nc.semaphore(n)) for n in names}
        self.c = {n: 0 for n in names}
        self.waits = []

    def inc(self, inst, name, amt=1):
        inst.then_inc(self.h[name], amt)
        self.c[name] += amt
        return self.c[name]

    def wait(self, eng, name, val):
        if val <= 0:
            return
        self.waits.append((name, val))
        eng.wait_ge(self.h[name], val)

    def check(self):
        for name, val in self.waits:
            assert val <= self.c[name], f"wait {name}>={val} but max {self.c[name]}"


class DrainAll:

    def __init__(self, eng):
        self._eng = eng

    def __getattr__(self, n):
        f = getattr(self._eng, n)
        if not callable(f):
            return f

        def g(*a, **k):
            r = f(*a, **k)
            self._eng.drain()
            return r
        return g


def build_program(upto="p3", debug=False, debug_names=None):
    _ORDER = ["init", "p0", "p1q", "p1k", "p1v", "attn", "n", "p3"]
    _lim = _ORDER.index(upto)
    def phase_on(name):
        return _ORDER.index(name) <= _lim

    nc = bass.Bass("TRN2", target_bir_lowering=False)

    x_hbm = nc.dram_tensor("x", [SEQ, D], F32, kind="ExternalInput")
    wqt = nc.dram_tensor("wqt", [D, D], F32, kind="ExternalInput")
    wkt = nc.dram_tensor("wkt", [D, D], F32, kind="ExternalInput")
    wvt = nc.dram_tensor("wvt", [D, D], F32, kind="ExternalInput")
    wot = nc.dram_tensor("wot", [D, D], F32, kind="ExternalInput")
    wft = nc.dram_tensor("wft", [D, D], I8, kind="ExternalInput")
    bq_h = nc.dram_tensor("bq", [D], F32, kind="ExternalInput")
    bk_h = nc.dram_tensor("bk", [D], F32, kind="ExternalInput")
    bv_h = nc.dram_tensor("bv", [D], F32, kind="ExternalInput")
    bo_h = nc.dram_tensor("bo", [D], F32, kind="ExternalInput")
    ws_h = nc.dram_tensor("ws", [D], F32, kind="ExternalInput")
    fb_h = nc.dram_tensor("fb", [D], F32, kind="ExternalInput")
    cc_h = nc.dram_tensor("cc", [128, SEQ], F32, kind="ExternalInput")
    ss_h = nc.dram_tensor("ss", [128, SEQ], F32, kind="ExternalInput")
    swap_h = nc.dram_tensor("swapm", [128, 128], F32, kind="ExternalInput")
    esel_h = nc.dram_tensor("esel", [128, H], F32, kind="ExternalInput")
    isel_h = nc.dram_tensor("isel", [128, NCH * 128], F32, kind="ExternalInput")
    mq_h = nc.dram_tensor("maskq", [16, NCH], F32, kind="ExternalInput")
    mk_h = nc.dram_tensor("maskk", [16, H], F32, kind="ExternalInput")
    idm_h = nc.dram_tensor("idm", [128, 128], F32, kind="ExternalInput")
    out_hbm = nc.dram_tensor("out", [SEQ, D], F32, kind="ExternalOutput")
    vspill = nc.dram_tensor("vspill", [SEQ, D], BF16, kind="Internal")
    qspill = nc.dram_tensor("qspill", [NCH * 128, SEQ], BF16, kind="Internal")
    if debug:
        dbg = {nm: nc.dram_tensor("dbg_" + nm, shp, dt, kind="ExternalOutput")
               for nm, shp, dt in [
                   ("KP", [128, H * SEQ], F32), ("A", [128, NCH * SEQ], F32),
                   ("Qwin", [128, 2 * SEQ], F32), ("den", [128, SEQ], F32),
                   ("negM", [128, 1], F32), ("B_o", [128, D], F32),
                   ("B_ws", [128, D], F32), ("B_fb", [128, D], F32),
                   ("nrm", [16, 64], F32), ("perd", [128, 24], F32),
                   ("x1b", [128, 2 * D], F32), ("vwin", [128, 2 * NT * 66], F32),
                   ("WA", [128, NCH * D], F32), ("WB", [128, NCH * D], F32),
                   ("Qwork", [128, 2 * SEQ], F32),
                   ("xstg", [128, 2 * D], F32), ("hstg", [128, D], F32),
                   ("hstg2", [128, D], F32), ("smalls", [128, 64], F32),
               ]}

    es = ExitStack()
    with es:
        sbuf = lambda *a: es.enter_context(nc.sbuf_tensor(*a))

        A = sbuf("A", [128, NCH * SEQ], BF16)
        KP = sbuf("KP", [128, H * SEQ], BF16)
        WA = sbuf("WA", [128, NCH * D], BF16)
        WB = sbuf("WB", [128, NCH * D], BF16)
        Qwork = sbuf("Qwork", [128, 2 * SEQ], BF16)
        Qwin = sbuf("Qwin", [128, 2 * SEQ], BF16)
        xstg = sbuf("xstg", [128, 2 * D], F32)
        R1 = sbuf("R1", [128, 8192], U8)
        R2 = sbuf("R2", [128, 8192], U8)
        R3 = sbuf("R3", [128, 8192], U8)
        R4 = sbuf("R4", [128, 8192], U8)
        vwin = sbuf("vwin", [128, 2 * NT * 66], BF16)
        iselb = sbuf("iselb", [128, NCH * 128], BF16)
        swapb = sbuf("swapb", [128, 128], BF16)
        eselb = sbuf("eselb", [128, H], BF16)
        maskq = sbuf("maskq_s", [16, NCH], F32)
        maskk = sbuf("maskk_s", [16, H], F32)
        smalls = sbuf("smalls", [128, 64], F32)
        perdq = sbuf("perdq", [128, NCH], F32)
        perdk = sbuf("perdk", [128, H], F32)
        perdqb = sbuf("perdqb", [128, NCH], BF16)
        perdkb = sbuf("perdkb", [128, H], BF16)
        nrm = sbuf("nrm", [16, 64], F32)
        negM = sbuf("negM", [128, 1], F32)
        E0 = sbuf("E0", [128, 128], BF16)
        negMb = sbuf("negMb", [128, 1], BF16)
        bqt = sbuf("bqt", [128, NCH], F32)
        bkt = sbuf("bkt", [128, NCH], F32)
        bvb = sbuf("bvb", [128, NCH], BF16)
        st3 = sbuf("st3", [128, 256], F32)
        idbf = sbuf("idbf", [128, 128], BF16)
        Brow = sbuf("Brow", [128, D], BF16)

        def at(base, off, name, shape, dtype):
            addr = nc.lookup_mloc(base).addr
            return nc.alloc_sbuf_tensor_at(name, shape, dtype, offset=addr + off)

        hstg = at(R1, 0, "hstg", [128, D], BF16)
        hstg2 = at(R1, 2048, "hstg2", [128, D], BF16)
        vstg = at(R1, 4096, "vstg", [128, 2 * D], BF16)
        h2b = at(R1, 0, "h2b", [128, 2 * D], F32)
        tmp1 = at(R2, 0, "tmp1", [128, SEQ], BF16)
        tmp2 = at(R2, 4096, "tmp2", [128, SEQ], BF16)
        denomC = at(R2, 0, "denomC", [128, SEQ], BF16)
        x1b = at(R2, 0, "x1b", [128, 2 * D], F32)
        ccb = at(R3, 0, "ccb", [128, SEQ], BF16)
        ssb = at(R3, 4096, "ssb", [128, SEQ], BF16)
        sqb = at(R3, 0, "sqb", [128, D], F32)
        iqb = at(R3, 4096, "iqb", [128, D], I32)
        Pt = at(R4, 0, "Pt", [128, 2 * 1024], BF16)
        stg = at(R4, 4096, "stg", [128, 2 * 1024], BF16)
        xqb = at(R4, 0, "xqb", [128, 2 * D], BF16)
        xqT = at(R4, 4096, "xqT", [128, 2 * D], BF16)
        B_o = at(Qwin, 0, "B_o", [128, D], F32)
        B_ws = at(Qwin, 4096, "B_ws", [128, D], F32)
        B_fb = at(Qwork, 0, "B_fb", [128, D], F32)
        rowbuf = at(Qwork, 4096, "rowbuf", [128, D], F32)
        rowbufb = at(R4, 4096, "rowbufb", [128, D], BF16)

        sems = SemBox(nc, es, [
            "sDX0", "sDX1", "sDI", "sDC", "sDWA", "sDWB",
            "sDT0", "sDT1", "sDVS0", "sDVS1", "sDVL0", "sDVL1",
            "sDQS0", "sDQS1", "sDQL0", "sDQL1", "sDSH0", "sDSH1",
            "sDRW", "sDO0", "sDO1", "sDOB0", "sDOB1", "sDXL0", "sDXL1",
            "sDG", "sTE", "sVE", "sAC", "sGP", "sPL"])
        cp = {}

        if phase_on('init'):
            with nc.Block() as blk:
                @blk.sync
                def _(sync):
                    sems.inc(sync.dma_start(bqt[:, :], bq_h.rearrange("(g p) -> p g", p=128), allow_slow_non_contiguous=True), "sDI", 16)
                    sems.inc(sync.dma_start(bkt[:, :], bk_h.rearrange("(g p) -> p g", p=128), allow_slow_non_contiguous=True), "sDI", 16)
                    sems.inc(sync.dma_start(maskq[:, :], mq_h[:, :]), "sDI", 16)
                    sems.inc(sync.dma_start(maskk[:, :], mk_h[:, :]), "sDI", 16)
                    cp["init_sync"] = sems.c["sDI"]

                @blk.gpsimd
                def _(g):
                    for dst, src in ((ccb, cc_h), (ssb, ss_h), (swapb, swap_h),
                                     (eselb, esel_h), (iselb, isel_h), (idbf, idm_h)):
                        sems.inc(g.dma_start(dst[:, :], src[:, :]), "sDC", 16)
                    sems.inc(g.dma_start(bvb[:, :], bv_h.rearrange("(g p) -> p g", p=128), allow_slow_non_contiguous=True), "sDC", 16)
                    cp["consts_loaded"] = sems.c["sDC"]
                    sems.inc(g.dma_start(WA[:, :], wqt.rearrange("(g p) e -> p g e", p=128)), "sDWA", 16)
                    cp["wq_loaded"] = sems.c["sDWA"]
                    sems.inc(g.dma_start(WB[:, :], wkt.rearrange("(g p) e -> p g e", p=128)), "sDWB", 16)
                    cp["wk_loaded"] = sems.c["sDWB"]

                @blk.vector
                def _(vraw):
                    v = vraw
                    v.memset(E0[:, :], 0)
                    v.drain()
                    v.memset(E0[0:1, :], 1.0)
                    v.memset(smalls[:, :], 0)
                    v.memset(nrm[:, :], 0)
                    v.memset(negM[:, :], 0)
                    v.memset(perdq[:, :], 0)
                    v.memset(perdk[:, :], 0)
                    v.memset(rowbufb[:, :], 0)
                    v.memset(Brow[:, :], 0)
                    v.memset(negMb[:, :], 0)
                    for hh in range(H):
                        v.memset(KP[:, hh * SEQ:(hh + 1) * SEQ], 0)
                    v.memset(vwin[:, :], 0)
                    v.drain()
                    inst = None
                    for j in range(2 * NT):
                        inst = v.memset(vwin[:, j * 66 + 64: j * 66 + 65], 1.0)
                    sems.inc(inst, "sVE")
                    cp["zeros_done"] = sems.c["sVE"]

        if phase_on('p0'):
            bVE0 = sems.c["sVE"]
            bAC0 = sems.c["sAC"]
            bTE0 = sems.c["sTE"]

            def veS(i):
                return bVE0 + 1 + i
            VPE0 = bVE0 + NT + 1
            COMB0 = bVE0 + NT + 2

            def veEv(i):
                return COMB0 + 1 + i

            def acH(i):
                return bAC0 + 2 + i

            def tr0(i):
                return bTE0 + 1 + i
            cp["p0_done"] = COMB0 + NT
            with ExitStack() as pes, nc.Block() as blk:
                psT0 = pes.enter_context(nc.psum_tensor("psT0", [128, D], BF16))
                psT1 = pes.enter_context(nc.psum_tensor("psT1", [128, D], BF16))

                @blk.sync
                def _(sync):
                    for i in range(NT):
                        if i >= 2:
                            sems.wait(sync, "sVE", veS(i - 2))
                        sems.inc(sync.dma_start(xstg[:, (i % 2) * D:(i % 2 + 1) * D],
                                                x_hbm[i * 128:(i + 1) * 128, :]),
                                 "sDX" + str(i % 2), 16)
                    for i in range(NT):
                        if i < 2:
                            sems.wait(sync, "sVE", veS(14 + i))
                        else:
                            sems.wait(sync, "sAC", acH(i - 2))
                        sems.inc(sync.dma_start(xstg[:, (i % 2) * D:(i % 2 + 1) * D],
                                                x_hbm[i * 128:(i + 1) * 128, :]),
                                 "sDX" + str(i % 2), 16)

                @blk.scalar
                def _(sc):
                    sems.wait(sc, "sVE", VPE0)
                    inst = sc.activation(st3[:, 208:224], st3[:, 208:224], AF.Sqrt)
                    assert sems.inc(inst, "sAC") == bAC0 + 1
                    for i in range(NT):
                        sems.wait(sc, "sVE", COMB0)
                        sems.wait(sc, "sDX" + str(i % 2), 16 * (8 + i // 2 + 1))
                        if i >= 2:
                            sems.wait(sc, "sTE", tr0(i - 2))
                        hh = hstg if i % 2 == 0 else hstg2
                        inst = sc.activation(hh[:, :], xstg[:, (i % 2) * D:(i % 2 + 1) * D],
                                             AF.Identity, bias=st3[:, 176 + i:177 + i],
                                             scale=st3[:, 160 + i:161 + i])
                        assert sems.inc(inst, "sAC") == acH(i)

                @blk.tensor
                def _(t):
                    sems.wait(t, "sDC", cp["consts_loaded"])
                    for i in range(NT):
                        sems.wait(t, "sAC", acH(i))
                        if i >= 2:
                            sems.wait(t, "sVE", veEv(i - 2))
                        hh = hstg if i % 2 == 0 else hstg2
                        pst = psT0 if i % 2 == 0 else psT1
                        for gch in range(NCH):
                            mm = t.transpose(pst[:, gch * 128:(gch + 1) * 128],
                                             hh[:, gch * 128:(gch + 1) * 128], idbf[:, :])
                        assert sems.inc(mm, "sTE") == tr0(i)

                @blk.vector
                def _(vraw):
                    st = st3
                    sA = smalls.rearrange("p (a b) -> p a b", b=2)
                    for i in range(NT):
                        sems.wait(vraw, "sDX" + str(i % 2), 16 * (i // 2 + 1))
                        xt = xstg[:, (i % 2) * D:(i % 2 + 1) * D]
                        vraw.bn_stats(st[:, 96:102], xt[:, 0:512])
                        vraw.bn_stats(st[:, 102:108], xt[:, 512:1024])
                        vraw.drain()
                        vraw.bn_aggr(smalls[:, 2 * i:2 * i + 2], st[:, 96:102])
                        inst = vraw.bn_aggr(smalls[:, 32 + 2 * i:34 + 2 * i], st[:, 102:108])
                        assert sems.inc(inst, "sVE") == veS(i)
                    vraw.drain()
                    v = DrainAll(vraw)
                    mA, vA = sA[:, 0:16, 0:1], sA[:, 0:16, 1:2]
                    mB, vB = sA[:, 16:32, 0:1], sA[:, 16:32, 1:2]
                    m_ = st[:, 112:128]
                    tA, tB = st[:, 128:144], st[:, 144:160]
                    is_, negmis = st[:, 160:176], st[:, 176:192]
                    vpe = st[:, 192:208]
                    v.tensor_tensor(m_, mA, mB, AL.add)
                    v.tensor_scalar(m_, m_, 0.5, None, AL.mult)
                    v.tensor_tensor(tA, mA, mB, AL.subtract)
                    v.tensor_tensor(tA, tA, tA, AL.mult)
                    v.tensor_tensor(vpe, vA, vB, AL.add)
                    v.tensor_scalar(vpe, vpe, 0.5, None, AL.mult)
                    v.tensor_scalar(tA, tA, 0.25, EPS_LN, AL.mult, AL.add)
                    v.tensor_tensor(vpe, vpe, tA, AL.add)
                    inst = v.tensor_copy(st[:, 208:224], vpe)
                    assert sems.inc(inst, "sVE") == VPE0
                    sems.wait(vraw, "sAC", bAC0 + 1)
                    v.reciprocal(is_, st[:, 208:224])
                    v.tensor_tensor(tB, is_, is_, AL.mult)
                    v.tensor_tensor(tB, tB, vpe, AL.mult)
                    v.tensor_scalar(tB, tB, 3.0, -0.5, AL.subtract, AL.mult)
                    v.tensor_tensor(is_, is_, tB, AL.mult)
                    v.tensor_tensor(negmis, m_, is_, AL.mult)
                    inst = v.tensor_scalar(negmis, negmis, -1.0, None, AL.mult)
                    assert sems.inc(inst, "sVE") == COMB0
                    for i in range(NT):
                        sems.wait(vraw, "sTE", tr0(i))
                        pst = psT0 if i % 2 == 0 else psT1
                        av = A.rearrange("p (g s) -> p g s", g=NCH)
                        inst = vraw.tensor_copy(av[:, :, i * 128:(i + 1) * 128],
                                                pst.rearrange("p (g c) -> p g c", g=NCH))
                        assert sems.inc(inst, "sVE") == veEv(i)

        if phase_on('p1q'):
            cnt_projq, cnt_swq = [], []
            bAC1 = sems.c["sAC"]
            bVE1 = sems.c["sVE"]
            with ExitStack() as pes, nc.Block() as blk:
                psP = pes.enter_context(nc.psum_tensor("psP", [128, SEQ], F32))
                psSW = pes.enter_context(nc.psum_tensor("psSW", [128, SEQ], F32))

                @blk.tensor
                def _(t):
                    sems.wait(t, "sVE", cp["p0_done"])
                    sems.wait(t, "sDWA", cp["wq_loaded"])
                    sems.wait(t, "sDC", cp["consts_loaded"])
                    for m in range(NCH + 1):
                        if m < NCH:
                            sems.wait(t, "sAC", bAC1 + m)
                            for nb in range(4):
                                for dd in range(NCH):
                                    mm = t.matmul(psP[:, nb * 512:(nb + 1) * 512],
                                                  WA[:, dd * D + m * 128: dd * D + (m + 1) * 128],
                                                  A[:, dd * SEQ + nb * 512: dd * SEQ + (nb + 1) * 512],
                                                  start=(dd == 0), stop=(dd == NCH - 1))
                            cnt_projq.append(sems.inc(mm, "sTE"))
                        if m >= 1:
                            j = m - 1
                            sems.wait(t, "sAC", bAC1 + j + 1)
                            if j >= 1:
                                sems.wait(t, "sVE", bVE1 + 4 * (j - 1) + 2)
                            src = Qwork[:, (j % 2) * SEQ:(j % 2 + 1) * SEQ]
                            for nb in range(4):
                                mm = t.matmul(psSW[:, nb * 512:(nb + 1) * 512], swapb[:, :],
                                              src[:, nb * 512:(nb + 1) * 512], start=True, stop=True)
                            cnt_swq.append(sems.inc(mm, "sTE"))

                @blk.scalar
                def _(sc):
                    sems.wait(sc, "sDI", 64)
                    for m in range(NCH):
                        sems.wait(sc, "sTE", cnt_projq[m])
                        if m >= 2:
                            sems.wait(sc, "sDQS" + str(m % 2), 16 * ((m - 2) // 2 + 1))
                            sems.wait(sc, "sVE", bVE1 + 4 * (m - 2) + 4)
                        inst = sc.activation(Qwork[:, (m % 2) * SEQ:(m % 2 + 1) * SEQ],
                                             psP[:, :], AF.Identity, bias=bqt[:, m:m + 1])
                        sems.inc(inst, "sAC")
                    cp["q_evac_done"] = sems.c["sAC"]

                @blk.vector
                def _(v):
                    for m in range(NCH):
                        src = Qwork[:, (m % 2) * SEQ:(m % 2 + 1) * SEQ]
                        sems.wait(v, "sTE", cnt_swq[m])
                        sems.inc(v.tensor_tensor(tmp1[:, :], src, ccb[:, :], AL.mult), "sVE")
                        sems.inc(v.tensor_tensor(src, psSW[:, :], ssb[:, :], AL.mult), "sVE")
                        v.drain()
                        sems.inc(v.tensor_tensor(src, src, tmp1[:, :], AL.add), "sVE")
                        v.drain()
                        inst = v.tensor_reduce(perdq[:, m:m + 1], src, AX.X, AL.max,
                                               apply_absolute_value=True)
                        sems.inc(inst, "sVE")
                        if m == NCH - 1:
                            v.drain()
                    cp["q_rope_done"] = sems.c["sVE"]

                @blk.gpsimd
                def _(g):
                    for m in range(NCH):
                        sems.wait(g, "sVE", bVE1 + 4 * m + 3)
                        inst = g.dma_start(qspill[m * 128:(m + 1) * 128, :],
                                           Qwork[:, (m % 2) * SEQ:(m % 2 + 1) * SEQ])
                        sems.inc(inst, "sDQS" + str(m % 2), 16)

        if phase_on('p1k'):
            cnt_projk, cnt_swk = [], []
            bACk = sems.c["sAC"]
            bVEk = sems.c["sVE"]
            with ExitStack() as pes, nc.Block() as blk:
                psP = pes.enter_context(nc.psum_tensor("psPk", [128, SEQ], F32))
                psSW = pes.enter_context(nc.psum_tensor("psSWk", [128, SEQ], F32))

                @blk.tensor
                def _(t):
                    sems.wait(t, "sDWB", cp["wk_loaded"])
                    sems.wait(t, "sVE", cp["zeros_done"])
                    for m in range(NCH + 1):
                        if m < NCH:
                            sems.wait(t, "sAC", bACk + m)
                            for nb in range(4):
                                for dd in range(NCH):
                                    mm = t.matmul(psP[:, nb * 512:(nb + 1) * 512],
                                                  WB[:, dd * D + m * 128: dd * D + (m + 1) * 128],
                                                  A[:, dd * SEQ + nb * 512: dd * SEQ + (nb + 1) * 512],
                                                  start=(dd == 0), stop=(dd == NCH - 1))
                            cnt_projk.append(sems.inc(mm, "sTE"))
                        if m >= 1:
                            for j in (2 * (m - 1), 2 * (m - 1) + 1):
                                sems.wait(t, "sAC", bACk + (m - 1) + 1)
                                if j >= 1:
                                    sems.wait(t, "sVE", bVEk + 4 * (j - 1) + 2)
                                src = KP[:, j * SEQ:(j + 1) * SEQ]
                                for nb in range(4):
                                    mm = t.matmul(psSW[:, nb * 512:(nb + 1) * 512], swapb[:, :],
                                                  src[:, nb * 512:(nb + 1) * 512],
                                                  start=True, stop=True)
                                cnt_swk.append(sems.inc(mm, "sTE"))

                @blk.scalar
                def _(sc):
                    sems.wait(sc, "sVE", cp["zeros_done"])
                    sems.wait(sc, "sDI", 64)
                    for m in range(NCH):
                        sems.wait(sc, "sTE", cnt_projk[m])
                        h0, h1 = 2 * m, 2 * m + 1
                        sc.activation(KP[0:64, h0 * SEQ:(h0 + 1) * SEQ], psP[0:64, :],
                                      AF.Identity, bias=bkt[0:64, m:m + 1])
                        inst = sc.activation(KP[64:128, h1 * SEQ:(h1 + 1) * SEQ],
                                             psP[64:128, :], AF.Identity,
                                             bias=bkt[64:128, m:m + 1])
                        sems.inc(inst, "sAC")
                    cp["k_evac_done"] = sems.c["sAC"]

                @blk.vector
                def _(v):
                    for j in range(H):
                        src = KP[:, j * SEQ:(j + 1) * SEQ]
                        sems.wait(v, "sTE", cnt_swk[j])
                        sems.inc(v.tensor_tensor(tmp1[:, :], src, ccb[:, :], AL.mult), "sVE")
                        sems.inc(v.tensor_tensor(src, psSW[:, :], ssb[:, :], AL.mult), "sVE")
                        v.drain()
                        sems.inc(v.tensor_tensor(src, src, tmp1[:, :], AL.add), "sVE")
                        v.drain()
                        inst = v.tensor_reduce(perdk[:, j:j + 1], src, AX.X, AL.max,
                                               apply_absolute_value=True)
                        sems.inc(inst, "sVE")
                        if j == H - 1:
                            v.drain()
                    cp["k_rope_done"] = sems.c["sVE"]

        if phase_on('p1v'):
            cnt_projv = []
            bACv = sems.c["sAC"]
            bVEv = sems.c["sVE"]
            cp["nb_sqrt_done"] = sems.c["sAC"] + NT + 1
            with ExitStack() as pes, nc.Block() as blk:
                psP = pes.enter_context(nc.psum_tensor("psPv", [128, SEQ], F32))
                psN = pes.enter_context(nc.psum_tensor("psN", [16, 64], F32))
                psNeg = pes.enter_context(nc.psum_tensor("psNeg", [128, 1], F32))

                @blk.gpsimd
                def _(g):
                    sems.wait(g, "sTE", cnt_projq[-1] if cnt_projq else 0)
                    sems.inc(g.dma_start(WA[:, :], wvt.rearrange("(g p) e -> p g e", p=128)), "sDWA", 16)
                    cp["wv_loaded"] = sems.c["sDWA"]

                @blk.tensor
                def _(t):
                    sems.wait(t, "sDWA", cp["wq_loaded"] + 16)
                    for i in range(NT):
                        half = (i % 2) * 1024
                        if i >= 2:
                            sems.wait(t, "sAC", bACv + (i - 2) + 1)
                        for dd in range(NCH):
                            for nb in range(2):
                                mm = t.matmul(psP[:, half + nb * 512: half + (nb + 1) * 512],
                                              A[:, dd * SEQ + i * 128: dd * SEQ + (i + 1) * 128],
                                              WA[:, dd * D + nb * 512: dd * D + (nb + 1) * 512],
                                              start=(dd == 0), stop=(dd == NCH - 1))
                        cnt_projv.append(sems.inc(mm, "sTE"))
                    sems.wait(t, "sVE", bVEv + 1)
                    t.matmul(psN[:, 0:NCH], eselb[:, :], perdqb[:, :], start=True, stop=True)
                    mm = t.matmul(psN[:, 32:32 + H], eselb[:, :], perdkb[:, :], start=True, stop=True)
                    cp["nb_mm_done"] = sems.inc(mm, "sTE")
                    sems.wait(t, "sAC", cp["nb_sqrt_done"])
                    mm = t.matmul(psNeg[:, :], E0[:, :], negMb[:, 0:1], start=True, stop=True)
                    cp["negmm_done"] = sems.inc(mm, "sTE")

                @blk.scalar
                def _(sc):
                    for i in range(NT):
                        half = (i % 2) * 1024
                        sems.wait(sc, "sTE", cnt_projv[i])
                        if i >= 2:
                            sems.wait(sc, "sDVS" + str(i % 2), 16 * ((i - 2) // 2 + 1))
                        inst = sc.activation(vstg[:, (i % 2) * D:(i % 2 + 1) * D],
                                             psP[:, half:half + D], AF.Copy)
                        sems.inc(inst, "sAC")
                    cp["v_evac_done"] = sems.c["sAC"]
                    sems.wait(sc, "sVE", bVEv + 3)
                    sc.activation(smalls[0:1, 33:34], smalls[0:1, 32:33], AF.Sqrt,
                                  scale=0.015625)
                    sc.drain()
                    sc.mul(smalls[0:1, 33:34], smalls[0:1, 33:34], -1.0)
                    sc.drain()
                    inst = sc.copy(negMb[0:1, 0:1], smalls[0:1, 33:34])
                    assert sems.c["sAC"] + 1 == cp["nb_sqrt_done"]
                    sems.inc(inst, "sAC")

                @blk.vector
                def _(vraw):
                    v = DrainAll(vraw)
                    sems.wait(vraw, "sDI", cp["init_sync"])
                    v.tensor_tensor(perdq[:, :], perdq[:, :], perdq[:, :], AL.mult)
                    v.tensor_tensor(perdk[:, :], perdk[:, :], perdk[:, :], AL.mult)
                    v.tensor_copy(perdqb[:, :], perdq[:, :])
                    inst = v.tensor_copy(perdkb[:, :], perdk[:, :])
                    cp["nb_sq_done"] = sems.inc(inst, "sVE")
                    sems.wait(vraw, "sTE", cp["nb_mm_done"])
                    v.tensor_tensor(nrm[:, 0:NCH], psN[:, 0:NCH], maskq[:, :], AL.mult)
                    v.tensor_tensor(nrm[:, 32:32 + H], psN[:, 32:32 + H], maskk[:, :], AL.mult)
                    v.tensor_reduce(nrm[:, 16:17], nrm[:, 0:NCH], AX.X, AL.add)
                    inst = v.tensor_reduce(nrm[:, 17:18], nrm[:, 32:32 + H], AX.X, AL.add)
                    cp["nb_red_done"] = sems.inc(inst, "sVE")
                    sems.wait(vraw, "sDG", 32)
                    v.tensor_reduce(smalls[0:1, 30:31], nrm[0:1, 20:36], AX.X, AL.max)
                    v.tensor_reduce(smalls[0:1, 31:32], nrm[0:1, 40:56], AX.X, AL.max)
                    inst = v.tensor_tensor(smalls[0:1, 32:33], smalls[0:1, 30:31],
                                           smalls[0:1, 31:32], AL.mult)
                    cp["nb_mul_done"] = sems.inc(inst, "sVE")
                    sems.wait(vraw, "sTE", cp["negmm_done"])
                    inst = v.tensor_copy(negM[:, :], psNeg[:, :])
                    cp["negM_ve"] = sems.inc(inst, "sVE")

                @blk.gpsimd
                def _(g):
                    basev = cp["v_evac_done"] - NT
                    for i in range(NT):
                        sems.wait(g, "sAC", basev + i + 1)
                        sems.inc(g.dma_start(vspill[i * 128:(i + 1) * 128, :],
                                             vstg[:, (i % 2) * D:(i % 2 + 1) * D]), "sDVS" + str(i % 2), 16)
                    sems.wait(g, "sVE", cp["nb_red_done"])
                    sems.inc(g.dma_start(nrm[0:1, 20:36], nrm[0:16, 16:17],
                                         allow_slow_non_contiguous=True), "sDG", 16)
                    sems.inc(g.dma_start(nrm[0:1, 40:56], nrm[0:16, 17:18],
                                         allow_slow_non_contiguous=True), "sDG", 16)
                    sems.wait(g, "sTE", cnt_projv[-1])
                    sems.inc(g.dma_start(WA[:, :], wot.rearrange("(g p) e -> p g e", p=128)), "sDWA", 16)
                    cp["wo_loaded"] = sems.c["sDWA"]
                    sems.inc(g.dma_start(WB[:, :], wft.rearrange("(g p) e -> p g e", p=128)), "sDWB", 16)
                    cp["wf_loaded"] = sems.c["sDWB"]

        if phase_on('attn'):
            bTEa = sems.c["sTE"]
            bACa = sems.c["sAC"]
            bVEa = sems.c["sVE"]

            NG = 2 * H * NT

            def s_cnt(g):
                return bTEa + (1 if g == 0 else 2 * g)

            def pv_cnt(g):
                return bTEa + (2 * g + 3 if g < NG - 1 else 2 * NG)

            def exp_cnt(g):
                return bACa + g + 1

            def drain_cnt(u):
                return bVEa + u + 1

            with ExitStack() as pes, nc.Block() as blk:
                Sa = pes.enter_context(nc.psum_tensor("Sa", [128, 1024], F32))
                Sb = pes.enter_context(nc.psum_tensor("Sb", [128, 1024], F32))
                PV0 = pes.enter_context(nc.psum_tensor("PV0", [65, 1024], F32))
                PV1 = pes.enter_context(nc.psum_tensor("PV1", [65, 1024], F32))

                @blk.tensor
                def _(t):
                    sems.wait(t, "sVE", cp["negM_ve"])

                    def emit_S(g):
                        u, tau = g // NT, g % NT
                        hh, sg = u // 2, u % 2
                        pp = hh // 2
                        if tau == 0 and sg == 0 and hh % 2 == 0:
                            sems.wait(t, "sDQL" + str(pp % 2), 16 * (pp // 2 + 1))
                        if tau == 0 and sg == 0:
                            sems.wait(t, "sDVL" + str(hh % 2), 16 * (hh // 2 + 1))
                        sps = Sa if g % 2 == 0 else Sb
                        if g >= 2:
                            sems.wait(t, "sAC", exp_cnt(g - 2))
                        qsl = Qwin[:, (pp % 2) * SEQ + sg * 1024:
                                   (pp % 2) * SEQ + (sg + 1) * 1024]
                        for nb in range(2):
                            mm = t.matmul(sps[:, nb * 512:(nb + 1) * 512],
                                          KP[:, hh * SEQ + tau * 128: hh * SEQ + (tau + 1) * 128],
                                          qsl[:, nb * 512:(nb + 1) * 512],
                                          start=True, stop=True)
                        sems.inc(mm, "sTE")

                    def emit_PV(g):
                        u, tau = g // NT, g % NT
                        hh = u // 2
                        pv = PV0 if u % 2 == 0 else PV1
                        if tau == 0 and u >= 2:
                            sems.wait(t, "sVE", drain_cnt(u - 2))
                        sems.wait(t, "sAC", exp_cnt(g))
                        pw = Pt[:, (g % 2) * 1024:(g % 2 + 1) * 1024]
                        vsl = vwin[:, (hh % 2) * NT * 66 + tau * 66:
                                   (hh % 2) * NT * 66 + tau * 66 + 65]
                        for nb in range(2):
                            mm = t.matmul(pv[:, nb * 512:(nb + 1) * 512], vsl,
                                          pw[:, nb * 512:(nb + 1) * 512],
                                          start=(tau == 0), stop=(tau == NT - 1))
                        sems.inc(mm, "sTE")

                    for g in range(NG):
                        emit_S(g)
                        if g >= 1:
                            emit_PV(g - 1)
                    emit_PV(NG - 1)
                    assert sems.c["sTE"] == bTEa + 2 * NG
                    cp["attn_te_done"] = sems.c["sTE"]

                @blk.scalar
                def _(sc):
                    for g in range(NG):
                        sps = Sa if g % 2 == 0 else Sb
                        sems.wait(sc, "sTE", s_cnt(g))
                        if g >= 2:
                            sems.wait(sc, "sTE", pv_cnt(g - 2))
                        inst = sc.activation(Pt[:, (g % 2) * 1024:(g % 2 + 1) * 1024],
                                             sps[:, :], AF.Exp, bias=negM[:, 0:1],
                                             scale=0.125)
                        sems.inc(inst, "sAC")
                    cp["attn_exp_done"] = sems.c["sAC"]

                @blk.vector
                def _(v):
                    v.memset(denomC[:, :], 1.0)
                    v.drain()
                    for u in range(2 * H):
                        pv = PV0 if u % 2 == 0 else PV1
                        sems.wait(v, "sTE", pv_cnt(u * NT + NT - 1))
                        if u >= 2:
                            sems.wait(v, "sDSH" + str(u % 2), 32 * ((u - 2) // 2 + 1))
                        inst = v.tensor_copy(stg[0:65, (u % 2) * 1024:(u % 2) * 1024 + 1024],
                                             pv[:, :])
                        sems.inc(inst, "sVE")
                    cp["attn_drain_done"] = sems.c["sVE"]

                @blk.gpsimd
                def _(g):
                    sems.wait(g, "sDQS0", 16 * 4)
                    sems.wait(g, "sDQS1", 16 * 4)
                    sems.wait(g, "sDVS0", 16 * 8)
                    sems.wait(g, "sDVS1", 16 * 8)
                    for pp in range(2):
                        sems.inc(g.dma_start(Qwin[:, (pp % 2) * SEQ:(pp % 2 + 1) * SEQ],
                                             qspill[pp * 128:(pp + 1) * 128, :]), "sDQL" + str(pp % 2), 16)
                    for hh in range(2):
                        dst = vwin.rearrange("p (w t c) -> p w t c", w=2, t=NT)
                        src = vspill.rearrange("(t p) e -> p t e", p=128)
                        sems.inc(g.dma_start(dst[:, hh % 2, :, 0:64],
                                             src[:, :, hh * 64:(hh + 1) * 64]), "sDVL" + str(hh % 2), 16)
                    for u in range(2 * H):
                        hh, sg = u // 2, u % 2
                        sems.wait(g, "sVE", drain_cnt(u))
                        ssl = stg[:, (u % 2) * 1024:(u % 2) * 1024 + 1024]
                        och, row0 = hh // 2, (hh % 2) * 64
                        sems.inc(g.dma_start(
                            A[row0:row0 + 64, och * SEQ + sg * 1024: och * SEQ + sg * 1024 + 1024],
                            ssl[0:64, :]), "sDSH" + str(u % 2), 16)
                        sems.inc(g.dma_start(
                            denomC[hh:hh + 1, sg * 1024:(sg + 1) * 1024],
                            ssl[64:65, :]), "sDSH" + str(u % 2), 16)
                        if sg == 1:
                            nh = hh + 2
                            if nh < H:
                                dst = vwin.rearrange("p (w t c) -> p w t c", w=2, t=NT)
                                src = vspill.rearrange("(t p) e -> p t e", p=128)
                                sems.inc(g.dma_start(dst[:, nh % 2, :, 0:64],
                                                     src[:, :, nh * 64:(nh + 1) * 64]),
                                         "sDVL" + str(nh % 2), 16)
                            if hh % 2 == 1:
                                np_ = hh // 2 + 2
                                if np_ < NCH:
                                    sems.inc(g.dma_start(
                                        Qwin[:, (np_ % 2) * SEQ:(np_ % 2 + 1) * SEQ],
                                        qspill[np_ * 128:(np_ + 1) * 128, :]), "sDQL" + str(np_ % 2), 16)

        if phase_on('n'):
            bTEn = sems.c["sTE"]
            bVEn = sems.c["sVE"]
            bACn = sems.c["sAC"]
            with ExitStack() as pes, nc.Block() as blk:
                psR = pes.enter_context(nc.psum_tensor("psR", [128, 1024], F32))
                psR2 = pes.enter_context(nc.psum_tensor("psR2", [128, 1024], F32))
                psBV = pes.enter_context(nc.psum_tensor("psBV", [1, D], F32))
                psBC = pes.enter_context(nc.psum_tensor("psBC", [128, D], F32))

                @blk.tensor
                def _(t):
                    sems.wait(t, "sVE", bVEn + 1)
                    for j in range(2 * NCH):
                        gch, sg = j // 2, j % 2
                        ps = psR if j % 2 == 0 else psR2
                        if j >= 2:
                            sems.wait(t, "sVE", bVEn + j)
                        for nb in range(2):
                            mm = t.matmul(ps[:, nb * 512:(nb + 1) * 512],
                                          iselb[:, gch * 128:(gch + 1) * 128],
                                          denomC[:, sg * 1024 + nb * 512: sg * 1024 + (nb + 1) * 512],
                                          start=True, stop=True)
                        sems.inc(mm, "sTE")
                    sems.wait(t, "sDWA", cp["wo_loaded"])
                    sems.wait(t, "sDC", cp["consts_loaded"])
                    for gch in range(NCH):
                        for nb in range(2):
                            mm = t.matmul(psBV[:, nb * 512:(nb + 1) * 512], bvb[:, gch:gch + 1],
                                          WA[:, gch * D + nb * 512: gch * D + (nb + 1) * 512],
                                          start=(gch == 0), stop=(gch == NCH - 1))
                    cp["bv_mm_done"] = sems.inc(mm, "sTE")
                    for k in range(3):
                        sems.wait(t, "sVE", bVEn + 18 + 2 * k)
                        for nb in range(2):
                            mm = t.matmul(psBC[:, nb * 512:(nb + 1) * 512], E0[:, :],
                                          rowbufb[:, nb * 512:(nb + 1) * 512],
                                          start=True, stop=True)
                        sems.inc(mm, "sTE")

                @blk.sync
                def _(sync):
                    sems.wait(sync, "sTE", cp["attn_te_done"])
                    sems.inc(sync.dma_start(rowbuf[0:1, :], ws_h[None, :]), "sDRW", 16)
                    sems.wait(sync, "sVE", bVEn + 20)
                    sems.inc(sync.dma_start(rowbuf[0:1, :], fb_h[None, :]), "sDRW", 16)
                    sems.wait(sync, "sVE", bVEn + 22)
                    sems.inc(sync.dma_start(rowbuf[0:1, :], bo_h[None, :]), "sDRW", 16)

                @blk.vector
                def _(vraw):
                    sems.wait(vraw, "sDSH0", 32 * 16)
                    sems.wait(vraw, "sDSH1", 32 * 16)
                    with nc.allow_low_precision(reason="bf16 softmax denom recip"):
                        inst = vraw.reciprocal(denomC[0:16, :], denomC[0:16, :])
                    sems.inc(inst, "sVE")
                    vraw.drain()
                    for j in range(2 * NCH):
                        gch, sg = j // 2, j % 2
                        ps = psR if j % 2 == 0 else psR2
                        sems.wait(vraw, "sTE", bTEn + j + 1)
                        osl = A[:, gch * SEQ + sg * 1024: gch * SEQ + sg * 1024 + 1024]
                        sems.inc(vraw.tensor_tensor(osl, osl, ps[:, :], AL.mult), "sVE")
                    cp["onorm_done"] = sems.c["sVE"]
                    v = DrainAll(vraw)
                    sems.wait(vraw, "sDRW", 16)
                    sems.inc(v.tensor_copy(rowbufb[0:1, :], rowbuf[0:1, :]), "sVE")
                    sems.wait(vraw, "sTE", bTEn + 18)
                    sems.inc(v.tensor_copy(B_ws[:, :], psBC[:, :]), "sVE")
                    v.tensor_copy(xstg[0:1, 0:D], rowbufb[0:1, :])
                    v.tensor_tensor(rowbuf[0:1, :], rowbuf[0:1, :], xstg[0:1, 0:D],
                                    AL.subtract)
                    sems.inc(v.tensor_copy(rowbufb[0:1, :], rowbuf[0:1, :]), "sVE")
                    sems.wait(vraw, "sTE", bTEn + 19)
                    sems.inc(v.tensor_tensor(B_ws[:, :], B_ws[:, :], psBC[:, :], AL.add), "sVE")
                    sems.wait(vraw, "sDRW", 32)
                    sems.inc(v.tensor_copy(rowbufb[0:1, :], rowbuf[0:1, :]), "sVE")
                    sems.wait(vraw, "sTE", bTEn + 20)
                    sems.inc(v.tensor_copy(B_fb[:, :], psBC[:, :]), "sVE")
                    sems.wait(vraw, "sDRW", 48)
                    sems.wait(vraw, "sTE", cp["bv_mm_done"])
                    v.tensor_tensor(rowbuf[0:1, :], rowbuf[0:1, :], psBV[:, :], AL.add)
                    inst = v.tensor_copy(Brow[0:1, :], rowbuf[0:1, :])
                    cp["brow_done"] = sems.inc(inst, "sVE")

        if phase_on('p3'):
            bVE3 = sems.c["sVE"]
            bAC3 = sems.c["sAC"]
            bTE3 = sems.c["sTE"]
            bX = (sems.c["sDX0"], sems.c["sDX1"])

            def veA(i):
                return bVE3 + i + 1
            VPE3 = bVE3 + NT + 1
            COMB3 = bVE3 + NT + 2
            veE = {}
            _c = COMB3
            for _s in range(NT + 2):
                if _s >= 2:
                    _c += 1; veE[(_s - 2, 3)] = _c
                    _c += 1; veE[(_s - 2, 4)] = _c
                    _c += 1; veE[(_s - 2, 5)] = _c
                if 1 <= _s <= NT:
                    _c += 1; veE[(_s - 1, 6)] = _c

            def plQi(j):
                return 2 * j + 1

            def plQx(j):
                return 2 * j + 2
            cnt_wo = [bTE3 + 1 + i for i in range(NT)]
            cnt_tr, cnt_ffn = {}, {}
            _t = bTE3 + NT
            for _s in range(NT + 1):
                if _s < NT:
                    _t += 1; cnt_tr[_s] = _t
                if _s >= 1:
                    _t += 1; cnt_ffn[_s - 1] = _t

            def acSq(j):
                return bAC3 + 2 + j

            x1r8 = at(KP, 0, "x1r8", [128, 8 * D], F32)
            sqb1 = at(KP, 49152, "sqb1", [128, D], F32)
            iqb1 = at(KP, 53248, "iqb1", [128, D], I32)
            sqbs, iqbs = [sqb, sqb1], [iqb, iqb1]

            with ExitStack() as pes, nc.Block() as blk:
                psW = pes.enter_context(nc.psum_tensor("psW", [128, D], F32))
                psF = pes.enter_context(nc.psum_tensor("psF", [128, D], F32))
                psF2 = pes.enter_context(nc.psum_tensor("psF2", [128, D], F32))
                psT = pes.enter_context(nc.psum_tensor("psT3", [128, D], F32))

                @blk.tensor
                def _(t):
                    sems.wait(t, "sVE", cp["brow_done"])
                    for i in range(NT):
                        if i >= 1:
                            sems.wait(t, "sVE", veA(i - 1))
                        for gch in range(NCH):
                            for nb in range(2):
                                t.matmul(psW[:, nb * 512:(nb + 1) * 512],
                                         A[:, gch * SEQ + i * 128: gch * SEQ + (i + 1) * 128],
                                         WA[:, gch * D + nb * 512: gch * D + (nb + 1) * 512],
                                         start=(gch == 0), stop=False)
                        for nb in range(2):
                            mm = t.matmul(psW[:, nb * 512:(nb + 1) * 512],
                                          E0[:, :], Brow[:, nb * 512:(nb + 1) * 512],
                                          start=False, stop=True)
                        assert sems.inc(mm, "sTE") == cnt_wo[i]
                    sems.wait(t, "sDWB", cp["wf_loaded"])
                    for ss in range(NT + 1):
                        if ss < NT:
                            j = ss
                            sems.wait(t, "sPL", plQx(j))
                            if j >= 1:
                                sems.wait(t, "sVE", veE[(j - 1, 6)])
                            xq = xqb[:, (j % 2) * D:(j % 2 + 1) * D]
                            for gch in range(NCH):
                                mm = t.matmul(psT[:, gch * 128:(gch + 1) * 128],
                                              xq[:, gch * 128:(gch + 1) * 128],
                                              idbf[:, :], start=True, stop=True)
                            assert sems.inc(mm, "sTE") == cnt_tr[j]
                        if ss >= 1:
                            j = ss - 1
                            psf = psF if j % 2 == 0 else psF2
                            sems.wait(t, "sVE", veE[(j, 6)])
                            if j >= 2:
                                sems.wait(t, "sVE", veE[(j - 2, 3)])
                            for gch in range(NCH):
                                for nb in range(2):
                                    mm = t.matmul(psf[:, nb * 512:(nb + 1) * 512],
                                                  xqT[:, (j % 2) * D + gch * 128:(j % 2) * D + (gch + 1) * 128],
                                                  WB[:, gch * D + nb * 512: gch * D + (nb + 1) * 512],
                                                  start=(gch == 0), stop=(gch == NCH - 1))
                            assert sems.inc(mm, "sTE") == cnt_ffn[j]

                @blk.sync
                def _(sync):
                    sems.wait(sync, "sVE", cp["brow_done"])
                    for i in range(NT):
                        if i >= 2:
                            sems.wait(sync, "sVE", veA(i - 2))
                        sems.inc(sync.dma_start(xstg[:, (i % 2) * D:(i % 2 + 1) * D],
                                                x_hbm[i * 128:(i + 1) * 128, :]),
                                 "sDX" + str(i % 2), 16)
                        sems.wait(sync, "sVE", veA(i))
                        sems.inc(sync.dma_start(out_hbm[i * 128:(i + 1) * 128, :],
                                                x1b[:, (i % 2) * D:(i % 2 + 1) * D]),
                                 "sDOB" + str(i % 2), 16)
                    for j in range(NT):
                        sems.wait(sync, "sVE", veE[(j, 5)])
                        sems.inc(sync.dma_start(out_hbm[j * 128:(j + 1) * 128, :],
                                                x1r8[:, (j % 8) * D:(j % 8 + 1) * D]),
                                 "sDO" + str(j % 2), 16)
                    sems.wait(sync, "sDO0", 16 * 8)
                    sems.wait(sync, "sDO1", 16 * 8)

                @blk.gpsimd
                def _(g):
                    for j in range(2):
                        sems.wait(g, "sDOB" + str(j % 2), 16)
                        sems.inc(g.dma_start(x1r8[:, (j % 8) * D:(j % 8 + 1) * D],
                                             out_hbm[j * 128:(j + 1) * 128, :]),
                                 "sDXL" + str(j % 2), 16)
                    for j in range(NT):
                        sems.wait(g, "sAC", acSq(j))
                        if j >= 2:
                            sems.wait(g, "sTE", cnt_tr[j - 2])
                        inst = g.tensor_copy(iqbs[j % 2][:, :], sqbs[j % 2][:, :])
                        assert sems.inc(inst, "sPL") == plQi(j)
                        g.drain()
                        inst = g.tensor_copy(
                            xqb[:, (j % 2) * D:(j % 2 + 1) * D], iqbs[j % 2][:, :])
                        assert sems.inc(inst, "sPL") == plQx(j)
                        g.drain()

                @blk.scalar
                def _(sc):
                    sems.wait(sc, "sVE", VPE3)
                    inst = sc.activation(st3[:, 240:256], st3[:, 240:256], AF.Sqrt)
                    assert sems.inc(inst, "sAC") == bAC3 + 1
                    for j in range(NT):
                        sems.wait(sc, "sDXL" + str(j % 2), 16 * (j // 2 + 1))
                        sems.wait(sc, "sVE", COMB3)
                        if j >= 2:
                            sems.wait(sc, "sPL", plQi(j - 2))
                        inst = sc.activation(sqbs[j % 2][:, :],
                                             x1r8[:, (j % 8) * D:(j % 8 + 1) * D],
                                             AF.Identity, bias=st3[:, 208 + j:209 + j],
                                             scale=st3[:, 192 + j:193 + j])
                        assert sems.inc(inst, "sAC") == acSq(j)
                        if 2 <= j + 1 < NT:
                            jj = j + 1
                            sems.wait(sc, "sDOB" + str(jj % 2), 16 * (jj // 2 + 1))
                            if jj >= 8:
                                sems.wait(sc, "sDO" + str(jj % 2), 16 * ((jj - 8) // 2 + 1))
                            sems.inc(sc.dma_start(x1r8[:, (jj % 8) * D:(jj % 8 + 1) * D],
                                                  out_hbm[jj * 128:(jj + 1) * 128, :]),
                                     "sDXL" + str(jj % 2), 16)

                @blk.vector
                def _(vraw):
                    st = st3
                    for i in range(NT):
                        psw = psW
                        xsl = xstg[:, (i % 2) * D:(i % 2 + 1) * D]
                        x1 = x1b[:, (i % 2) * D:(i % 2 + 1) * D]
                        sems.wait(vraw, "sTE", cnt_wo[i])
                        sems.wait(vraw, "sDX" + str(i % 2), bX[i % 2] + 16 * (i // 2 + 1))
                        if i >= 2:
                            sems.wait(vraw, "sDOB" + str(i % 2), 16 * (i // 2))
                        inst = vraw.tensor_tensor(x1, psw[:, :], xsl, AL.add)
                        assert sems.inc(inst, "sVE") == veA(i)
                        vraw.drain()
                        vraw.bn_stats(st[:, 96:102], x1[:, 0:512])
                        vraw.bn_stats(st[:, 102:108], x1[:, 512:1024])
                        vraw.tensor_reduce(st[:, 64 + i:65 + i], x1, AX.X, AL.max)
                        vraw.tensor_reduce(st[:, 80 + i:81 + i], x1, AX.X, AL.min)
                        vraw.drain()
                        vraw.bn_aggr(st[:, 2 * i:2 * i + 2], st[:, 96:102])
                        vraw.bn_aggr(st[:, 32 + 2 * i:34 + 2 * i], st[:, 102:108])
                    vraw.drain()
                    v = DrainAll(vraw)
                    stv = st3.rearrange("p (a b) -> p a b", b=2)
                    mA, vA = stv[:, 0:16, 0:1], stv[:, 0:16, 1:2]
                    mB, vB = stv[:, 16:32, 0:1], stv[:, 16:32, 1:2]
                    mx, mn = st[:, 64:80], st[:, 80:96]
                    m_, vpe = st[:, 112:128], st[:, 128:144]
                    tA, tB = st[:, 144:160], st[:, 160:176]
                    is_, isr = st[:, 176:192], st[:, 192:208]
                    nmisr, xsC = st[:, 208:224], st[:, 224:240]
                    v.tensor_tensor(m_, mA, mB, AL.add)
                    v.tensor_scalar(m_, m_, 0.5, None, AL.mult)
                    v.tensor_tensor(tA, mA, mB, AL.subtract)
                    v.tensor_tensor(tA, tA, tA, AL.mult)
                    v.tensor_tensor(vpe, vA, vB, AL.add)
                    v.tensor_scalar(vpe, vpe, 0.5, None, AL.mult)
                    v.tensor_scalar(tA, tA, 0.25, EPS_LN, AL.mult, AL.add)
                    v.tensor_tensor(vpe, vpe, tA, AL.add)
                    inst = v.tensor_copy(st[:, 240:256], vpe)
                    assert sems.inc(inst, "sVE") == VPE3
                    sems.wait(vraw, "sAC", bAC3 + 1)
                    v.reciprocal(is_, st[:, 240:256])
                    v.tensor_tensor(tB, is_, is_, AL.mult)
                    v.tensor_tensor(tB, tB, vpe, AL.mult)
                    v.tensor_scalar(tB, tB, 3.0, -0.5, AL.subtract, AL.mult)
                    v.tensor_tensor(is_, is_, tB, AL.mult)
                    v.tensor_tensor(tA, mx, m_, AL.subtract)
                    v.tensor_tensor(tB, m_, mn, AL.subtract)
                    v.tensor_tensor(tA, tA, tB, AL.max)
                    v.tensor_tensor(tA, tA, is_, AL.mult)
                    v.tensor_scalar(tA, tA, EPS_Q, None, AL.max)
                    v.tensor_scalar(xsC, tA, 1.0 / 127.0, None, AL.mult)
                    v.reciprocal(tB, tA)
                    v.tensor_scalar(tB, tB, 127.0, None, AL.mult)
                    v.tensor_tensor(isr, is_, tB, AL.mult)
                    v.tensor_tensor(nmisr, m_, isr, AL.mult)
                    inst = v.tensor_scalar(nmisr, nmisr, -1.0, None, AL.mult)
                    assert sems.inc(inst, "sVE") == COMB3
                    for ss in range(NT + 2):
                        if ss >= 2:
                            j = ss - 2
                            psf = psF if j % 2 == 0 else psF2
                            e1 = h2b[:, (j % 2) * D:(j % 2 + 1) * D]
                            sems.wait(vraw, "sTE", cnt_ffn[j])
                            inst = vraw.tensor_tensor(e1, psf[:, :], B_ws[:, :], AL.mult)
                            assert sems.inc(inst, "sVE") == veE[(j, 3)]
                            vraw.drain()
                            inst = vraw.scalar_tensor_tensor(
                                e1, e1, st[:, 224 + j:225 + j], B_fb[:, :],
                                AL.mult, AL.add)
                            assert sems.inc(inst, "sVE") == veE[(j, 4)]
                            vraw.drain()
                            x1sl = x1r8[:, (j % 8) * D:(j % 8 + 1) * D]
                            inst = vraw.tensor_tensor(x1sl, x1sl, e1, AL.add)
                            assert sems.inc(inst, "sVE") == veE[(j, 5)]
                        if 1 <= ss <= NT:
                            j = ss - 1
                            sems.wait(vraw, "sTE", cnt_tr[j])
                            if j >= 2:
                                sems.wait(vraw, "sTE", cnt_ffn[j - 2])
                            inst = vraw.tensor_copy(
                                xqT[:, (j % 2) * D:(j % 2 + 1) * D], psT[:, :])
                            assert sems.inc(inst, "sVE") == veE[(j, 6)]
                    cp["p3_done"] = sems.c["sVE"]

        if _lim < len(_ORDER) - 1:
            with nc.Block() as blk:
                @blk.sync
                def _(sync):
                    for nm, v in list(sems.c.items()):
                        if v > 0:
                            sems.wait(sync, nm, v)
                    sems.inc(sync.dma_start(out_hbm[0:128, :], xstg[:, 0:D]), "sDO0", 16)
                    sems.wait(sync, "sDO0", sems.c["sDO0"])
        if debug:
            with nc.Block() as blk:
                @blk.vector
                def _(v):
                    for nm, val in list(sems.c.items()):
                        if val > 0 and nm not in ("sGP",):
                            sems.wait(v, nm, val)
                    v.engine_nop().then_inc(sems.h["sGP"], 1)
                    sems.c["sGP"] += 1

                @blk.sync
                def _(sync):
                    sems.wait(sync, "sGP", sems.c["sGP"])
                    ndma = 0
                    def dump(name, src, cast=True):
                        nonlocal ndma
                        pass
                @blk.gpsimd
                def _(g):
                    g.wait_ge(sems.h["sGP"], sems.c["sGP"])
                    nd = 0
                    _all = [("KP", KP[:, :]), ("A", A[:, :]),
                            ("Qwin", Qwin[:, :]), ("den", denomC[:, :]),
                            ("negM", negM[:, :]), ("B_o", B_o[:, :]),
                            ("B_ws", B_ws[:, :]), ("B_fb", B_fb[:, :]),
                            ("nrm", nrm[:, :]), ("perd", None),
                            ("x1b", x1b[:, :]), ("vwin", vwin[:, :]),
                            ("WA", WA[:, :]), ("WB", WB[:, :]),
                            ("Qwork", Qwork[:, :]),
                            ("xstg", xstg[:, :]), ("hstg", hstg[:, :]),
                            ("hstg2", hstg2[:, :]), ("smalls", smalls[:, :])]
                    if debug_names is not None:
                        _all = [p for p in _all if p[0] in debug_names]
                    for nm, src in _all:
                        if nm == "perd":
                            g.dma_start(dbg[nm][:, 0:NCH], perdq[:, :]).then_inc(sems.h["sDG"], 16)
                            g.dma_start(dbg[nm][:, 8:8 + H], perdk[:, :]).then_inc(sems.h["sDG"], 16)
                            sems.c["sDG"] += 32
                            nd += 2
                            continue
                        g.dma_start(dbg[nm][:, :], src).then_inc(sems.h["sDG"], 16)
                        sems.c["sDG"] += 16
                        nd += 1
                    g.wait_ge(sems.h["sDG"], sems.c["sDG"])
        sems.check()
    return nc


def _host_tables():
    idx = np.arange(32, dtype=np.float64)
    inv_freq = 10000.0 ** (-2.0 * idx / 64.0)
    t = np.arange(SEQ, dtype=np.float64)
    fr = np.outer(inv_freq, t)
    c32, s32 = np.cos(fr), np.sin(fr)
    cc = np.zeros((128, SEQ), np.float64)
    ss = np.zeros((128, SEQ), np.float64)
    for p in range(128):
        hd = p % 64
        cc[p] = c32[hd // 2]
        ss[p] = (-s32[hd // 2]) if hd % 2 == 0 else s32[hd // 2]
    swapm = np.zeros((128, 128), np.float32)
    for p in range(0, 128, 2):
        swapm[p + 1, p] = 1.0
        swapm[p, p + 1] = 1.0
    esel = np.zeros((128, H), np.float32)
    for hh in range(H):
        r0 = (hh % 2) * 64
        esel[r0:r0 + 64, hh] = 1.0
    isel = np.zeros((128, NCH, 128), np.float32)
    for gch in range(NCH):
        for p in range(128):
            isel[2 * gch + p // 64, gch, p] = 1.0
    maskq = np.zeros((16, NCH), np.float32)
    maskk = np.zeros((16, H), np.float32)
    for hh in range(H):
        maskq[hh, hh // 2] = 1.0
        maskk[hh, hh] = 1.0
    return (cc.astype(np.float32), ss.astype(np.float32), swapm, esel,
            isel.reshape(128, NCH * 128), maskq, maskk,
            np.eye(128, dtype=np.float32))


def kernel(L0_attn_x, L0_attn_Wq, L0_attn_bq, L0_attn_Wk, L0_attn_bk,
           L0_attn_Wv, L0_attn_bv, L0_attn_Wo, L0_attn_bo,
           L0_ffn_weight_q, L0_ffn_weight_scale, L0_ffn_bias):
    x = np.ascontiguousarray(np.asarray(L0_attn_x, dtype=np.float32))
    cc, ss, swapm, esel, isel, maskq, maskk, idm = _host_tables()
    shared = {
        "wqt": np.ascontiguousarray(np.asarray(L0_attn_Wq, np.float32).T),
        "wkt": np.ascontiguousarray(np.asarray(L0_attn_Wk, np.float32).T),
        "wvt": np.ascontiguousarray(np.asarray(L0_attn_Wv, np.float32).T),
        "wot": np.ascontiguousarray(np.asarray(L0_attn_Wo, np.float32).T),
        "wft": np.ascontiguousarray(np.asarray(L0_ffn_weight_q, np.int8).T),
        "bq": np.asarray(L0_attn_bq, np.float32),
        "bk": np.asarray(L0_attn_bk, np.float32),
        "bv": np.asarray(L0_attn_bv, np.float32),
        "bo": np.asarray(L0_attn_bo, np.float32),
        "ws": np.asarray(L0_ffn_weight_scale, np.float32),
        "fb": np.asarray(L0_ffn_bias, np.float32),
        "cc": cc, "ss": ss, "swapm": swapm, "esel": esel, "isel": isel,
        "maskq": maskq, "maskk": maskk, "idm": idm,
    }
    in_maps = [dict(shared, x=np.ascontiguousarray(x[c])) for c in range(8)]
    nc = build_program()
    res = run_bass_kernel_spmd(nc, in_maps, core_ids=list(range(8)))
    return np.stack([r["out"] for r in res.results], axis=0)
